# version 01
# baseline (speedup 1.0000x reference)
import numpy as np
import ml_dtypes
import concourse.bass as bass
import concourse.mybir as mybir
from concourse.bass_utils import run_bass_kernel_spmd

F32 = mybir.dt.float32
BF16 = mybir.dt.bfloat16
AF = mybir.ActivationFunctionType
ALU = mybir.AluOpType
AX = mybir.AxisListType
BF = ml_dtypes.bfloat16

NCORE = 8
S_GLOB = 8192
DM = 1024
C = 8
NT_PRE = 64
NT_OWN = 16
NSLOT = 8
NEG = -1e30
POSINF = 1e30

def _ranges_for(cb):
    g = sorted([cb, 7 - cb, 8 + cb, 15 - cb, 16 + cb, 23 - cb, 24 + cb, 31 - cb],
               reverse=True)
    tr = [2 * (gi + 1) for gi in g]
    return g, tr

def _R(s):
    return 64 - 8 * s

N_MAIN = sum(_R(s) - 2 for s in range(NSLOT))
ORDER = [0, 4, 1, 5, 2, 6, 3, 7, 8, 12, 9, 13, 10, 14, 11, 15]
INV = [ORDER.index(h) for h in range(16)]



def _host_prep(inp):
    f32 = np.float32
    sh = {}
    x0 = np.ascontiguousarray(inp["L0_attn_x"], f32)

    sh["wq"] = np.ascontiguousarray(
        np.asarray(inp["L0_attn_Wq"], f32).reshape(DM, 16, 64)[:, ORDER, :].reshape(DM, DM))
    sh["wkv"] = np.ascontiguousarray(
        np.concatenate([inp["L0_attn_Wk"], inp["L0_attn_Wv"]], axis=1), f32)
    sh["wo"] = np.ascontiguousarray(
        np.asarray(inp["L0_attn_Wo"], f32).reshape(16, 64, DM)[ORDER].reshape(DM, DM))
    sh["bq_r"] = np.ascontiguousarray(
        np.asarray(inp["L0_attn_bq"], f32).reshape(16, 64)[ORDER].reshape(1, DM))
    sh["bkv_r"] = np.ascontiguousarray(
        np.concatenate([inp["L0_attn_bk"], inp["L0_attn_bv"]])[None, :].astype(f32))
    sh["bo_c"] = np.ascontiguousarray(np.asarray(inp["L0_attn_bo"], f32).reshape(8, 128).T)

    sh["centT"] = np.ascontiguousarray(np.asarray(inp["L0_ffn_centroids"], f32).T)
    sh["gm_c"] = np.ascontiguousarray(
        np.asarray(inp["L0_ffn_gamma"], f32).reshape(8, 8, 128).transpose(2, 0, 1).reshape(128, 64))
    sh["bt_c"] = np.ascontiguousarray(
        np.asarray(inp["L0_ffn_beta"], f32).reshape(8, 8, 128).transpose(2, 0, 1).reshape(128, 64))
    sh["w1t"] = np.ascontiguousarray(np.asarray(inp["L0_ffn_W1"], f32).transpose(0, 2, 1))
    sh["b1_c"] = np.ascontiguousarray(
        np.asarray(inp["L0_ffn_b1"], f32).reshape(8, 32, 128).transpose(2, 0, 1).reshape(128, 256))
    sh["w2t"] = np.ascontiguousarray(np.asarray(inp["L0_ffn_W2"], f32).transpose(0, 2, 1))
    sh["b2_c"] = np.ascontiguousarray(
        np.asarray(inp["L0_ffn_b2"], f32).reshape(8, 8, 128).transpose(2, 0, 1).reshape(128, 64))

    sh["wqkv"] = np.ascontiguousarray(inp["L1_attn_W_qkv"], f32)
    sh["qn_r"] = np.ascontiguousarray(np.asarray(inp["L1_attn_qn_w"], f32)[None, :])
    sh["kn_r"] = np.ascontiguousarray(np.asarray(inp["L1_attn_kn_w"], f32)[None, :])
    sh["wout"] = np.ascontiguousarray(inp["L1_attn_W_out"], f32)
    sh["bout_c"] = np.ascontiguousarray(np.asarray(inp["L1_attn_b_out"], f32).reshape(8, 128).T)

    sh["l1w1"] = np.ascontiguousarray(np.asarray(inp["L1_ffn_w1_w"], f32).T)
    sh["l1w2"] = np.ascontiguousarray(np.asarray(inp["L1_ffn_w2_w"], f32).T)
    sh["l1h1"] = np.ascontiguousarray(np.asarray(inp["L1_ffn_h1_w"], f32).T)
    sh["l1h2"] = np.ascontiguousarray(np.asarray(inp["L1_ffn_h2_w"], f32).T)
    sh["l1w3"] = np.ascontiguousarray(np.asarray(inp["L1_ffn_w3_w"], f32).T)
    sh["s1_c"] = np.ascontiguousarray(np.asarray(inp["L1_ffn_w1_s"], f32).reshape(32, 128).T)
    sh["s2_c"] = np.ascontiguousarray(np.asarray(inp["L1_ffn_w2_s"], f32).reshape(32, 128).T)
    sh["hs1_c"] = np.ascontiguousarray(np.asarray(inp["L1_ffn_h1_s"], f32).reshape(32, 128).T)
    sh["hs2_c"] = np.ascontiguousarray(np.asarray(inp["L1_ffn_h2_s"], f32).reshape(32, 128).T)
    sh["s3_c"] = np.ascontiguousarray(np.asarray(inp["L1_ffn_w3_s"], f32).reshape(8, 128).T)

    inv = 1.0 / 10000.0 ** (np.arange(0, 64, 2, dtype=f32) / 64.0)
    ang = np.arange(S_GLOB, dtype=f32)[:, None] * inv[None, :]
    cos_full = np.cos(ang).astype(f32)
    sin_full = np.sin(ang).astype(f32)
    sh["cos_tm"] = np.ascontiguousarray(cos_full.reshape(64, 128, 32).transpose(1, 0, 2).reshape(128, 64 * 32))
    sh["sin_tm"] = np.ascontiguousarray(sin_full.reshape(64, 128, 32).transpose(1, 0, 2).reshape(128, 64 * 32))
    kk = np.arange(128)[:, None]
    qq = np.arange(256)[None, :]
    sh["maskA"] = np.where(kk <= qq, 0.0, POSINF).astype(f32)
    sh["maskB"] = np.where(kk <= qq - 128, 0.0, POSINF).astype(f32)
    sh["id16"] = np.eye(128, dtype=BF)
    sh["idf"] = np.eye(128, dtype=f32)

    per = []
    rows_all = []
    for c in range(NCORE):
        b, cb = divmod(c, 4)
        g, tr = _ranges_for(cb)
        rows = np.concatenate([np.arange(gi * 256, (gi + 1) * 256) for gi in g])
        rows_all.append(rows)
        d = {}
        d["x_own"] = np.ascontiguousarray(x0[b][rows])
        d["x_batch"] = x0[b]
        d["cos_own"] = np.ascontiguousarray(
            cos_full[rows].reshape(16, 128, 32).transpose(1, 0, 2).reshape(128, 16 * 32))
        d["sin_own"] = np.ascontiguousarray(
            sin_full[rows].reshape(16, 128, 32).transpose(1, 0, 2).reshape(128, 16 * 32))
        fb = np.zeros((128, N_MAIN), f32)
        idx = 0
        for s in range(NSLOT):
            for j in range(_R(s) - 2):
                if j >= tr[s] - 2:
                    fb[:, idx] = NEG
                idx += 1
        d["flagbias"] = fb
        dsrc = [tr[s] - 2 + dd for s in range(NSLOT) for dd in (0, 1)]
        d["diag_src"] = np.asarray(dsrc, np.int32)[None, :]
        drows = np.concatenate([np.arange(bk * 128, (bk + 1) * 128) for bk in dsrc])
        d["x_diag"] = np.ascontiguousarray(x0[b][drows])
        d["cos_diag"] = np.ascontiguousarray(
            cos_full[drows].reshape(16, 128, 32).transpose(1, 0, 2).reshape(128, 512))
        d["sin_diag"] = np.ascontiguousarray(
            sin_full[drows].reshape(16, 128, 32).transpose(1, 0, 2).reshape(128, 512))
        per.append(d)
    return sh, per, rows_all



class Sem:
    def __init__(self, nc, name):
        self.h = nc.alloc_semaphore(name)
        self.n = 0

    def inc(self, instr, k=1):
        instr.then_inc(self.h, k)
        self.n += k
        return self.n


class Arena:

    def __init__(self, nc, name, cols, dtype):
        self.t = nc.alloc_sbuf_tensor(name, [128, cols], dtype)
        self.cols = cols
        self.off = 0

    def take(self, n):
        assert self.off + n <= self.cols, (self.off, n, self.cols)
        o = self.off
        self.off += n
        return o

    def view(self, off, n):
        return self.t[:, off:off + n]



A16_A = 0
A16_B = 16640
A16_C = 33280
A16_H = 49664
A16_TOT = 53760

A32_M = 0
A32_T = 16384
A32_TOT = 17408


def build_launch1(dbg=""):
    nc = bass.Bass("TRN2", target_bir_lowering=False)

    x_own = nc.dram_tensor("x_own", [2048, DM], F32, kind="ExternalInput")
    x_batch = nc.dram_tensor("x_batch", [S_GLOB, DM], F32, kind="ExternalInput")
    cos_own = nc.dram_tensor("cos_own", [128, 512], F32, kind="ExternalInput")
    x_diag = nc.dram_tensor("x_diag", [2048, DM], F32, kind="ExternalInput")
    cos_diag = nc.dram_tensor("cos_diag", [128, 512], F32, kind="ExternalInput")
    sin_diag = nc.dram_tensor("sin_diag", [128, 512], F32, kind="ExternalInput")
    sin_own = nc.dram_tensor("sin_own", [128, 512], F32, kind="ExternalInput")
    flagbias = nc.dram_tensor("flagbias", [128, N_MAIN], F32, kind="ExternalInput")
    diag_src = nc.dram_tensor("diag_src", [1, 16], mybir.dt.int32, kind="ExternalInput")
    wq = nc.dram_tensor("wq", [DM, DM], F32, kind="ExternalInput")
    wkv = nc.dram_tensor("wkv", [DM, 512], F32, kind="ExternalInput")
    wo = nc.dram_tensor("wo", [DM, DM], F32, kind="ExternalInput")
    bq_r = nc.dram_tensor("bq_r", [1, DM], F32, kind="ExternalInput")
    bkv_r = nc.dram_tensor("bkv_r", [1, 512], F32, kind="ExternalInput")
    bo_c = nc.dram_tensor("bo_c", [128, 8], F32, kind="ExternalInput")
    centT = nc.dram_tensor("centT", [DM, 8], F32, kind="ExternalInput")
    gm_c = nc.dram_tensor("gm_c", [128, 64], F32, kind="ExternalInput")
    bt_c = nc.dram_tensor("bt_c", [128, 64], F32, kind="ExternalInput")
    w1t = nc.dram_tensor("w1t", [8, DM, 4096], F32, kind="ExternalInput")
    b1_c = nc.dram_tensor("b1_c", [128, 256], F32, kind="ExternalInput")
    w2t = nc.dram_tensor("w2t", [8, 4096, DM], F32, kind="ExternalInput")
    b2_c = nc.dram_tensor("b2_c", [128, 64], F32, kind="ExternalInput")
    wqkv = nc.dram_tensor("wqkv", [DM, 3072], F32, kind="ExternalInput")
    qn_r = nc.dram_tensor("qn_r", [1, DM], F32, kind="ExternalInput")
    kn_r = nc.dram_tensor("kn_r", [1, DM], F32, kind="ExternalInput")
    l1w1 = nc.dram_tensor("l1w1", [DM, 4096], F32, kind="ExternalInput")
    l1w2 = nc.dram_tensor("l1w2", [DM, 4096], F32, kind="ExternalInput")
    l1h1 = nc.dram_tensor("l1h1", [4096, 4096], F32, kind="ExternalInput")
    l1h2 = nc.dram_tensor("l1h2", [4096, 4096], F32, kind="ExternalInput")
    l1w3 = nc.dram_tensor("l1w3", [4096, DM], F32, kind="ExternalInput")
    s1_c = nc.dram_tensor("s1_c", [128, 32], F32, kind="ExternalInput")
    s2_c = nc.dram_tensor("s2_c", [128, 32], F32, kind="ExternalInput")
    hs1_c = nc.dram_tensor("hs1_c", [128, 32], F32, kind="ExternalInput")
    hs2_c = nc.dram_tensor("hs2_c", [128, 32], F32, kind="ExternalInput")
    s3_c = nc.dram_tensor("s3_c", [128, 8], F32, kind="ExternalInput")
    cos_tm = nc.dram_tensor("cos_tm", [128, 2048], F32, kind="ExternalInput")
    sin_tm = nc.dram_tensor("sin_tm", [128, 2048], F32, kind="ExternalInput")
    maskA = nc.dram_tensor("maskA", [128, 256], F32, kind="ExternalInput")
    maskB = nc.dram_tensor("maskB", [128, 256], F32, kind="ExternalInput")
    id16 = nc.dram_tensor("id16", [128, 128], BF16, kind="ExternalInput")
    idf = nc.dram_tensor("idf", [128, 128], F32, kind="ExternalInput")

    xn_d = nc.dram_tensor("xn_d", [128, 16384], BF16, kind="Internal")
    z_d = nc.dram_tensor("z_d", [128, 16384], BF16, kind="Internal")
    q_d = nc.dram_tensor("q_d", [128, 16384], BF16, kind="Internal")
    attn_d = nc.dram_tensor("attn_d", [128, 16384], BF16, kind="Internal")
    y_d = nc.dram_tensor("y_d", [128, 16384], F32, kind="Internal")
    xn1_d = nc.dram_tensor("xn1_d", [128, 16384], BF16, kind="Internal")
    h1b = nc.dram_tensor("h1b", [4096, 4096], BF16, kind="Internal")
    h2b = nc.dram_tensor("h2b", [4096, 4096], BF16, kind="Internal")
    w1b = nc.dram_tensor("w1b", [DM, 4096], BF16, kind="Internal")
    w2b = nc.dram_tensor("w2b", [DM, 4096], BF16, kind="Internal")
    w3b = nc.dram_tensor("w3b", [4096, DM], BF16, kind="Internal")

    ypart = nc.dram_tensor("ypart", [128, 16384], F32, kind="ExternalOutput")
    ffn1 = nc.dram_tensor("ffn1", [128, 16384], F32, kind="ExternalOutput")
    qp_o = nc.dram_tensor("qp_o", [128, 16384], BF16, kind="ExternalOutput")
    kvpart = nc.dram_tensor("kvpart", [64, 1040], F32, kind="ExternalOutput")
    dbg_outs = {}
    if dbg:
        for nm, shp, dt in [
            ("d_xn", [128, 16384], BF16), ("d_z", [128, 16384], BF16),
            ("d_q", [128, 16384], BF16), ("d_amask", [128, 128], F32),
            ("d_kfm", [128, 2 * S_GLOB], BF16), ("d_vst", [128, NT_PRE * 260], BF16),
            ("d_sc", [128, 128], F32), ("d_chi", [128, 128], BF16),
            ("d_attn", [128, 16384], BF16), ("d_y", [128, 16384], F32),
            ("d_acc", [128, 16384], F32),
        ]:
            dbg_outs[nm] = nc.dram_tensor(nm, shp, dt, kind="ExternalOutput")

    a16 = Arena(nc, "a16", A16_TOT, BF16)
    a32 = Arena(nc, "a32", A32_TOT, F32)
    sb_id16 = nc.alloc_sbuf_tensor("sb_id16", [128, 128], BF16)
    sb_idf = nc.alloc_sbuf_tensor("sb_idf", [128, 128], F32)
    sb_maskA = nc.alloc_sbuf_tensor("sb_maskA", [128, 256], F32)
    sb_maskB = nc.alloc_sbuf_tensor("sb_maskB", [128, 256], F32)
    sb_flag = nc.alloc_sbuf_tensor("sb_flag", [128, N_MAIN], F32)
    sb_b1c = nc.alloc_sbuf_tensor("sb_b1c", [128, 256], F32)
    sb_b2c = nc.alloc_sbuf_tensor("sb_b2c", [128, 64], F32)
    sb_gmc = nc.alloc_sbuf_tensor("sb_gmc", [128, 64], F32)
    sb_btc = nc.alloc_sbuf_tensor("sb_btc", [128, 64], F32)
    sb_boc = nc.alloc_sbuf_tensor("sb_boc", [128, 8], F32)
    sb_s1c = nc.alloc_sbuf_tensor("sb_s1c", [128, 32], F32)
    sb_s2c = nc.alloc_sbuf_tensor("sb_s2c", [128, 32], F32)
    sb_hs1c = nc.alloc_sbuf_tensor("sb_hs1c", [128, 32], F32)
    sb_hs2c = nc.alloc_sbuf_tensor("sb_hs2c", [128, 32], F32)
    sb_s3c = nc.alloc_sbuf_tensor("sb_s3c", [128, 8], F32)
    sb_amall = nc.alloc_sbuf_tensor("sb_amall", [128, 128], F32)
    sb_scdbg = nc.alloc_sbuf_tensor("sb_scdbg", [128, 128], F32)
    sb_row = nc.alloc_sbuf_tensor("sb_row", [1, DM], F32)
    sb_row2 = nc.alloc_sbuf_tensor("sb_row2", [1, 512], F32)
    sb_mrow = nc.alloc_sbuf_tensor("sb_mrow", [1, 256], F32)
    sb_arow = nc.alloc_sbuf_tensor("sb_arow", [1, 256], F32)
    sb_drow = nc.alloc_sbuf_tensor("sb_drow", [1, 256], F32)
    sb_ones1 = nc.alloc_sbuf_tensor("sb_ones1", [1, 128], F32)
    sb_dsrc = nc.alloc_sbuf_tensor("sb_dsrc", [1, 16], mybir.dt.int32)
    sb_stat = nc.alloc_sbuf_tensor("sb_stat", [128, 24], F32)

    pA = nc.alloc_psum_tensor("pA", [128, 512], F32)
    pB = nc.alloc_psum_tensor("pB", [128, 512], F32)
    pD = nc.alloc_psum_tensor("pD", [128, 2048], F32)
    pT = nc.alloc_psum_tensor("pT", [128, 1024], BF16)
    pE = nc.alloc_psum_tensor("pE", [128, 512], F32)

    sX = Sem(nc, "sX")
    sG = Sem(nc, "sG")
    sT = Sem(nc, "sT")
    sV = Sem(nc, "sV")
    sA = Sem(nc, "sA")
    sO = Sem(nc, "sO")
    sP = Sem(nc, "sP")
    sW = Sem(nc, "sW")
    sXT = Sem(nc, "sXT")
    sQS = Sem(nc, "sQS")
    sAB = Sem(nc, "sAB")
    sZL = Sem(nc, "sZL")
    sFL = Sem(nc, "sFL")
    sGA = Sem(nc, "sGA")
    sGB = Sem(nc, "sGB")
    sAC = Sem(nc, "sAC")
    sAB2 = Sem(nc, "sAB2")

    ctx = dict(locals())
    _l1_p0(ctx)
    if dbg == "p0":
        ctx["own_tail"] = dict(v=(sV, sV.n), s=(sA, sA.n), t=(sT, sT.n), o=(sO, sO.n))
        _l1_dbg_tail(ctx, ["d_chi"])
        return nc, ctx
    _l1_p2a(ctx)
    if dbg == "p2a":
        ctx["own_tail"] = dict(v=(sV, sV.n), s=(sA, sA.n), t=(sT, sT.n), o=(sO, sO.n))
        _l1_dbg_tail(ctx, ["d_chi", "d_kfm", "d_vst"])
        return nc, ctx
    _l1_own(ctx)
    _l1_p2a_diag(ctx)
    if dbg == "own":
        _l1_dbg_tail(ctx, ["d_xn", "d_z", "d_q", "d_amask", "d_kfm", "d_vst", "d_sc", "d_chi"])
        return nc, ctx
    _l1_attn(ctx)
    if dbg == "attn":
        _l1_dbg_tail(ctx, ["d_attn"])
        return nc, ctx
    _l1_wo(ctx)
    _l1_moe(ctx)
    if dbg == "moe":
        _l1_dbg_tail(ctx, ["d_y", "d_acc", "d_sc", "d_attn"])
        return nc, ctx
    _l1_block1(ctx)
    _l1_ffn1(ctx)
    _l1_tail(ctx)
    return nc, ctx



class Prog:

    def __init__(self):
        self.ops = []

    def add(self, eng_name, emit, waits=(), sem=None, inc=1):
        if sem is not None:
            sem.n += inc
            tok = (sem, sem.n)
        else:
            tok = None
        self.ops.append((eng_name, sem, inc, emit, [w for w in waits if w], tok))
        return tok

    def run(self, nc, engines=("sync", "gpsimd", "tensor", "vector", "scalar"),
            serialize=False):
        if serialize:
            prev = None
            for (en, sem, inc, emit, waits, tok) in self.ops:
                if prev is not None:
                    waits.append(prev)
                if tok is not None:
                    prev = tok
        with nc.Block() as blk:
            for name in engines:
                def body(eng, name=name):
                    for (en, sem, inc, emit, waits, tok) in self.ops:
                        if en != name:
                            continue
                        for (ws, wv) in waits:
                            eng.wait_ge(ws.h, wv)
                        instr = emit(eng)
                        if sem is not None:
                            instr.then_inc(sem.h, inc)
                getattr(blk, name)(body)


A32 = dict(
    cos=0, sin=2048, x3=4096, xnf=7168, bq=8192, bkv=9216, kvb=9728,
    cown=10240, sown=10752, tmpa=11264, tmpb=11296,
    m_pl=0, m_dA=256, m_dB=512, tmpS0=768, tmpS1=1024, recb=1312,
    yst=0, x3e=4096, wotmp=7168,
    acc=0, abc=16384, abc2=16640,
    ysl=0, ysl2=2048, qnb=4096, knb=5120, bc1=6144, bc2=6656, sttmp=16896,
    facc=0, ftmp=4096, ftmp2=4608, ftmp3=5120,
)
H16 = dict(stage0=0, stage1=1024, qro=2048, kro=3072, chi=3328, clo=3392,
           qslot=0, kscr=0, vaug=1024, qscr=2176,
           hbuf=0, rt=2048, rm=2304,
           w3s=0, w3s2=1024)


def _l1_p0(c):
    a16, a32 = c["a16"], c["a32"]
    sX, sG, sT, sV, sA = c["sX"], c["sG"], c["sT"], c["sV"], c["sA"]
    H = A16_H
    p = Prog()

    loads = [
        (c["sb_id16"][:], "id16"), (c["sb_idf"][:], "idf"),
        (c["sb_maskA"][:], "maskA"), (c["sb_maskB"][:], "maskB"),
        (c["sb_flag"][:], "flagbias"), (c["sb_b1c"][:], "b1_c"),
        (c["sb_b2c"][:], "b2_c"), (c["sb_gmc"][:], "gm_c"),
        (c["sb_btc"][:], "bt_c"), (c["sb_boc"][:], "bo_c"),
        (c["sb_s1c"][:], "s1_c"), (c["sb_s2c"][:], "s2_c"),
        (c["sb_hs1c"][:], "hs1_c"), (c["sb_hs2c"][:], "hs2_c"),
        (c["sb_s3c"][:], "s3_c"), (c["sb_dsrc"][:], "diag_src"),
        (a32.view(A32["cos"], 2048), "cos_tm"),
        (a32.view(A32["sin"], 2048), "sin_tm"),
        (a32.view(A32["cown"], 512), "cos_own"),
        (a32.view(A32["sown"], 512), "sin_own"),
        (c["sb_row"][:], "bq_r"), (c["sb_row2"][:], "bkv_r"),
        (a32.view(11392, 512), "cos_diag"),
        (a32.view(11904, 512), "sin_diag"),
    ]
    for dst, nm in loads:
        p.add("sync", lambda e, d=dst, n=nm: e.dma_start(d, c[n][:]), sem=sX, inc=16)
    p.add("sync", lambda e: e.dma_start(
        a32.view(A32["x3"], 64).rearrange("p (c e) -> p c e", c=8),
        c["centT"][:].rearrange("(c p) e -> p c e", p=128)), sem=sX, inc=16)
    ldbar = (sX, sX.n)

    p.add("gpsimd", lambda e: e.dma_start(
        a16.view(A16_C, 8 * DM).rearrange("p (c o) -> p c o", c=8),
        c["wq"][:].rearrange("(c p) o -> p c o", p=128)), sem=sG, inc=16)
    p.add("gpsimd", lambda e: e.dma_start(
        a16.view(A16_C + 12288, 8 * 512).rearrange("p (c o) -> p c o", c=8),
        c["wkv"][:].rearrange("(c p) o -> p c o", p=128)), sem=sG, inc=16)
    p.add("gpsimd", lambda e: e.dma_start(
        a16.view(H + H16["chi"], 64).rearrange("p (c e) -> p c e", c=8),
        c["centT"][:].rearrange("(c p) e -> p c e", p=128)), sem=sG, inc=16)
    gbar = (sG, sG.n)
    tvone = p.add("gpsimd", lambda e: e.memset(
        a16.view(A16_A, 64 * 260).rearrange("p (b e) -> p b e", e=65)[:, :, 64:65], 1.0),
        sem=c["sP"])
    tones = p.add("gpsimd", lambda e: e.memset(c["sb_ones1"][:], 1.0), sem=c["sP"])
    p.add("vector", lambda e: (
        e.memset(c["sb_stat"][:, 18:19], 1e-5),
        e.memset(c["sb_stat"][:, 19:20], 1e-6))[-1], sem=c["sV"])

    tclo = p.add("vector", lambda e: e.tensor_sub(
        a16.view(H + H16["clo"], 64), a32.view(A32["x3"], 64), a16.view(H + H16["chi"], 64)),
        waits=[ldbar, gbar], sem=sV)
    tbq = p.add("scalar", lambda e: e.mul(c["sb_row"][:], c["sb_row"][:], 0.125),
                waits=[ldbar], sem=sA)
    tmm = p.add("tensor", lambda e: (
        e.matmul(c["pA"][:], c["sb_ones1"][:], c["sb_row"][0:1, 0:512], start=True, stop=True),
        e.matmul(c["pB"][:], c["sb_ones1"][:], c["sb_row"][0:1, 512:1024], start=True, stop=True))[-1],
        waits=[tbq, tones], sem=sT)
    tcp = p.add("scalar", lambda e: (
        e.copy(a32.view(A32["bq"], 512), c["pA"][:]),
        e.copy(a32.view(A32["bq"] + 512, 512), c["pB"][:]))[-1],
        waits=[tmm], sem=sA)
    tmm2 = p.add("tensor", lambda e: e.matmul(
        c["pA"][:], c["sb_ones1"][:], c["sb_row2"][:], start=True, stop=True),
        waits=[ldbar, tones, tcp], sem=sT)
    tbkv = p.add("scalar", lambda e: e.copy(a32.view(A32["bkv"], 512), c["pA"][:]),
                 waits=[tmm2], sem=sA)
    sW = c["sW"]
    for i in range(16):
        p.add("gpsimd", lambda e, i=i: e.dma_start(
            c["h1b"][i * 256:(i + 1) * 256, :], c["l1h1"][i * 256:(i + 1) * 256, :]),
            sem=sW, inc=16)
    for i in range(16):
        p.add("gpsimd", lambda e, i=i: e.dma_start(
            c["h2b"][i * 256:(i + 1) * 256, :], c["l1h2"][i * 256:(i + 1) * 256, :]),
            sem=sW, inc=16)
    for i in range(4):
        p.add("gpsimd", lambda e, i=i: e.dma_start(
            c["w1b"][i * 256:(i + 1) * 256, :], c["l1w1"][i * 256:(i + 1) * 256, :]),
            sem=sW, inc=16)
        p.add("gpsimd", lambda e, i=i: e.dma_start(
            c["w2b"][i * 256:(i + 1) * 256, :], c["l1w2"][i * 256:(i + 1) * 256, :]),
            sem=sW, inc=16)
    for i in range(4):
        p.add("gpsimd", lambda e, i=i: e.dma_start(
            c["w3b"][i * 1024:(i + 1) * 1024, :], c["l1w3"][i * 1024:(i + 1) * 1024, :]),
            sem=sW, inc=16)
    c["tok_conv"] = (sW, sW.n)
    p.run(c["nc"])
    c["tok_wq"] = gbar
    c["tok_wkv"] = gbar
    c["tok_vones"] = tvone
    c["tok_bkv"] = tbkv
    c["tok_bq"] = tcp
    c["tok_clo"] = tclo
    c["tok_ld"] = ldbar


def _stats_chain(p, c, x_ap, sq_ap, cb, wx, prev_read_tok, eps_col=18):
    st = c["sb_stat"]
    sV, sA = c["sV"], c["sA"]
    t_red = p.add("vector", lambda e: e.reduce_sum(st[:, cb:cb + 1], x_ap, axis=AX.X),
                  waits=[wx], sem=sV)
    t_mu = p.add("vector", lambda e: e.tensor_scalar_mul(
        st[:, cb + 1:cb + 2], st[:, cb:cb + 1], 1.0 / 1024.0), waits=[t_red], sem=sV)
    t_sq = p.add("scalar", lambda e: e.activation(
        sq_ap, x_ap, AF.Square, accum_out=st[:, cb + 2:cb + 3]),
        waits=[wx, prev_read_tok], sem=sA)
    t_va = p.add("vector", lambda e: (
        e.tensor_scalar_mul(st[:, cb + 3:cb + 4], st[:, cb + 2:cb + 3], 1.0 / 1024.0),
        e.tensor_tensor(st[:, 20:21], st[:, cb + 1:cb + 2], st[:, cb + 1:cb + 2],
                        ALU.mult))[-1],
        waits=[t_sq, t_mu], sem=sV)
    t_var = p.add("vector", lambda e: e.tensor_sub(
        st[:, cb + 4:cb + 5], st[:, cb + 3:cb + 4], st[:, 20:21]),
        waits=[t_va], sem=sV)
    t_sd = p.add("scalar", lambda e: e.activation(
        st[:, cb + 5:cb + 6], st[:, cb + 4:cb + 5], AF.Sqrt,
        bias=st[:, eps_col:eps_col + 1]), waits=[t_var], sem=sA)
    t_rc = p.add("vector", lambda e: e.reciprocal(st[:, cb + 6:cb + 7], st[:, cb + 5:cb + 6]),
                 waits=[t_sd], sem=sV)
    t_rstd = p.add("vector", lambda e: e.tensor_scalar(
        st[:, cb + 7:cb + 8], st[:, cb + 1:cb + 2], st[:, cb + 6:cb + 7],
        -1.0, ALU.mult, ALU.mult), waits=[t_rc], sem=sV)
    return t_sq, t_var, t_rstd


def _rope_emit(p, c, src_view, dst_view, nheads, cos_ap, sin_ap, wfirst, extra_waits=()):
    a32 = c["a32"]
    sV = c["sV"]
    last = wfirst
    toks = []
    for h in range(nheads):
        ta1 = a32.view(A32["tmpa"], 32)
        tb1 = a32.view(A32["tmpb"], 32)
        ta2 = a32.view(A32["tmpa"] + 64, 32)
        tb2 = a32.view(A32["tmpb"] + 64, 32)
        av = src_view[:, h * 64:h * 64 + 32]
        bv = src_view[:, h * 64 + 32:h * 64 + 64]
        g1 = p.add("vector", lambda e, av=av, bv=bv, ta1=ta1, tb1=tb1, ta2=ta2, tb2=tb2: (
            e.tensor_tensor(ta1, av, cos_ap, ALU.mult),
            e.tensor_tensor(tb1, bv, sin_ap, ALU.mult),
            e.tensor_tensor(ta2, bv, cos_ap, ALU.mult),
            e.tensor_tensor(tb2, av, sin_ap, ALU.mult))[-1],
            waits=[last] + list(extra_waits if h == 0 else []), sem=sV)
        g2 = p.add("vector", lambda e, h=h, ta1=ta1, tb1=tb1, ta2=ta2, tb2=tb2: (
            e.tensor_sub(dst_view[:, h * 64:h * 64 + 32], ta1, tb1),
            e.tensor_add(dst_view[:, h * 64 + 32:h * 64 + 64], ta2, tb2))[-1],
            waits=[g1], sem=sV)
        last = g2
        toks.append(g2)
    return toks[-1]


def _l1_p2a(c):
    a16, a32 = c["a16"], c["a32"]
    sX, sT, sV, sA = c["sX"], c["sT"], c["sV"], c["sA"]
    pA, pT = c["pA"], c["pT"]
    H = A16_H
    st = c["sb_stat"]
    x3o = [A32["x3"] + j * 1024 for j in range(3)]
    stage0 = a16.view(H + H16["stage0"], 1024)
    stage1 = a16.view(H + H16["stage1"], 1024)
    kro = a16.view(H + H16["kro"], 256)
    wkv_sb = lambda ic: a16.view(A16_C + 12288 + ic * 512, 512)

    p = Prog()
    xn_tok = [None] * NT_PRE
    tT_tok = [None] * NT_PRE
    ev_tok = [None] * NT_PRE
    kv_tok = [None] * NT_PRE
    bias_tok = [None] * NT_PRE
    vcp_tok = [None] * NT_PRE
    kT_tok = [None] * NT_PRE
    kev_tok = [None] * NT_PRE
    var_tok = [None] * NT_PRE
    prev = lambda lst, i: lst[i - 1] if i >= 1 else None

    for i in range(NT_PRE):
        xv = a32.view(x3o[i % 3], 1024)
        t_x = p.add("sync", lambda e, i=i, xv=xv: e.dma_start(
            xv, c["x_batch"][i * 128:(i + 1) * 128, :]),
            waits=[xn_tok[i - 3] if i >= 3 else c["tok_clo"]], sem=c["sXT"], inc=16)
        sq_tok, var_tok[i], t_rstd = _stats_chain(
            p, c, xv, a32.view(A32["xnf"], 1024), 0, t_x, prev(var_tok, i))
        xn_tok[i] = p.add("vector", lambda e, xv=xv: e.tensor_scalar(
            stage0, xv, st[:, 6:7], st[:, 7:8], ALU.mult, ALU.add),
            waits=[t_rstd, prev(tT_tok, i)], sem=sV)
        tT_tok[i] = p.add("tensor", lambda e: [
            e.transpose(pT[:, ic * 128:(ic + 1) * 128],
                        stage0[:, ic * 128:(ic + 1) * 128], c["sb_id16"][:])
            for ic in range(8)][-1],
            waits=[xn_tok[i], prev(ev_tok, i), prev(kev_tok, i)], sem=sT)
        ev_tok[i] = p.add("vector", lambda e: [
            e.tensor_copy(stage1[:, ic * 128:(ic + 1) * 128], pT[:, ic * 128:(ic + 1) * 128])
            for ic in range(8)][-1],
            waits=[tT_tok[i], prev(kv_tok, i)], sem=sV)
        kv_tok[i] = p.add("tensor", lambda e: [
            e.matmul(pA[:], stage1[:, ic * 128:(ic + 1) * 128], wkv_sb(ic),
                     start=(ic == 0), stop=(ic == 7)) for ic in range(8)][-1],
            waits=[ev_tok[i], prev(bias_tok, i), c["tok_wkv"] if i == 0 else None],
            sem=sT)
        bias_tok[i] = p.add("vector", lambda e: e.tensor_tensor(
            a32.view(A32["kvb"], 512), pA[:], a32.view(A32["bkv"], 512), ALU.add),
            waits=[kv_tok[i], prev(vcp_tok, i),
                   c["tok_bkv"] if i == 0 else None], sem=sV)
        rope_tok = _rope_emit(
            p, c, a32.view(A32["kvb"], 512), kro, 4,
            a32.view(A32["cos"] + i * 32, 32), a32.view(A32["sin"] + i * 32, 32),
            bias_tok[i], extra_waits=[prev(kT_tok, i)])
        vcp_tok[i] = p.add("scalar", lambda e, i=i: e.copy(
            a16.view(A16_A + i * 260, 260).rearrange("p (h x) -> p h x", h=4)[:, :, 0:64],
            a32.view(A32["kvb"] + 256, 256).rearrange("p (h x) -> p h x", h=4)),
            waits=[bias_tok[i], c["tok_vones"] if i == 0 else None], sem=sA)
        kT_tok[i] = p.add("tensor", lambda e: (
            e.transpose(pT[:, 0:128], kro[:, 0:128], c["sb_id16"][:]),
            e.transpose(pT[:, 128:256], kro[:, 128:256], c["sb_id16"][:]))[-1],
            waits=[rope_tok], sem=sT)
        kev_tok[i] = p.add("vector", lambda e, i=i: (
            e.tensor_copy(a16.view(A16_B + i * 128, 128), pT[:, 0:128]),
            e.tensor_copy(a16.view(A16_B + S_GLOB + i * 128, 128), pT[:, 128:256]))[-1],
            waits=[kT_tok[i]], sem=sV)
    p.run(c["nc"])
    c["p2a_last_v"] = (sV, sV.n)
    c["p2a_last_s"] = (sA, sA.n)


def _l1_own(c):
    a16, a32 = c["a16"], c["a32"]
    sX, sT, sV, sA, sO = c["sX"], c["sT"], c["sV"], c["sA"], c["sO"]
    pA, pT, pD, pE = c["pA"], c["pT"], c["pD"], c["pE"]
    H = A16_H
    st = c["sb_stat"]
    x3o = [A32["x3"] + j * 1024 for j in range(3)]
    stage0 = a16.view(H + H16["stage0"], 1024)
    stage1 = a16.view(H + H16["stage1"], 1024)
    qro = a16.view(H + H16["qro"], 1024)
    chi = a16.view(H + H16["chi"], 64)
    clo = a16.view(H + H16["clo"], 64)
    wq_sb = lambda ic: a16.view(A16_C + ic * DM, DM)
    xnf = a32.view(A32["xnf"], 1024)
    scp = pE[:, 256:264]

    p = Prog()
    T = NT_OWN
    xn_tok = [None] * T; tT_tok = [None] * T; ev_tok = [None] * T
    sc_tok = [None] * T; am_tok = [None] * T; zT_tok = [None] * T; scp_tok = [None] * T
    zev_tok = [None] * T; zdma_tok = [None] * T; q_tok = [None] * T
    qb_tok = [None] * T; qro_tok = [None] * T; qT_tok = [None] * T
    qev_tok = [None] * T; qdma_tok = [None] * T; xndma_tok = [None] * T
    loT_tok = [None] * T; loev_tok = [None] * T; sq2_tok = [None] * T
    var_tok = [None] * T; var2_tok = [None] * T; z_tok = [None] * T
    xlo_tok = [None] * T
    prev = lambda lst, i: lst[i - 1] if i >= 1 else None

    p2a_v = c["p2a_last_v"]
    p2a_s = c["p2a_last_s"]

    for i in range(T):
        xv = a32.view(x3o[i % 3], 1024)
        t_x = p.add("sync", lambda e, i=i, xv=xv: e.dma_start(
            xv, c["x_own"][i * 128:(i + 1) * 128, :]),
            waits=[sq2_tok[i - 3] if i >= 3 else p2a_s], sem=c["sXT"], inc=16)
        t_sq, var_tok[i], t_rstd = _stats_chain(
            p, c, xv, xnf, 0, t_x, prev(var_tok, i) or p2a_v)
        xn_tok[i] = p.add("vector", lambda e, xv=xv: (
            e.tensor_scalar(stage0, xv, st[:, 6:7], st[:, 7:8], ALU.mult, ALU.add),
            e.tensor_scalar(xnf, xv, st[:, 6:7], st[:, 7:8], ALU.mult, ALU.add))[-1],
            waits=[t_rstd, prev(loT_tok, i) or p2a_v], sem=sV)
        tT_tok[i] = p.add("tensor", lambda e: [
            e.transpose(pT[:, ic * 128:(ic + 1) * 128],
                        stage0[:, ic * 128:(ic + 1) * 128], c["sb_id16"][:])
            for ic in range(8)][-1],
            waits=[xn_tok[i], prev(ev_tok, i) or p2a_v], sem=sT)
        ev_tok[i] = p.add("vector", lambda e: [
            e.tensor_copy(stage1[:, ic * 128:(ic + 1) * 128], pT[:, ic * 128:(ic + 1) * 128])
            for ic in range(8)][-1],
            waits=[tT_tok[i], prev(q_tok, i), xndma_tok[i - 1] if i >= 1 else None],
            sem=sV)
        xndma_tok[i] = p.add("sync", lambda e, i=i: e.dma_start(
            c["xn_d"][:].rearrange("p (x t) -> p x t", x=8)[:, :, i * 128:(i + 1) * 128],
            stage1[:].rearrange("p (x t) -> p x t", x=8)),
            waits=[ev_tok[i]], sem=sO, inc=16)
        xlo_tok[i] = p.add("vector", lambda e: e.tensor_sub(qro, xnf, stage0),
                           waits=[xn_tok[i], zdma_tok[i - 1] if i >= 1 else None],
                           sem=sV)
        loT_tok[i] = p.add("tensor", lambda e: [
            e.transpose(pT[:, ic * 128:(ic + 1) * 128],
                        qro[:, ic * 128:(ic + 1) * 128], c["sb_id16"][:])
            for ic in range(8)][-1],
            waits=[xlo_tok[i], ev_tok[i]], sem=sT)
        loev_tok[i] = p.add("vector", lambda e: [
            e.tensor_copy(stage0[:, ic * 128:(ic + 1) * 128], pT[:, ic * 128:(ic + 1) * 128])
            for ic in range(8)][-1],
            waits=[loT_tok[i]], sem=sV)
        def sc_mms(e, i=i):
            last = None
            for n, (stat, mov) in enumerate([(stage1, chi), (stage0, chi), (stage1, clo)]):
                for ic in range(8):
                    last = e.matmul(pE[:, 256 + n * 8:264 + n * 8],
                                    stat[:, ic * 128:(ic + 1) * 128],
                                    mov[:, ic * 8:(ic + 1) * 8],
                                    start=(ic == 0), stop=(ic == 7))
            return last
        sc_tok[i] = p.add("tensor", sc_mms,
                          waits=[loev_tok[i], prev(scp_tok, i),
                                 c["tok_clo"] if i == 0 else None], sem=sT)
        scsb = c["sb_scdbg"][:, i * 8:(i + 1) * 8]
        t_sc0 = p.add("scalar", lambda e, scsb=scsb: e.copy(scsb, pE[:, 256:264]),
                      waits=[sc_tok[i]], sem=sA)
        t_sc1 = p.add("vector", lambda e, scsb=scsb: e.tensor_add(scsb, scsb, pE[:, 264:272]),
                      waits=[t_sc0], sem=sV)
        scp_tok[i] = p.add("vector", lambda e, scsb=scsb: e.tensor_add(scsb, scsb, pE[:, 272:280]),
                          waits=[t_sc1], sem=sV)
        t_am0 = p.add("vector", lambda e, scsb=scsb: e.reduce_max(st[:, 16:17], scsb, axis=AX.X),
                      waits=[scp_tok[i], prev(am_tok, i)], sem=sV)
        t_am0b = p.add("vector", lambda e: e.tensor_scalar_add(
            st[:, 21:22], st[:, 16:17], -1e-4), waits=[t_am0], sem=sV)
        t_am1 = p.add("vector", lambda e, i=i, scsb=scsb: e.tensor_scalar(
            c["sb_amall"][:, i * 8:(i + 1) * 8], scsb, st[:, 21:22], None, ALU.is_ge),
            waits=[t_am0b], sem=sV)
        t_alpha = p.add("scalar", lambda e: e.activation(
            st[:, 17:18], st[:, 16:17], AF.Sigmoid), waits=[t_am1], sem=sA)
        am_tok[i] = p.add("vector", lambda e, i=i: e.tensor_scalar(
            c["sb_amall"][:, i * 8:(i + 1) * 8], c["sb_amall"][:, i * 8:(i + 1) * 8],
            st[:, 17:18], None, ALU.mult), waits=[t_alpha], sem=sV)
        sq2_tok[i], var2_tok[i], t_rstd2 = _stats_chain(
            p, c, xnf, xv, 8, xn_tok[i], prev(var2_tok, i) or p2a_v)
        z_tok[i] = p.add("vector", lambda e: e.tensor_scalar(
            stage0, xnf, st[:, 14:15], st[:, 15:16], ALU.mult, ALU.add),
            waits=[t_rstd2, loT_tok[i], am_tok[i]], sem=sV)
        zT_tok[i] = p.add("tensor", lambda e: [
            e.transpose(pT[:, ic * 128:(ic + 1) * 128],
                        stage0[:, ic * 128:(ic + 1) * 128], c["sb_id16"][:])
            for ic in range(8)][-1],
            waits=[z_tok[i], loev_tok[i], sc_tok[i]], sem=sT)
        zev_tok[i] = p.add("vector", lambda e: [
            e.tensor_copy(qro[:, ic * 128:(ic + 1) * 128], pT[:, ic * 128:(ic + 1) * 128])
            for ic in range(8)][-1],
            waits=[zT_tok[i], loT_tok[i]], sem=sV)
        zdma_tok[i] = p.add("sync", lambda e, i=i: e.dma_start(
            c["z_d"][:].rearrange("p (x t) -> p x t", x=8)[:, :, i * 128:(i + 1) * 128],
            qro[:].rearrange("p (x t) -> p x t", x=8)),
            waits=[zev_tok[i]], sem=sO, inc=16)
        q_tok[i] = p.add("tensor", lambda e: [
            e.matmul(pD[:, oh * 512:(oh + 1) * 512], stage1[:, ic * 128:(ic + 1) * 128],
                     wq_sb(ic)[:, oh * 512:(oh + 1) * 512],
                     start=(ic == 0), stop=(ic == 7))
            for oh in range(2) for ic in range(8)][-1],
            waits=[ev_tok[i], qb_tok[i - 1] if i >= 1 else None,
                   c["tok_wq"] if i == 0 else None], sem=sT)
        qb_tok[i] = p.add("vector", lambda e: [
            e.scalar_tensor_tensor(xnf[:, oh * 512:(oh + 1) * 512],
                                   pD[:, oh * 512:(oh + 1) * 512], 0.125,
                                   a32.view(A32["bq"] + oh * 512, 512),
                                   ALU.mult, ALU.add) for oh in range(2)][-1],
            waits=[q_tok[i], sq2_tok[i], z_tok[i],
                   c["tok_bq"] if i == 0 else None], sem=sV)
        qro_tok[i] = _rope_emit(
            p, c, xnf, qro, 16,
            a32.view(A32["cown"] + i * 32, 32), a32.view(A32["sown"] + i * 32, 32),
            qb_tok[i], extra_waits=[z_tok[i], zdma_tok[i]])
        qT_tok[i] = p.add("tensor", lambda e: [
            e.transpose(pT[:, ic * 128:(ic + 1) * 128],
                        qro[:, ic * 128:(ic + 1) * 128], c["sb_id16"][:])
            for ic in range(8)][-1],
            waits=[qro_tok[i], zev_tok[i]], sem=sT)
        qev_tok[i] = p.add("vector", lambda e: [
            e.tensor_copy(stage1[:, ic * 128:(ic + 1) * 128], pT[:, ic * 128:(ic + 1) * 128])
            for ic in range(8)][-1],
            waits=[qT_tok[i], xndma_tok[i], q_tok[i]], sem=sV)
        qdma_tok[i] = p.add("sync", lambda e, i=i: e.dma_start(
            c["q_d"][:].rearrange("p (x t) -> p x t", x=8)[:, :, i * 128:(i + 1) * 128],
            stage1[:].rearrange("p (x t) -> p x t", x=8)),
            waits=[qev_tok[i]], sem=sO, inc=16)
    p.run(c["nc"])
    c["own_tail"] = dict(qdma=qdma_tok[-1], am=am_tok[-1], v=(sV, sV.n),
                         s=(sA, sA.n), t=(sT, sT.n), o=(sO, sO.n))


def _l1_dbg_tail(c, names):
    sO, sX = c["sO"], c["sX"]
    a16 = c["a16"]
    p = Prog()
    w = [c["own_tail"]["v"], c["own_tail"]["s"], c["own_tail"]["t"],
         c["own_tail"]["o"]] if "own_tail" in c else []
    if "moe_tail" in c:
        w += [c["moe_tail"]]
    for k in ("attn_tail", "attn_tail_v", "attn_tail_t", "attn_tail_s"):
        if k in c:
            w += [c[k]]
    srcs = {
        "d_xn": lambda: c["xn_d"][:], "d_z": lambda: c["z_d"][:],
        "d_q": lambda: c["q_d"][:], "d_amask": lambda: c["sb_amall"][:],
        "d_kfm": lambda: a16.view(A16_B, 2 * S_GLOB),
        "d_vst": lambda: a16.view(A16_A, NT_PRE * 260),
        "d_sc": lambda: c["sb_scdbg"][:],
        "d_chi": lambda: a16.view(A16_H + H16["chi"], 128),
        "d_attn": lambda: c["attn_d"][:], "d_y": lambda: c["y_d"][:],
        "d_acc": lambda: c["a32"].view(A32["acc"], 16384),
    }
    for nm in names:
        p.add("sync", lambda e, nm=nm: e.dma_start(c["dbg_outs"][nm][:], srcs[nm]()),
              waits=w, sem=sO, inc=16)
    for nm in ("ypart", "ffn1"):
        p.add("gpsimd", lambda e, nm=nm: e.dma_start(c[nm][:, 0:128], c["xn_d"][:, 0:128]),
              waits=w, sem=sO, inc=16)
    p.add("sync", lambda e: e.dma_start(c["qp_o"][:, 0:128], c["xn_d"][:, 0:128]),
          waits=w, sem=sO, inc=16)
    p.add("gpsimd", lambda e: e.dma_start(c["kvpart"][0:64, 0:128], c["xn_d"][0:64, 0:128]),
          waits=w, sem=sO, inc=16)
    fin = (sO, sO.n)
    p.add("gpsimd", lambda e: e.wait_ge(fin[0].h, fin[1]), sem=None)
    p.run(c["nc"])


def _l1_attn(c):
    a16, a32 = c["a16"], c["a32"]
    sX, sG, sT, sV, sA, sO = c["sX"], c["sG"], c["sT"], c["sV"], c["sA"], c["sO"]
    pA, pB, pE = c["pA"], c["pB"], c["pE"]
    st = c["sb_stat"]
    H = A16_H
    KD = A16_C
    VD = A16_C + 4096
    AST = A16_C + 8256
    PT0 = A16_C + 10304
    PT1 = A16_C + 10560
    ot = c["diag_tail"] | {"o": c["own_tail"]["o"]}

    p = Prog()
    dt = c["diag_tail"]
    dg_tok = dt["v"]

    qs_dma = [None] * NSLOT
    qs_done = [None] * NSLOT
    stg_dma = [None] * NSLOT
    epi_tok = [None] * NSLOT
    last_recb = None
    last_t_out = None

    for s in range(NSLOT):
        qsl = H + (s % 2) * 2048
        R = _R(s)
        qs_dma[s] = p.add("sync", lambda e, s=s, qsl=qsl: e.dma_start(
            a16.t[:, qsl:qsl + 2048].rearrange("p (x t) -> p x t", x=8),
            c["q_d"][:].rearrange("p (x t) -> p x t", x=8)[:, :, s * 256:(s + 1) * 256]),
            waits=[ot["o"], qs_done[s - 2] if s >= 2 else None], sem=c["sQS"], inc=16)
        hd_toks = []
        for h in range(16):
            kvh = h // 4
            slot_h = INV[h]
            qrow = (slot_h % 2) * 64
            qcol = qsl + (slot_h // 2) * 256
            krow = (kvh % 2) * 64
            kbase = A16_B + (kvh // 2) * S_GLOB
            red_toks = []
            for qs2 in range(2):
                mcol = 16 + qs2
                for kb4 in range(R // 4):
                    pS = pA if kb4 % 2 == 0 else pB
                    t_mm = p.add("tensor", lambda e, pS=pS, qrow=qrow, qcol=qcol,
                                 qs2=qs2, krow=krow, kbase=kbase, kb4=kb4: e.matmul(
                        pS[:], a16.t[qrow:qrow + 64, qcol + qs2 * 128:qcol + qs2 * 128 + 128],
                        a16.t[krow:krow + 64, kbase + kb4 * 512:kbase + (kb4 + 1) * 512],
                        start=True, stop=True),
                        waits=[qs_dma[s],
                               red_toks[-2] if len(red_toks) >= 2 else last_t_out],
                        sem=sT)
                    if kb4 == 0:
                        t_red = p.add("vector", lambda e, pS=pS, mcol=mcol: e.reduce_max(
                            st[:, mcol:mcol + 1], pS[:], axis=AX.X),
                            waits=[t_mm], sem=sV)
                    else:
                        t_r0 = p.add("vector", lambda e, pS=pS: e.reduce_max(
                            st[:, 20:21], pS[:], axis=AX.X), waits=[t_mm], sem=sV)
                        t_red = p.add("vector", lambda e, mcol=mcol: e.tensor_max(
                            st[:, mcol:mcol + 1], st[:, mcol:mcol + 1], st[:, 20:21]),
                            waits=[t_r0], sem=sV)
                    red_toks.append(t_red)
            tT = p.add("tensor", lambda e: (
                e.transpose(pE[0:1, 256:384], st[:, 16:17], c["sb_idf"][:]),
                e.transpose(pE[0:1, 384:512], st[:, 17:18], c["sb_idf"][:]))[-1],
                waits=[red_toks[-1], last_recb], sem=sT)
            tmr = p.add("scalar", lambda e: (
                e.copy(c["sb_mrow"][0:1, 0:128], pE[0:1, 256:384]),
                e.copy(c["sb_mrow"][0:1, 128:256], pE[0:1, 384:512]))[-1],
                waits=[tT], sem=sA)
            tmb = p.add("tensor", lambda e: e.matmul(
                pE[:, 0:256], c["sb_ones1"][:], c["sb_mrow"][:], start=True, stop=True),
                waits=[tmr, last_t_out], sem=sT)
            tmp_cp = p.add("scalar", lambda e: e.copy(
                a32.view(A32["m_pl"], 256), pE[:, 0:256]), waits=[tmb], sem=sA)
            tdA = p.add("vector", lambda e: e.tensor_add(
                a32.view(A32["m_dA"], 256), a32.view(A32["m_pl"], 256), c["sb_maskA"][:]),
                waits=[tmp_cp], sem=sV)
            tdB = p.add("vector", lambda e: e.tensor_add(
                a32.view(A32["m_dB"], 256), a32.view(A32["m_pl"], 256), c["sb_maskB"][:]),
                waits=[tmp_cp], sem=sV)
            fb_base = sum(_R(s2) - 2 for s2 in range(s))
            av_toks = []
            exp_toks = [None, None]
            for j in range(R - 2 + 2):
                diag = j >= R - 2
                dd = j - (R - 2)
                par = j % 2
                pS = pA if par == 0 else pB
                ptile = PT0 if par == 0 else PT1
                tmps = A32["tmpS0"] if par == 0 else A32["tmpS1"]
                if not diag:
                    stat = (krow, kbase + j * 128)
                    mop = A32["m_pl"]
                else:
                    stat = (krow, KD + (kvh // 2) * 2048 + (s * 2 + dd) * 128)
                    mop = A32["m_dA"] if dd == 0 else A32["m_dB"]
                t_mm = p.add("tensor", lambda e, pS=pS, stat=stat, qrow=qrow, qcol=qcol: e.matmul(
                    pS[:, 0:256], a16.t[stat[0]:stat[0] + 64, stat[1]:stat[1] + 128],
                    a16.t[qrow:qrow + 64, qcol:qcol + 256], start=True, stop=True),
                    waits=[tmb if j == 0 else None,
                           red_toks[-1] if j < 2 else None,
                           exp_toks[par] if j >= 2 else None,
                           tdA if diag and dd == 0 else None,
                           tdB if diag and dd == 1 else None,
                           dg_tok if (s == 0 and h == 0 and diag) else None], sem=sT)
                t_sub = p.add("vector", lambda e, pS=pS, tmps=tmps, mop=mop: e.tensor_sub(
                    a32.view(tmps, 256), pS[:, 0:256], a32.view(mop, 256)),
                    waits=[t_mm], sem=sV)
                if not diag:
                    fb = c["sb_flag"][:, fb_base + j:fb_base + j + 1]
                    t_exp = p.add("scalar", lambda e, tmps=tmps, ptile=ptile, fb=fb:
                                  e.activation(a16.view(ptile, 256), a32.view(tmps, 256),
                                               AF.Exp, bias=fb),
                                  waits=[t_sub], sem=sA)
                else:
                    t_exp = p.add("scalar", lambda e, tmps=tmps, ptile=ptile:
                                  e.activation(a16.view(ptile, 256), a32.view(tmps, 256),
                                               AF.Exp),
                                  waits=[t_sub], sem=sA)
                exp_toks[par] = t_exp
                vstat = (VD + (s * 2 + dd) * 260 + 65 * kvh) if diag else (j * 260 + 65 * kvh)
                t_av = p.add("tensor", lambda e, vstat=vstat, ptile=ptile, j=j: e.matmul(
                    pE[0:65, 0:256], a16.t[:, vstat:vstat + 65], a16.view(ptile, 256),
                    start=(j == 0), stop=(j == R - 1)),
                    waits=[t_exp], sem=sT)
                av_toks.append(t_av)
            t_dr = p.add("scalar", lambda e: e.copy(c["sb_drow"][:], pE[64:65, 0:256]),
                         waits=[av_toks[-1]], sem=sA)
            t_rc = p.add("vector", lambda e: e.reciprocal(c["sb_drow"][:], c["sb_drow"][:]),
                         waits=[t_dr], sem=sV)
            t_db = p.add("tensor", lambda e: e.matmul(
                pE[0:64, 256:512], c["sb_ones1"][0:1, 0:64], c["sb_drow"][:],
                start=True, stop=True), waits=[t_rc, tmr], sem=sT)
            t_rb = p.add("scalar", lambda e: e.copy(
                a32.view(A32["recb"], 256)[0:64, :], pE[0:64, 256:512]),
                waits=[t_db], sem=sA)
            last_recb = t_rb
            t_out = p.add("vector", lambda e, qrow=qrow, slot_h=slot_h, s=s: e.tensor_tensor(
                a16.t[qrow:qrow + 64, AST + (slot_h // 2) * 256:AST + (slot_h // 2) * 256 + 256],
                pE[0:64, 0:256], a32.view(A32["recb"], 256)[0:64, :], ALU.mult),
                waits=[t_rb, stg_dma[s - 1] if (s >= 1 and h <= 1) else None], sem=sV)
            hd_toks.append(t_out)
            epi_tok[s] = t_out
            last_t_out = t_out
        qs_done[s] = av_toks[-1]
        stg_dma[s] = p.add("sync", lambda e, s=s: e.dma_start(
            c["attn_d"][:].rearrange("p (x t) -> p x t", x=8)[:, :, s * 256:(s + 1) * 256],
            a16.t[:, AST:AST + 2048].rearrange("p (x t) -> p x t", x=8)),
            waits=hd_toks, sem=sO, inc=16)
    p.run(c["nc"])
    c["attn_tail"] = stg_dma[-1]
    c["attn_tail_v"] = (sV, sV.n)
    c["attn_tail_t"] = (sT, sT.n)
    c["attn_tail_s"] = (sA, sA.n)


def _l1_p2a_diag(c):
    a16, a32 = c["a16"], c["a32"]
    sX, sT, sV, sA = c["sX"], c["sT"], c["sV"], c["sA"]
    pA, pT = c["pA"], c["pT"]
    H = A16_H
    st = c["sb_stat"]
    x3o = [A32["x3"] + j * 1024 for j in range(3)]
    stage0 = a16.view(H + H16["stage0"], 1024)
    stage1 = a16.view(H + H16["stage1"], 1024)
    kro = a16.view(H + H16["kro"], 256)
    wkv_sb = lambda ic: a16.view(A16_C + 12288 + ic * 512, 512)
    KD = A16_C
    VD = A16_C + 4096
    ot = c["own_tail"]

    p = Prog()
    N = 16
    xn_tok = [None] * N; tT_tok = [None] * N; ev_tok = [None] * N
    kv_tok = [None] * N; bias_tok = [None] * N; vcp_tok = [None] * N
    kT_tok = [None] * N; kev_tok = [None] * N; var_tok = [None] * N
    prev = lambda lst, i: lst[i - 1] if i >= 1 else None
    t_ones = p.add("vector", lambda e: e.memset(
        a16.view(VD, 16 * 260).rearrange("p (b x) -> p b x", x=65)[:, :, 64:65], 1.0),
        waits=[ot["t"], ot["v"], ot["s"]], sem=c["sV"])

    for i in range(N):
        xv = a32.view(x3o[i % 3], 1024)
        t_x = p.add("sync", lambda e, i=i, xv=xv: e.dma_start(
            xv, c["x_diag"][i * 128:(i + 1) * 128, :]),
            waits=[xn_tok[i - 3] if i >= 3 else ot["s"]], sem=c["sXT"], inc=16)
        sq_tok, var_tok[i], t_rstd = _stats_chain(
            p, c, xv, a32.view(A32["xnf"], 1024), 0, t_x, prev(var_tok, i) or ot["v"])
        xn_tok[i] = p.add("vector", lambda e, xv=xv: e.tensor_scalar(
            stage0, xv, st[:, 6:7], st[:, 7:8], ALU.mult, ALU.add),
            waits=[t_rstd, prev(tT_tok, i) or ot["t"]], sem=sV)
        tT_tok[i] = p.add("tensor", lambda e: [
            e.transpose(pT[:, ic * 128:(ic + 1) * 128],
                        stage0[:, ic * 128:(ic + 1) * 128], c["sb_id16"][:])
            for ic in range(8)][-1],
            waits=[xn_tok[i], prev(ev_tok, i), prev(kev_tok, i)], sem=sT)
        ev_tok[i] = p.add("vector", lambda e: [
            e.tensor_copy(stage1[:, ic * 128:(ic + 1) * 128], pT[:, ic * 128:(ic + 1) * 128])
            for ic in range(8)][-1],
            waits=[tT_tok[i], prev(kv_tok, i)], sem=sV)
        kv_tok[i] = p.add("tensor", lambda e: [
            e.matmul(pA[:], stage1[:, ic * 128:(ic + 1) * 128], wkv_sb(ic),
                     start=(ic == 0), stop=(ic == 7)) for ic in range(8)][-1],
            waits=[ev_tok[i], prev(bias_tok, i)], sem=sT)
        bias_tok[i] = p.add("vector", lambda e: e.tensor_tensor(
            a32.view(A32["kvb"], 512), pA[:], a32.view(A32["bkv"], 512), ALU.add),
            waits=[kv_tok[i], prev(vcp_tok, i)], sem=sV)
        rope_tok = _rope_emit(
            p, c, a32.view(A32["kvb"], 512), kro, 4,
            a32.view(11392 + i * 32, 32), a32.view(11904 + i * 32, 32),
            bias_tok[i], extra_waits=[prev(kT_tok, i)])
        vcp_tok[i] = p.add("scalar", lambda e, i=i: e.copy(
            a16.view(VD + i * 260, 260).rearrange("p (h x) -> p h x", h=4)[:, :, 0:64],
            a32.view(A32["kvb"] + 256, 256).rearrange("p (h x) -> p h x", h=4)),
            waits=[bias_tok[i], t_ones if i == 0 else None], sem=sA)
        kT_tok[i] = p.add("tensor", lambda e: (
            e.transpose(pT[:, 0:128], kro[:, 0:128], c["sb_id16"][:]),
            e.transpose(pT[:, 128:256], kro[:, 128:256], c["sb_id16"][:]))[-1],
            waits=[rope_tok], sem=sT)
        kev_tok[i] = p.add("vector", lambda e, i=i: (
            e.tensor_copy(a16.view(KD + i * 128, 128), pT[:, 0:128]),
            e.tensor_copy(a16.view(KD + 2048 + i * 128, 128), pT[:, 128:256]))[-1],
            waits=[kT_tok[i]], sem=sV)
    p.run(c["nc"])
    c["diag_tail"] = dict(v=(sV, sV.n), s=(sA, sA.n), t=(sT, sT.n))


def _l1_wo(c):
    a16, a32 = c["a16"], c["a32"]
    sX, sG, sT, sV, sA, sO = c["sX"], c["sG"], c["sT"], c["sV"], c["sA"], c["sO"]
    pA, pB, pD, pT = c["pA"], c["pB"], c["pD"], c["pT"]
    at = c["attn_tail"]
    WO = A16_C
    AB0 = A16_C + 8192
    AB1 = A16_C + 12288
    x3o = [A32["x3"] + j * 1024 for j in range(3)]
    p = Prog()

    two = p.add("gpsimd", lambda e: e.dma_start(
        a16.view(WO, 8 * DM).rearrange("p (c o) -> p c o", c=8),
        c["wo"][:].rearrange("(c p) o -> p c o", p=128)),
        waits=[c["attn_tail_t"]], sem=c["sW"], inc=16)
    ab_dma = [None] * 4
    ab_done = [None] * 4
    ystg_dma = [None] * 4
    xT_tok = [None, None, None, None]
    last_ev = c["attn_tail_v"]
    last_sq = c["attn_tail_s"]
    for tc in range(4):
        ab = AB0 if tc % 2 == 0 else AB1
        ab_dma[tc] = p.add("sync", lambda e, tc=tc, ab=ab: e.dma_start(
            a16.t[:, ab:ab + 4096].rearrange("p (x t) -> p x t", x=8),
            c["attn_d"][:].rearrange("p (x t) -> p x t", x=8)[:, :, tc * 512:(tc + 1) * 512]),
            waits=[at if tc == 0 else None, ab_done[tc - 2] if tc >= 2 else None],
            sem=c["sAB"] if tc % 2 == 0 else c["sAB2"], inc=16)
        ev_toks = []
        for t4 in range(4):
            i = tc * 4 + t4
            xv = a32.view(x3o[i % 3], 1024)
            t_x = p.add("sync", lambda e, i=i, xv=xv: e.dma_start(
                xv, c["x_own"][i * 128:(i + 1) * 128, :]),
                waits=[xT_tok[(i - 3) % 4] if i >= 3 else last_sq], sem=c["sXT"], inc=16)
            tT = p.add("tensor", lambda e, xv=xv: [
                e.transpose(pT[:, 0:1024].bitcast(F32)[:, ic * 128:(ic + 1) * 128]
                            if False else pD[:, ic * 128:(ic + 1) * 128],
                            xv[:, ic * 128:(ic + 1) * 128], c["sb_idf"][:])
                for ic in range(8)][-1],
                waits=[t_x, last_ev], sem=sT)
            xT_tok[i % 4] = tT
            t_ev = p.add("vector", lambda e, t4=t4: [
                e.tensor_copy(a32.view(A32["yst"] + ic * 512 + t4 * 128, 128),
                              pD[:, ic * 128:(ic + 1) * 128]) for ic in range(8)][-1],
                waits=[tT, ystg_dma[tc - 1] if (tc >= 1 and t4 == 0) else None], sem=sV)
            last_ev = t_ev
            ev_toks.append(t_ev)
        oc_toks = []
        last_comb = None
        for oc in range(8):
            pS = pD[:, 1024:1536] if oc % 2 == 0 else pD[:, 1536:2048]
            t_mm = p.add("tensor", lambda e, pS=pS, oc=oc, ab=ab: [
                e.matmul(pS, a16.t[:, WO + ic * DM + oc * 128:WO + ic * DM + oc * 128 + 128],
                         a16.t[:, ab + ic * 512:ab + (ic + 1) * 512],
                         start=(ic == 0), stop=(ic == 7)) for ic in range(8)][-1],
                waits=[ab_dma[tc], two if (tc == 0 and oc == 0) else None,
                       ev_toks[-1] if oc == 0 else None,
                       oc_toks[-1] if oc_toks else
                       (ab_done[tc - 1] if tc >= 1 else None)], sem=sT)
            t_comb = p.add("vector", lambda e, pS=pS, oc=oc: e.scalar_tensor_tensor(
                a32.view(A32["yst"] + oc * 512, 512), pS, c["sb_boc"][:, oc:oc + 1],
                a32.view(A32["yst"] + oc * 512, 512), ALU.add, ALU.add),
                waits=[t_mm, ev_toks[-1] if oc == 0 else None], sem=sV)
            oc_toks.append(t_comb)
            last_comb = t_comb
        ab_done[tc] = last_comb
        ystg_dma[tc] = p.add("sync", lambda e, tc=tc: e.dma_start(
            c["y_d"][:].rearrange("p (x t) -> p x t", x=8)[:, :, tc * 512:(tc + 1) * 512],
            a32.view(A32["yst"], 4096).rearrange("p (x t) -> p x t", x=8)),
            waits=oc_toks, sem=sO, inc=16)
    p.run(c["nc"], serialize=True)
    c["wo_tail"] = dict(o=ystg_dma[-1], v=(sV, sV.n), s=(sA, sA.n), t=(sT, sT.n))


def _l1_moe(c):
    a16, a32 = c["a16"], c["a32"]
    sX, sG, sT, sV, sA, sO, sP = (c["sX"], c["sG"], c["sT"], c["sV"], c["sA"],
                                  c["sO"], c["sP"])
    pA, pD, pE = c["pA"], c["pD"], c["pE"]
    H = A16_H
    wt = c["wo_tail"]
    p = Prog()

    t_z = p.add("sync", lambda e: e.dma_start(
        a16.view(A16_C, 16384), c["z_d"][:]), waits=[wt["t"], wt["v"]], sem=c["sZL"], inc=16)
    t_acc = p.add("gpsimd", lambda e: e.dma_start(
        a32.view(A32["acc"], 16384), c["xn_d"][:]),
        waits=[wt["o"], wt["v"]], sem=c["sAC"], inc=16)

    t_dbg0 = p.add("scalar", lambda e: e.copy(
        c["sb_scdbg"][:, 0:64], a32.view(A32["acc"], 64)), waits=[t_acc], sem=c["sA"])
    c["_dbg0"] = t_dbg0
    wbuf_done = [wt["t"], wt["t"]]
    wdma = [None, None]
    prev_evac = None
    prev_rm = None
    prev_rt = None
    prev_hb = None
    rm_war = [None] * 8
    prev_bc = None
    bc_cp = None
    first_bc = True
    first_h = True
    for e8 in range(8):
        for qq in range(4):
            par = (e8 * 4 + qq) % 2
            wb = A16_A if par == 0 else A16_B
            sGP = c["sGA"] if par == 0 else c["sGB"]
            p.add("gpsimd", lambda e, e8=e8, qq=qq, wb=wb: e.dma_start(
                a16.t[:, wb:wb + 8192].rearrange("p (c o) -> p c o", c=8),
                c["w1t"][e8][:, qq * 1024:(qq + 1) * 1024]
                .rearrange("(c p) o -> p c o", p=128)),
                waits=[wbuf_done[par]], sem=sGP, inc=16)
            wdma[par] = p.add("gpsimd", lambda e, e8=e8, qq=qq, wb=wb: e.dma_start(
                a16.t[:, wb + 8192:wb + 16384].rearrange("p (c o) -> p c o", c=8),
                c["w2t"][e8][qq * 1024:(qq + 1) * 1024, :]
                .rearrange("(c p) o -> p c o", p=128)),
                sem=sGP, inc=16)
            for tt in range(8):
                tt1 = p.add("tensor", lambda e, e8=e8, tt=tt: (
                    e.transpose(pE[0:1, 256:384],
                                c["sb_amall"][:, (2 * tt) * 8 + e8:(2 * tt) * 8 + e8 + 1],
                                c["sb_idf"][:]),
                    e.transpose(pE[0:1, 384:512],
                                c["sb_amall"][:, (2 * tt + 1) * 8 + e8:(2 * tt + 1) * 8 + e8 + 1],
                                c["sb_idf"][:]))[-1],
                    waits=[bc_cp, c["own_tail"]["v"] if first_bc else None,
                           wt["s"] if first_bc else None], sem=sT)
                tt2 = p.add("scalar", lambda e: (
                    e.copy(c["sb_arow"][0:1, 0:128], pE[0:1, 256:384]),
                    e.copy(c["sb_arow"][0:1, 128:256], pE[0:1, 384:512]))[-1],
                    waits=[tt1], sem=sA)
                tt3 = p.add("tensor", lambda e: e.matmul(
                    pE[:, 0:256], c["sb_ones1"][:], c["sb_arow"][:], start=True, stop=True),
                    waits=[tt2], sem=sT)
                bc_cp = p.add("scalar", lambda e: e.copy(
                    a32.view(A32["abc"], 256), pE[:, 0:256]),
                    waits=[tt3, prev_bc], sem=sA)
                first_bc = False
                t_h = p.add("vector", lambda e, e8=e8, tt=tt: [
                    e.tensor_scalar(a16.view(H + H16["hbuf"] + ic * 256, 256),
                                    a16.view(A16_C + ic * 2048 + tt * 256, 256),
                                    c["sb_gmc"][:, e8 * 8 + ic:e8 * 8 + ic + 1],
                                    c["sb_btc"][:, e8 * 8 + ic:e8 * 8 + ic + 1],
                                    ALU.mult, ALU.add) for ic in range(8)][-1],
                    waits=[t_z if first_h else None, prev_hb], sem=sV)
                first_h = False
                last_y1 = None
                last_y2 = None
                rt_v = a16.view(A16_B + 16384, 256)
                rm_all = lambda fc: a16.view(H + 2048 + fc * 256, 256)
                rm_toks = []
                for fc in range(8):
                    t_y1 = p.add("tensor", lambda e, wb=wb, fc=fc: [
                        e.matmul(pA[:, 0:256],
                                 a16.t[:, wb + ic * 1024 + fc * 128:wb + ic * 1024 + fc * 128 + 128],
                                 a16.view(H + H16["hbuf"] + ic * 256, 256),
                                 start=(ic == 0), stop=(ic == 7)) for ic in range(8)][-1],
                        waits=[t_h if fc == 0 else None,
                               wdma[par] if tt == 0 and fc == 0 else None,
                               prev_rt, prev_evac if fc == 0 else None], sem=sT)
                    last_y1 = t_y1
                    t_r = p.add("scalar", lambda e, e8=e8, qq=qq, fc=fc: e.activation(
                        rt_v, pA[:, 0:256], AF.Relu,
                        bias=c["sb_b1c"][:, e8 * 32 + qq * 8 + fc:e8 * 32 + qq * 8 + fc + 1]),
                        waits=[t_y1, prev_rm], sem=sA)
                    prev_rt = t_r
                    t_rm = p.add("vector", lambda e, fc=fc: e.tensor_tensor(
                        rm_all(fc), rt_v, a32.view(A32["abc"], 256), ALU.mult),
                        waits=[t_r, bc_cp if fc == 0 else None,
                               rm_war[fc]], sem=sV)
                    prev_rm = t_rm
                    rm_toks.append(t_rm)
                for oc in range(8):
                    t_y2 = p.add("tensor", lambda e, wb=wb, oc=oc: [
                        e.matmul(pD[:, oc * 256:(oc + 1) * 256],
                                 a16.t[:, wb + 8192 + fc * 1024 + oc * 128:wb + 8192 + fc * 1024 + oc * 128 + 128],
                                 rm_all(fc), start=(fc == 0), stop=(fc == 7))
                        for fc in range(8)][-1],
                        waits=[rm_toks[-1] if oc == 0 else None,
                               prev_evac if oc == 0 else None], sem=sT)
                    last_y2 = t_y2
                for fc in range(8):
                    rm_war[fc] = last_y2
                prev_hb = last_y1
                if e8 == 0 and qq == 0 and tt == 0:
                    p.add("scalar", lambda e: e.copy(
                        c["sb_scdbg"][:, 64:128], pD[:, 0:64]),
                        waits=[last_y2], sem=c["sA"])
                ev = None
                for oc in range(8):
                    if qq == 0:
                        t1 = p.add("vector", lambda e, oc=oc, e8=e8: e.scalar_tensor_tensor(
                            a32.view(A32["sttmp"], 256), a32.view(A32["abc"], 256),
                            c["sb_b2c"][:, e8 * 8 + oc:e8 * 8 + oc + 1],
                            pD[:, oc * 256:(oc + 1) * 256], ALU.mult, ALU.add),
                            waits=[last_y2, ev, bc_cp,
                                   t_acc if e8 == 0 and tt == 0 and oc == 0 else None,
                                   c["_dbg0"] if e8 == 0 and tt == 0 and oc == 0 else None,
                                   (c["sA"], c["sA"].n) if e8 == 0 and qq == 0 and tt == 0 and oc == 0 else None],
                            sem=sV)
                        ev = p.add("vector", lambda e, oc=oc, tt=tt: e.tensor_add(
                            a32.view(A32["acc"] + oc * 2048 + tt * 256, 256),
                            a32.view(A32["acc"] + oc * 2048 + tt * 256, 256),
                            a32.view(A32["sttmp"], 256)),
                            waits=[t1], sem=sV)
                    else:
                        ev = p.add("vector", lambda e, oc=oc, tt=tt: e.tensor_add(
                            a32.view(A32["acc"] + oc * 2048 + tt * 256, 256),
                            a32.view(A32["acc"] + oc * 2048 + tt * 256, 256),
                            pD[:, oc * 256:(oc + 1) * 256]),
                            waits=[last_y2, t_acc if e8 == 0 and tt == 0 and oc == 0 else None,
                                   ev], sem=sV)
                prev_evac = ev
                prev_bc = ev if qq == 0 else prev_rt
            wbuf_done[par] = prev_evac
    fold_dma = []
    last_fold = None
    for k in range(32):
        slot = 16384 if k % 2 == 0 else 16896
        t_l = p.add("sync", lambda e, k=k, slot=slot: e.dma_start(
            a32.view(slot, 512), c["y_d"][:, k * 512:(k + 1) * 512]),
            waits=[prev_evac if k == 0 else None,
                   fold_dma[k - 2] if k >= 2 else None], sem=c["sFL"], inc=16)
        t_a = p.add("vector", lambda e, k=k, slot=slot: e.tensor_add(
            a32.view(slot, 512), a32.view(slot, 512), a32.view(A32["acc"] + k * 512, 512)),
            waits=[t_l, last_fold], sem=sV)
        last_fold = t_a
        t_o = p.add("sync", lambda e, k=k, slot=slot: e.dma_start(
            c["ypart"][:, k * 512:(k + 1) * 512], a32.view(slot, 512)),
            waits=[t_a], sem=sO, inc=16)
        fold_dma.append(t_o)
    p.run(c["nc"])
    c["moe_tail"] = fold_dma[-1]
    c["moe_tail_v"] = (sV, sV.n)
    c["moe_tail_t"] = (sT, sT.n)
    c["moe_tail_s"] = (sA, sA.n)
    c["moe_tail_g"] = (sG, sG.n)



def _l1_block1(c):
    a16, a32 = c["a16"], c["a32"]
    sX, sT, sV, sA, sO = c["sX"], c["sT"], c["sV"], c["sA"], c["sO"]
    pA, pB, pD, pT = c["pA"], c["pB"], c["pD"], c["pT"]
    st = c["sb_stat"]
    H = A16_H
    mt = c["moe_tail"]
    p = Prog()
    w = lambda *t: [x for x in t if x]

    t0 = p.add("sync", lambda e: e.dma_start(c["sb_row"][:], c["qn_r"][:]),
               waits=[mt, c["moe_tail_v"], c["moe_tail_t"], c["moe_tail_s"]],
               sem=sX, inc=16)
    for half in range(2):
        tm = p.add("tensor", lambda e, half=half: e.matmul(
            pA[:], c["sb_ones1"][:], c["sb_row"][0:1, half * 512:(half + 1) * 512],
            start=True, stop=True), sem=sT)
        p.add("scalar", lambda e, half=half: e.copy(
            a32.view(A32["qnb"] + half * 512, 512), pA[:]), sem=sA)
    p.add("sync", lambda e: e.dma_start(c["sb_row"][:], c["kn_r"][:]), sem=sX, inc=16)
    for half in range(2):
        p.add("tensor", lambda e, half=half: e.matmul(
            pA[:], c["sb_ones1"][:], c["sb_row"][0:1, half * 512:(half + 1) * 512],
            start=True, stop=True), sem=sT)
        p.add("scalar", lambda e, half=half: e.copy(
            a32.view(A32["knb"] + half * 512, 512), pA[:]), sem=sA)
    p.add("gpsimd", lambda e: e.dma_start(
        a16.view(0, 24576).rearrange("p (c o) -> p c o", c=8),
        c["wqkv"][:].rearrange("(c p) o -> p c o", p=128)), sem=c["sAC"], inc=16)
    p.add("vector", lambda e: e.memset(st[:, 23:24], 1.0), sem=sV)
    p.add("vector", lambda e: e.memset(a32.view(6144, 1040), 0.0), sem=sV)

    for sub in range(4):
        p.add("vector", lambda e: (
            e.memset(a32.view(A32["bc1"], 512)[0:1, :], 0.0),
            e.memset(a32.view(A32["bc2"], 512)[0:1, :], 0.0))[-1], sem=sV)
        for c8 in range(8):
            p.add("sync", lambda e, sub=sub, c8=c8: e.dma_start(
                a32.view(A32["ysl"] + (c8 % 2) * 1024, 512),
                c["ypart"][:, c8 * 2048 + sub * 512:c8 * 2048 + sub * 512 + 512]),
                sem=c["sFL"], inc=16)
            p.add("tensor", lambda e, c8=c8: e.matmul(
                pA[0:1, 0:512], st[:, 23:24], a32.view(A32["ysl"] + (c8 % 2) * 1024, 512),
                start=True, stop=True), sem=sT)
            p.add("vector", lambda e: e.tensor_add(
                a32.view(A32["bc2"], 512)[0:1, :], a32.view(A32["bc2"], 512)[0:1, :],
                pA[0:1, 0:512]), sem=sV)
            p.add("scalar", lambda e, c8=c8: e.activation(
                a32.view(A32["sttmp"], 512), a32.view(A32["ysl"] + (c8 % 2) * 1024, 512),
                AF.Square), sem=sA)
            p.add("tensor", lambda e: e.matmul(
                pA[0:1, 0:512], st[:, 23:24], a32.view(A32["sttmp"], 512),
                start=True, stop=True), sem=sT)
            p.add("vector", lambda e: e.tensor_add(
                a32.view(A32["bc1"], 512)[0:1, :], a32.view(A32["bc1"], 512)[0:1, :],
                pA[0:1, 0:512]), sem=sV)
        p.add("vector", lambda e: e.tensor_scalar_mul(
            a32.view(A32["bc2"], 512)[0:1, :], a32.view(A32["bc2"], 512)[0:1, :],
            1.0 / 1024.0), sem=sV)
        p.add("vector", lambda e: e.tensor_scalar_mul(
            a32.view(A32["bc1"], 512)[0:1, :], a32.view(A32["bc1"], 512)[0:1, :],
            1.0 / 1024.0), sem=sV)
        p.add("vector", lambda e: e.tensor_tensor(
            a32.view(A32["sttmp"], 512)[0:1, :], a32.view(A32["bc2"], 512)[0:1, :],
            a32.view(A32["bc2"], 512)[0:1, :], ALU.mult), sem=sV)
        p.add("vector", lambda e: e.tensor_sub(
            a32.view(A32["bc1"], 512)[0:1, :], a32.view(A32["bc1"], 512)[0:1, :],
            a32.view(A32["sttmp"], 512)[0:1, :]), sem=sV)
        p.add("scalar", lambda e: e.activation(
            a32.view(A32["bc1"], 512)[0:1, :], a32.view(A32["bc1"], 512)[0:1, :],
            AF.Sqrt, bias=st[0:1, 18:19]), sem=sA)
        p.add("vector", lambda e: e.reciprocal(
            a32.view(A32["bc1"], 512)[0:1, :], a32.view(A32["bc1"], 512)[0:1, :]), sem=sV)
        p.add("tensor", lambda e: e.matmul(
            pA[:, 0:512], c["sb_ones1"][:], a32.view(A32["bc1"], 512)[0:1, :],
            start=True, stop=True), sem=sT)
        p.add("tensor", lambda e: e.matmul(
            pB[:, 0:512], c["sb_ones1"][:], a32.view(A32["bc2"], 512)[0:1, :],
            start=True, stop=True), sem=sT)
        for c8 in range(8):
            p.add("sync", lambda e, sub=sub, c8=c8: e.dma_start(
                a32.view(A32["ysl"] + (c8 % 2) * 1024, 512),
                c["ypart"][:, c8 * 2048 + sub * 512:c8 * 2048 + sub * 512 + 512]),
                sem=c["sFL"], inc=16)
            p.add("vector", lambda e, c8=c8: e.tensor_tensor(
                a32.view(A32["sttmp"], 512), a32.view(A32["ysl"] + (c8 % 2) * 1024, 512),
                pB[:, 0:512], ALU.subtract), sem=sV)
            p.add("vector", lambda e, sub=sub, c8=c8: e.tensor_tensor(
                a16.view(A16_C + c8 * 2048 + sub * 512, 512), a32.view(A32["sttmp"], 512),
                pA[:, 0:512], ALU.mult), sem=sV)

    for stt_ in range(16):
        xtile = lambda ic, stt_=stt_: a16.view(A16_C + ic * 2048 + stt_ * 128, 128)
        for nm, obase, dstH, nbofs in (("k", DM, 0, A32["knb"]), ("q", 0, 2176, A32["qnb"])):
            for half in range(2):
                p.add("tensor", lambda e, obase=obase, half=half, xtile=xtile: [
                    e.matmul(pA[:, 0:512] if half == 0 else pB[:, 0:512], xtile(ic),
                             a16.view(ic * 3072 + obase + half * 512, 512),
                             start=(ic == 0), stop=(ic == 7)) for ic in range(8)][-1],
                    sem=sT)
                p.add("scalar", lambda e, half=half: e.activation(
                    a32.view(A32["sttmp"], 512), (pA if half == 0 else pB)[:, 0:512],
                    AF.Square, accum_out=st[:, 20 + half:21 + half]), sem=sA)
            p.add("vector", lambda e: e.tensor_add(st[:, 22:23], st[:, 20:21], st[:, 21:22]), sem=sV)
            p.add("vector", lambda e: e.tensor_scalar_mul(st[:, 22:23], st[:, 22:23], 1.0 / 1024.0), sem=sV)
            p.add("scalar", lambda e: e.activation(st[:, 22:23], st[:, 22:23], AF.Sqrt,
                                                   bias=st[:, 19:20]), sem=sA)
            p.add("vector", lambda e: e.reciprocal(st[:, 22:23], st[:, 22:23]), sem=sV)
            for half in range(2):
                p.add("vector", lambda e, half=half, nbofs=nbofs: e.tensor_tensor(
                    a32.view(A32["sttmp"], 512), (pA if half == 0 else pB)[:, 0:512],
                    a32.view(nbofs + half * 512, 512), ALU.mult), sem=sV)
                p.add("scalar", lambda e, half=half, dstH=dstH: e.activation(
                    a16.view(H + dstH + half * 512, 512), a32.view(A32["sttmp"], 512),
                    AF.Relu, scale=st[:, 22:23]), sem=sA)
        for half in range(2):
            p.add("tensor", lambda e, half=half, xtile=xtile: [
                e.matmul(pA[:, 0:512], xtile(ic),
                         a16.view(ic * 3072 + 2048 + half * 512, 512),
                         start=(ic == 0), stop=(ic == 7)) for ic in range(8)][-1],
                sem=sT)
            p.add("scalar", lambda e, half=half: e.copy(
                a16.view(H + 1024, 1040).rearrange("p (h x) -> p h x", x=65)
                [:, half * 8:(half + 1) * 8, 0:64],
                pA[:, 0:512].rearrange("p (h x) -> p h x", x=64)), sem=sA)
        p.add("vector", lambda e: e.memset(
            a16.view(H + 1024, 1040).rearrange("p (h x) -> p h x", x=65)[:, :, 64:65], 1.0),
            sem=sV)
        p.add("tensor", lambda e: [
            e.matmul(pD[0:64, h * 128:h * 128 + 65], a16.view(H + h * 64, 64),
                     a16.view(H + 1024 + h * 65, 65), start=True, stop=True)
            for h in range(16)][-1], sem=sT)
        p.add("vector", lambda e: e.tensor_add(
            a32.view(6144, 1040).rearrange("p (h x) -> p h x", x=65)[0:64],
            a32.view(6144, 1040).rearrange("p (h x) -> p h x", x=65)[0:64],
            pD[0:64, 0:2048].rearrange("p (h x) -> p h x", x=128)[:, :, 0:65]), sem=sV)
        p.add("tensor", lambda e: [
            e.transpose(pT[:, ic * 128:(ic + 1) * 128],
                        a16.view(H + 2176 + ic * 128, 128), c["sb_id16"][:])
            for ic in range(8)][-1], sem=sT)
        p.add("vector", lambda e: [
            e.tensor_copy(a16.view(H, 1024)[:, ic * 128:(ic + 1) * 128],
                          pT[:, ic * 128:(ic + 1) * 128]) for ic in range(8)][-1], sem=sV)
        p.add("sync", lambda e, stt_=stt_: e.dma_start(
            c["qp_o"][:].rearrange("p (x t) -> p x t", x=8)[:, :, stt_ * 128:(stt_ + 1) * 128],
            a16.view(H, 1024).rearrange("p (x t) -> p x t", x=8)),
            sem=sO, inc=16)
    p.add("sync", lambda e: e.dma_start(c["kvpart"][:], a32.view(6144, 1040)[0:64, :]),
          sem=sO, inc=16)
    p.run(c["nc"], serialize=True)
    c["b1_tail"] = dict(o=(sO, sO.n), v=(sV, sV.n), s=(sA, sA.n), t=(sT, sT.n))


def _l1_ffn1(c):
    a16, a32 = c["a16"], c["a32"]
    sX, sT, sV, sA, sO = c["sX"], c["sT"], c["sV"], c["sA"], c["sO"]
    pA, pB = c["pA"], c["pB"]
    H = A16_H
    bt = c["b1_tail"]
    p = Prog()
    p.add("gpsimd", lambda e: e.wait_ge(c["sW"].h, c["tok_conv"][1]),
          waits=[bt["o"], bt["v"], bt["t"], bt["s"]], sem=c["sP"])

    for tt in range(4):
        for fg in range(32):
            slab = H + (fg % 2) * 2048
            p.add("sync", lambda e, fg=fg, slab=slab: e.dma_start(
                a16.view(slab, 1024).rearrange("p (c o) -> p c o", c=8),
                c["w1b"][:, fg * 128:(fg + 1) * 128].rearrange("(c p) o -> p c o", p=128)),
                sem=c["sFL"], inc=16)
            p.add("sync", lambda e, fg=fg, slab=slab: e.dma_start(
                a16.view(slab + 1024, 1024).rearrange("p (c o) -> p c o", c=8),
                c["w2b"][:, fg * 128:(fg + 1) * 128].rearrange("(c p) o -> p c o", p=128)),
                sem=c["sFL"], inc=16)
            p.add("tensor", lambda e, tt=tt, slab=slab: [
                e.matmul(pA[:, 0:512], a16.view(slab + ic * 128, 128),
                         a16.view(A16_C + ic * 2048 + tt * 512, 512),
                         start=(ic == 0), stop=(ic == 7)) for ic in range(8)][-1], sem=sT)
            p.add("tensor", lambda e, tt=tt, slab=slab: [
                e.matmul(pB[:, 0:512], a16.view(slab + 1024 + ic * 128, 128),
                         a16.view(A16_C + ic * 2048 + tt * 512, 512),
                         start=(ic == 0), stop=(ic == 7)) for ic in range(8)][-1], sem=sT)
            p.add("scalar", lambda e, fg=fg: e.activation(
                a32.view(A32["ftmp"], 512), pA[:, 0:512], AF.Silu,
                scale=c["sb_s1c"][:, fg:fg + 1]), sem=sA)
            p.add("vector", lambda e, fg=fg: e.tensor_scalar(
                a32.view(A32["ftmp2"], 512), pB[:, 0:512], c["sb_s2c"][:, fg:fg + 1],
                None, ALU.mult), sem=sV)
            p.add("vector", lambda e, fg=fg: e.tensor_tensor(
                a16.view(A16_A + fg * 512, 512), a32.view(A32["ftmp"], 512),
                a32.view(A32["ftmp2"], 512), ALU.mult), sem=sV)
        for oc in range(32):
            p.add("sync", lambda e, oc=oc: e.dma_start(
                a16.view(H, 4096).rearrange("p (c o) -> p c o", c=32),
                c["h1b"][:, oc * 128:(oc + 1) * 128].rearrange("(c p) o -> p c o", p=128)),
                sem=c["sFL"], inc=16)
            p.add("tensor", lambda e, oc=oc: [
                e.matmul(pA[:, 0:512], a16.view(H + ic * 128, 128),
                         a16.view(A16_A + ic * 512, 512),
                         start=(ic == 0), stop=(ic == 31)) for ic in range(32)][-1], sem=sT)
            p.add("scalar", lambda e, oc=oc: e.activation(
                a16.view(A16_B + oc * 512, 512), pA[:, 0:512], AF.Gelu,
                scale=c["sb_hs1c"][:, oc:oc + 1]), sem=sA)
        for oc in range(32):
            p.add("sync", lambda e, oc=oc: e.dma_start(
                a16.view(H, 4096).rearrange("p (c o) -> p c o", c=32),
                c["h2b"][:, oc * 128:(oc + 1) * 128].rearrange("(c p) o -> p c o", p=128)),
                sem=c["sFL"], inc=16)
            p.add("tensor", lambda e, oc=oc: [
                e.matmul(pA[:, 0:512], a16.view(H + ic * 128, 128),
                         a16.view(A16_B + ic * 512, 512),
                         start=(ic == 0), stop=(ic == 31)) for ic in range(32)][-1], sem=sT)
            p.add("scalar", lambda e, oc=oc: e.activation(
                a16.view(A16_A + oc * 512, 512), pA[:, 0:512], AF.Gelu,
                scale=c["sb_hs2c"][:, oc:oc + 1]), sem=sA)
        for half in range(2):
            p.add("sync", lambda e, half=half: e.dma_start(
                a16.view(A16_B, 16384).rearrange("p (c o) -> p c o", c=16),
                c["w3b"][half * 2048:(half + 1) * 2048, :]
                .rearrange("(c p) o -> p c o", p=128)), sem=c["sFL"], inc=16)
            for oc in range(8):
                p.add("tensor", lambda e, half=half, oc=oc: [
                    e.matmul(pA[:, 0:512], a16.view(A16_B + fc * 1024 + oc * 128, 128),
                             a16.view(A16_A + (half * 16 + fc) * 512, 512),
                             start=(fc == 0), stop=(fc == 15)) for fc in range(16)][-1],
                    sem=sT)
                if half == 0:
                    p.add("vector", lambda e, oc=oc: e.tensor_scalar(
                        a32.view(A32["facc"] + oc * 512, 512), pA[:, 0:512],
                        c["sb_s3c"][:, oc:oc + 1], None, ALU.mult), sem=sV)
                else:
                    p.add("vector", lambda e, oc=oc: e.scalar_tensor_tensor(
                        a32.view(A32["facc"] + oc * 512, 512), pA[:, 0:512],
                        c["sb_s3c"][:, oc:oc + 1], a32.view(A32["facc"] + oc * 512, 512),
                        ALU.mult, ALU.add), sem=sV)
        p.add("sync", lambda e, tt=tt: e.dma_start(
            c["ffn1"][:].rearrange("p (x t) -> p x t", x=8)[:, :, tt * 512:(tt + 1) * 512],
            a32.view(A32["facc"], 4096).rearrange("p (x t) -> p x t", x=8)),
            sem=sO, inc=16)
    p.run(c["nc"], serialize=True)
    c["ffn_tail"] = (sO, sO.n)


def _l1_tail(c):
    p = Prog()
    p.add("gpsimd", lambda e: e.wait_ge(c["sO"].h, c["sO"].n),
          waits=[c["ffn_tail"], c["b1_tail"]["o"]], sem=None)
    p.run(c["nc"])



def build_launch2():
    nc = bass.Bass("TRN2", target_bir_lowering=False)
    yp_i = nc.dram_tensor("yp_i", [128, 16384], F32, kind="ExternalInput")
    ff_i = nc.dram_tensor("ff_i", [128, 16384], F32, kind="ExternalInput")
    qp_i = nc.dram_tensor("qp_i", [128, 16384], BF16, kind="ExternalInput")
    kvall = nc.dram_tensor("kvall", [4, 64, 1040], F32, kind="ExternalInput")
    wout = nc.dram_tensor("wout", [DM, DM], F32, kind="ExternalInput")
    bout_c = nc.dram_tensor("bout_c", [128, 8], F32, kind="ExternalInput")
    idf = nc.dram_tensor("idf", [128, 128], F32, kind="ExternalInput")
    out_tm = nc.dram_tensor("out_tm", [2048, DM], F32, kind="ExternalOutput")

    s_kv4 = nc.alloc_sbuf_tensor("s_kv4", [64, 4 * 1040], F32)
    s_kvb = nc.alloc_sbuf_tensor("s_kvb", [128, 1040], BF16)
    s_wout = nc.alloc_sbuf_tensor("s_wout", [128, 8192], BF16)
    s_qp = nc.alloc_sbuf_tensor("s_qp", [128, 4096], BF16)
    s_att = nc.alloc_sbuf_tensor("s_att", [128, 4096], BF16)
    s_yst = nc.alloc_sbuf_tensor("s_yst", [128, 4096], F32)
    s_y1 = nc.alloc_sbuf_tensor("s_y1", [128, 512], F32)
    s_y2 = nc.alloc_sbuf_tensor("s_y2", [128, 512], F32)
    s_rec = nc.alloc_sbuf_tensor("s_rec", [64, 512], F32)
    s_drow = nc.alloc_sbuf_tensor("s_drow", [1, 512], F32)
    s_ones = nc.alloc_sbuf_tensor("s_ones", [1, 128], F32)
    s_idf = nc.alloc_sbuf_tensor("s_idf", [128, 128], F32)
    s_boc = nc.alloc_sbuf_tensor("s_boc", [128, 8], F32)
    s_out = nc.alloc_sbuf_tensor("s_out", [128, 1024], F32)
    pA = nc.alloc_psum_tensor("pA", [128, 512], F32)
    pB = nc.alloc_psum_tensor("pB", [128, 512], F32)
    pC = nc.alloc_psum_tensor("pC", [128, 1024], F32)
    sX = Sem(nc, "sX"); sG = Sem(nc, "sG"); sT = Sem(nc, "sT")
    sV = Sem(nc, "sV"); sA = Sem(nc, "sA"); sO = Sem(nc, "sO")

    p = Prog()
    p.add("sync", lambda e: e.dma_start(s_idf[:], idf[:]), sem=sX, inc=16)
    p.add("sync", lambda e: e.dma_start(s_boc[:], bout_c[:]), sem=sX, inc=16)
    p.add("sync", lambda e: e.dma_start(
        s_kv4[:].rearrange("p (r n) -> p r n", r=4),
        kvall[:].rearrange("r p n -> p r n")), sem=sX, inc=16)
    p.add("gpsimd", lambda e: e.dma_start(
        s_wout[:].rearrange("p (c o) -> p c o", c=8),
        wout[:].rearrange("(c p) o -> p c o", p=128)), sem=sG, inc=16)
    p.add("vector", lambda e: e.memset(s_ones[:], 1.0), sem=sV)
    for r in range(1, 4):
        p.add("vector", lambda e, r=r: e.tensor_add(
            s_kv4[:, 0:1040], s_kv4[:, 0:1040], s_kv4[:, r * 1040:(r + 1) * 1040]), sem=sV)
    p.add("scalar", lambda e: e.copy(s_kvb[0:64, :], s_kv4[:, 0:1040]), sem=sA)
    p.add("vector", lambda e: e.tensor_copy(s_kvb[64:128, :], s_kvb[0:64, :]), sem=sV)

    for tc in range(4):
        p.add("sync", lambda e, tc=tc: e.dma_start(
            s_qp[:].rearrange("p (x t) -> p x t", x=8),
            qp_i[:].rearrange("p (x t) -> p x t", x=8)[:, :, tc * 512:(tc + 1) * 512]),
            sem=sX, inc=16)
        for h in range(16):
            qrow = (h % 2) * 64
            p.add("tensor", lambda e, h=h, qrow=qrow: e.matmul(
                pA[0:65, 0:512], s_kvb[qrow:qrow + 64, 65 * h:65 * h + 65],
                s_qp[qrow:qrow + 64, (h // 2) * 512:(h // 2) * 512 + 512],
                start=True, stop=True), sem=sT)
            p.add("scalar", lambda e: e.copy(s_drow[:], pA[64:65, 0:512]), sem=sA)
            p.add("vector", lambda e: e.tensor_scalar_add(s_drow[:], s_drow[:], 1e-6), sem=sV)
            p.add("vector", lambda e: e.reciprocal(s_drow[:], s_drow[:]), sem=sV)
            p.add("tensor", lambda e: e.matmul(
                pB[0:64, 0:512], s_ones[0:1, 0:64], s_drow[:], start=True, stop=True),
                sem=sT)
            p.add("scalar", lambda e: e.copy(s_rec[:], pB[0:64, 0:512]), sem=sA)
            p.add("vector", lambda e, h=h, qrow=qrow: e.tensor_tensor(
                s_att[qrow:qrow + 64, (h // 2) * 512:(h // 2) * 512 + 512],
                pA[0:64, 0:512], s_rec[:], ALU.mult), sem=sV)
        for oc in range(8):
            p.add("tensor", lambda e, oc=oc: [
                e.matmul(pA[:, 0:512], s_wout[:, ic * DM + oc * 128:ic * DM + oc * 128 + 128],
                         s_att[:, ic * 512:(ic + 1) * 512],
                         start=(ic == 0), stop=(ic == 7)) for ic in range(8)][-1], sem=sT)
            p.add("sync", lambda e, oc=oc, tc=tc: e.dma_start(
                s_y1[:], yp_i[:, oc * 2048 + tc * 512:oc * 2048 + tc * 512 + 512]),
                sem=sX, inc=16)
            p.add("sync", lambda e, oc=oc, tc=tc: e.dma_start(
                s_y2[:], ff_i[:, oc * 2048 + tc * 512:oc * 2048 + tc * 512 + 512]),
                sem=sX, inc=16)
            p.add("vector", lambda e, oc=oc: e.scalar_tensor_tensor(
                s_y1[:], pA[:, 0:512], s_boc[:, oc:oc + 1], s_y1[:], ALU.add, ALU.add),
                sem=sV)
            p.add("vector", lambda e, oc=oc: e.tensor_add(
                s_yst[:, oc * 512:(oc + 1) * 512], s_y1[:], s_y2[:]), sem=sV)
        for t4 in range(4):
            p.add("tensor", lambda e, t4=t4: [
                e.transpose(pC[:, ic * 128:(ic + 1) * 128],
                            s_yst[:, ic * 512 + t4 * 128:ic * 512 + t4 * 128 + 128],
                            s_idf[:]) for ic in range(8)][-1], sem=sT)
            p.add("vector", lambda e: [
                e.tensor_copy(s_out[:, ic * 128:(ic + 1) * 128], pC[:, ic * 128:(ic + 1) * 128])
                for ic in range(8)][-1], sem=sV)
            p.add("sync", lambda e, tc=tc, t4=t4: e.dma_start(
                out_tm[tc * 512 + t4 * 128:tc * 512 + t4 * 128 + 128, :], s_out[:]),
                sem=sO, inc=16)
    p.add("gpsimd", lambda e: e.wait_ge(sO.h, sO.n), sem=None)
    p.run(nc, serialize=True)
    return nc



def kernel(**inputs):
    sh, per, rows_all = _host_prep(inputs)
    nc1, _ = build_launch1()
    in_maps = [dict(sh, **per[c]) for c in range(NCORE)]
    res1 = run_bass_kernel_spmd(nc1, in_maps, core_ids=list(range(NCORE)))

    nc2 = build_launch2()
    kvs = [np.asarray(res1.results[c]["kvpart"]) for c in range(NCORE)]
    in2 = []
    for c in range(NCORE):
        b = c // 4
        in2.append({
            "yp_i": np.asarray(res1.results[c]["ypart"]),
            "ff_i": np.asarray(res1.results[c]["ffn1"]),
            "qp_i": np.asarray(res1.results[c]["qp_o"]),
            "kvall": np.ascontiguousarray(np.stack([kvs[4 * b + j] for j in range(4)])),
            "wout": sh["wout"], "bout_c": sh["bout_c"], "idf": sh["idf"],
        })
    res2 = run_bass_kernel_spmd(nc2, in2, core_ids=list(range(NCORE)))
    out = np.empty((2, S_GLOB, DM), np.float32)
    for c in range(NCORE):
        b = c // 4
        out[b][rows_all[c]] = np.asarray(res2.results[c]["out_tm"])
    return out


# version 02
# speedup vs baseline: 1.1058x; 1.1058x over previous
import numpy as np
import ml_dtypes
import concourse.bass as bass
import concourse.mybir as mybir
from concourse.bass_utils import run_bass_kernel_spmd

F32 = mybir.dt.float32
BF16 = mybir.dt.bfloat16
AF = mybir.ActivationFunctionType
ALU = mybir.AluOpType
AX = mybir.AxisListType
BF = ml_dtypes.bfloat16

NCORE = 8
S_GLOB = 8192
DM = 1024
C = 8
NT_PRE = 64
NT_OWN = 16
NSLOT = 8
NEG = -1e30
POSINF = 1e30

def _ranges_for(cb):
    g = sorted([cb, 7 - cb, 8 + cb, 15 - cb, 16 + cb, 23 - cb, 24 + cb, 31 - cb],
               reverse=True)
    tr = [2 * (gi + 1) for gi in g]
    return g, tr

def _R(s):
    return 64 - 8 * s

N_MAIN = sum(_R(s) - 2 for s in range(NSLOT))
ORDER = [0, 4, 1, 5, 2, 6, 3, 7, 8, 12, 9, 13, 10, 14, 11, 15]
INV = [ORDER.index(h) for h in range(16)]



def _host_prep(inp):
    f32 = np.float32
    sh = {}
    x0 = np.ascontiguousarray(inp["L0_attn_x"], f32)

    sh["wq"] = np.ascontiguousarray(
        np.asarray(inp["L0_attn_Wq"], f32).reshape(DM, 16, 64)[:, ORDER, :].reshape(DM, DM))
    sh["wkv"] = np.ascontiguousarray(
        np.concatenate([inp["L0_attn_Wk"], inp["L0_attn_Wv"]], axis=1), f32)
    sh["wo"] = np.ascontiguousarray(
        np.asarray(inp["L0_attn_Wo"], f32).reshape(16, 64, DM)[ORDER].reshape(DM, DM))
    sh["bq_r"] = np.ascontiguousarray(
        np.asarray(inp["L0_attn_bq"], f32).reshape(16, 64)[ORDER].reshape(1, DM))
    sh["bkv_r"] = np.ascontiguousarray(
        np.concatenate([inp["L0_attn_bk"], inp["L0_attn_bv"]])[None, :].astype(f32))
    sh["bo_c"] = np.ascontiguousarray(np.asarray(inp["L0_attn_bo"], f32).reshape(8, 128).T)

    sh["centT"] = np.ascontiguousarray(np.asarray(inp["L0_ffn_centroids"], f32).T)
    sh["gm_c"] = np.ascontiguousarray(
        np.asarray(inp["L0_ffn_gamma"], f32).reshape(8, 8, 128).transpose(2, 0, 1).reshape(128, 64))
    sh["bt_c"] = np.ascontiguousarray(
        np.asarray(inp["L0_ffn_beta"], f32).reshape(8, 8, 128).transpose(2, 0, 1).reshape(128, 64))
    sh["w1t"] = np.ascontiguousarray(np.asarray(inp["L0_ffn_W1"], f32).transpose(0, 2, 1))
    sh["b1_c"] = np.ascontiguousarray(
        np.asarray(inp["L0_ffn_b1"], f32).reshape(8, 32, 128).transpose(2, 0, 1).reshape(128, 256))
    sh["w2t"] = np.ascontiguousarray(np.asarray(inp["L0_ffn_W2"], f32).transpose(0, 2, 1))
    sh["b2_c"] = np.ascontiguousarray(
        np.asarray(inp["L0_ffn_b2"], f32).reshape(8, 8, 128).transpose(2, 0, 1).reshape(128, 64))

    sh["wqkv"] = np.ascontiguousarray(inp["L1_attn_W_qkv"], f32)
    sh["qn_r"] = np.ascontiguousarray(np.asarray(inp["L1_attn_qn_w"], f32)[None, :])
    sh["kn_r"] = np.ascontiguousarray(np.asarray(inp["L1_attn_kn_w"], f32)[None, :])
    sh["wout"] = np.ascontiguousarray(inp["L1_attn_W_out"], f32)
    sh["bout_c"] = np.ascontiguousarray(np.asarray(inp["L1_attn_b_out"], f32).reshape(8, 128).T)

    sh["l1w1"] = np.ascontiguousarray(np.asarray(inp["L1_ffn_w1_w"], f32).T)
    sh["l1w2"] = np.ascontiguousarray(np.asarray(inp["L1_ffn_w2_w"], f32).T)
    sh["l1h1"] = np.ascontiguousarray(np.asarray(inp["L1_ffn_h1_w"], f32).T)
    sh["l1h2"] = np.ascontiguousarray(np.asarray(inp["L1_ffn_h2_w"], f32).T)
    sh["l1w3"] = np.ascontiguousarray(np.asarray(inp["L1_ffn_w3_w"], f32).T)
    sh["s1_c"] = np.ascontiguousarray(np.asarray(inp["L1_ffn_w1_s"], f32).reshape(32, 128).T)
    sh["s2_c"] = np.ascontiguousarray(np.asarray(inp["L1_ffn_w2_s"], f32).reshape(32, 128).T)
    sh["hs1_c"] = np.ascontiguousarray(np.asarray(inp["L1_ffn_h1_s"], f32).reshape(32, 128).T)
    sh["hs2_c"] = np.ascontiguousarray(np.asarray(inp["L1_ffn_h2_s"], f32).reshape(32, 128).T)
    sh["s3_c"] = np.ascontiguousarray(np.asarray(inp["L1_ffn_w3_s"], f32).reshape(8, 128).T)

    inv = 1.0 / 10000.0 ** (np.arange(0, 64, 2, dtype=f32) / 64.0)
    ang = np.arange(S_GLOB, dtype=f32)[:, None] * inv[None, :]
    cos_full = np.cos(ang).astype(f32)
    sin_full = np.sin(ang).astype(f32)
    sh["cos_tm"] = np.ascontiguousarray(cos_full.reshape(64, 128, 32).transpose(1, 0, 2).reshape(128, 64 * 32))
    sh["sin_tm"] = np.ascontiguousarray(sin_full.reshape(64, 128, 32).transpose(1, 0, 2).reshape(128, 64 * 32))
    kk = np.arange(128)[:, None]
    qq = np.arange(256)[None, :]
    sh["maskA"] = np.where(kk <= qq, 0.0, POSINF).astype(f32)
    sh["maskB"] = np.where(kk <= qq - 128, 0.0, POSINF).astype(f32)
    sh["id16"] = np.eye(128, dtype=BF)
    sh["idf"] = np.eye(128, dtype=f32)

    per = []
    rows_all = []
    for c in range(NCORE):
        b, cb = divmod(c, 4)
        g, tr = _ranges_for(cb)
        rows = np.concatenate([np.arange(gi * 256, (gi + 1) * 256) for gi in g])
        rows_all.append(rows)
        d = {}
        d["x_own"] = np.ascontiguousarray(x0[b][rows])
        d["x_batch"] = x0[b]
        d["cos_own"] = np.ascontiguousarray(
            cos_full[rows].reshape(16, 128, 32).transpose(1, 0, 2).reshape(128, 16 * 32))
        d["sin_own"] = np.ascontiguousarray(
            sin_full[rows].reshape(16, 128, 32).transpose(1, 0, 2).reshape(128, 16 * 32))
        fb = np.zeros((128, N_MAIN), f32)
        idx = 0
        for s in range(NSLOT):
            for j in range(_R(s) - 2):
                if j >= tr[s] - 2:
                    fb[:, idx] = NEG
                idx += 1
        d["flagbias"] = fb
        dsrc = [tr[s] - 2 + dd for s in range(NSLOT) for dd in (0, 1)]
        d["diag_src"] = np.asarray(dsrc, np.int32)[None, :]
        drows = np.concatenate([np.arange(bk * 128, (bk + 1) * 128) for bk in dsrc])
        d["x_diag"] = np.ascontiguousarray(x0[b][drows])
        d["cos_diag"] = np.ascontiguousarray(
            cos_full[drows].reshape(16, 128, 32).transpose(1, 0, 2).reshape(128, 512))
        d["sin_diag"] = np.ascontiguousarray(
            sin_full[drows].reshape(16, 128, 32).transpose(1, 0, 2).reshape(128, 512))
        per.append(d)
    return sh, per, rows_all



class Sem:
    def __init__(self, nc, name):
        self.h = nc.alloc_semaphore(name)
        self.n = 0

    def inc(self, instr, k=1):
        instr.then_inc(self.h, k)
        self.n += k
        return self.n


class Arena:

    def __init__(self, nc, name, cols, dtype):
        self.t = nc.alloc_sbuf_tensor(name, [128, cols], dtype)
        self.cols = cols
        self.off = 0

    def take(self, n):
        assert self.off + n <= self.cols, (self.off, n, self.cols)
        o = self.off
        self.off += n
        return o

    def view(self, off, n):
        return self.t[:, off:off + n]



A16_A = 0
A16_B = 16640
A16_C = 33280
A16_H = 49664
A16_TOT = 53760

A32_M = 0
A32_T = 16384
A32_TOT = 17408


def build_launch1(dbg=""):
    nc = bass.Bass("TRN2", target_bir_lowering=False)

    x_own = nc.dram_tensor("x_own", [2048, DM], F32, kind="ExternalInput")
    x_batch = nc.dram_tensor("x_batch", [S_GLOB, DM], F32, kind="ExternalInput")
    cos_own = nc.dram_tensor("cos_own", [128, 512], F32, kind="ExternalInput")
    x_diag = nc.dram_tensor("x_diag", [2048, DM], F32, kind="ExternalInput")
    cos_diag = nc.dram_tensor("cos_diag", [128, 512], F32, kind="ExternalInput")
    sin_diag = nc.dram_tensor("sin_diag", [128, 512], F32, kind="ExternalInput")
    sin_own = nc.dram_tensor("sin_own", [128, 512], F32, kind="ExternalInput")
    flagbias = nc.dram_tensor("flagbias", [128, N_MAIN], F32, kind="ExternalInput")
    diag_src = nc.dram_tensor("diag_src", [1, 16], mybir.dt.int32, kind="ExternalInput")
    wq = nc.dram_tensor("wq", [DM, DM], F32, kind="ExternalInput")
    wkv = nc.dram_tensor("wkv", [DM, 512], F32, kind="ExternalInput")
    wo = nc.dram_tensor("wo", [DM, DM], F32, kind="ExternalInput")
    bq_r = nc.dram_tensor("bq_r", [1, DM], F32, kind="ExternalInput")
    bkv_r = nc.dram_tensor("bkv_r", [1, 512], F32, kind="ExternalInput")
    bo_c = nc.dram_tensor("bo_c", [128, 8], F32, kind="ExternalInput")
    centT = nc.dram_tensor("centT", [DM, 8], F32, kind="ExternalInput")
    gm_c = nc.dram_tensor("gm_c", [128, 64], F32, kind="ExternalInput")
    bt_c = nc.dram_tensor("bt_c", [128, 64], F32, kind="ExternalInput")
    w1t = nc.dram_tensor("w1t", [8, DM, 4096], F32, kind="ExternalInput")
    b1_c = nc.dram_tensor("b1_c", [128, 256], F32, kind="ExternalInput")
    w2t = nc.dram_tensor("w2t", [8, 4096, DM], F32, kind="ExternalInput")
    b2_c = nc.dram_tensor("b2_c", [128, 64], F32, kind="ExternalInput")
    wqkv = nc.dram_tensor("wqkv", [DM, 3072], F32, kind="ExternalInput")
    qn_r = nc.dram_tensor("qn_r", [1, DM], F32, kind="ExternalInput")
    kn_r = nc.dram_tensor("kn_r", [1, DM], F32, kind="ExternalInput")
    l1w1 = nc.dram_tensor("l1w1", [DM, 4096], F32, kind="ExternalInput")
    l1w2 = nc.dram_tensor("l1w2", [DM, 4096], F32, kind="ExternalInput")
    l1h1 = nc.dram_tensor("l1h1", [4096, 4096], F32, kind="ExternalInput")
    l1h2 = nc.dram_tensor("l1h2", [4096, 4096], F32, kind="ExternalInput")
    l1w3 = nc.dram_tensor("l1w3", [4096, DM], F32, kind="ExternalInput")
    s1_c = nc.dram_tensor("s1_c", [128, 32], F32, kind="ExternalInput")
    s2_c = nc.dram_tensor("s2_c", [128, 32], F32, kind="ExternalInput")
    hs1_c = nc.dram_tensor("hs1_c", [128, 32], F32, kind="ExternalInput")
    hs2_c = nc.dram_tensor("hs2_c", [128, 32], F32, kind="ExternalInput")
    s3_c = nc.dram_tensor("s3_c", [128, 8], F32, kind="ExternalInput")
    cos_tm = nc.dram_tensor("cos_tm", [128, 2048], F32, kind="ExternalInput")
    sin_tm = nc.dram_tensor("sin_tm", [128, 2048], F32, kind="ExternalInput")
    maskA = nc.dram_tensor("maskA", [128, 256], F32, kind="ExternalInput")
    maskB = nc.dram_tensor("maskB", [128, 256], F32, kind="ExternalInput")
    id16 = nc.dram_tensor("id16", [128, 128], BF16, kind="ExternalInput")
    idf = nc.dram_tensor("idf", [128, 128], F32, kind="ExternalInput")

    xn_d = nc.dram_tensor("xn_d", [128, 16384], BF16, kind="Internal")
    z_d = nc.dram_tensor("z_d", [128, 16384], BF16, kind="Internal")
    q_d = nc.dram_tensor("q_d", [128, 16384], BF16, kind="Internal")
    attn_d = nc.dram_tensor("attn_d", [128, 16384], BF16, kind="Internal")
    y_d = nc.dram_tensor("y_d", [128, 16384], F32, kind="Internal")
    xn1_d = nc.dram_tensor("xn1_d", [128, 16384], BF16, kind="Internal")
    h1b = nc.dram_tensor("h1b", [4096, 4096], BF16, kind="Internal")
    h2b = nc.dram_tensor("h2b", [4096, 4096], BF16, kind="Internal")
    w1b = nc.dram_tensor("w1b", [DM, 4096], BF16, kind="Internal")
    w2b = nc.dram_tensor("w2b", [DM, 4096], BF16, kind="Internal")
    w3b = nc.dram_tensor("w3b", [4096, DM], BF16, kind="Internal")

    ypart = nc.dram_tensor("ypart", [128, 16384], F32, kind="ExternalOutput")
    ffn1 = nc.dram_tensor("ffn1", [128, 16384], F32, kind="ExternalOutput")
    qp_o = nc.dram_tensor("qp_o", [128, 16384], BF16, kind="ExternalOutput")
    kvpart = nc.dram_tensor("kvpart", [64, 1040], F32, kind="ExternalOutput")
    dbg_outs = {}
    if dbg:
        for nm, shp, dt in [
            ("d_xn", [128, 16384], BF16), ("d_z", [128, 16384], BF16),
            ("d_q", [128, 16384], BF16), ("d_amask", [128, 128], F32),
            ("d_kfm", [128, 2 * S_GLOB], BF16), ("d_vst", [128, NT_PRE * 260], BF16),
            ("d_sc", [128, 128], F32), ("d_chi", [128, 128], BF16),
            ("d_attn", [128, 16384], BF16), ("d_y", [128, 16384], F32),
            ("d_acc", [128, 16384], F32),
        ]:
            dbg_outs[nm] = nc.dram_tensor(nm, shp, dt, kind="ExternalOutput")

    a16 = Arena(nc, "a16", A16_TOT, BF16)
    a32 = Arena(nc, "a32", A32_TOT, F32)
    sb_id16 = nc.alloc_sbuf_tensor("sb_id16", [128, 128], BF16)
    sb_idf = nc.alloc_sbuf_tensor("sb_idf", [128, 128], F32)
    sb_maskA = nc.alloc_sbuf_tensor("sb_maskA", [128, 256], F32)
    sb_maskB = nc.alloc_sbuf_tensor("sb_maskB", [128, 256], F32)
    sb_flag = nc.alloc_sbuf_tensor("sb_flag", [128, N_MAIN], F32)
    sb_b1c = nc.alloc_sbuf_tensor("sb_b1c", [128, 256], F32)
    sb_b2c = nc.alloc_sbuf_tensor("sb_b2c", [128, 64], F32)
    sb_gmc = nc.alloc_sbuf_tensor("sb_gmc", [128, 64], F32)
    sb_btc = nc.alloc_sbuf_tensor("sb_btc", [128, 64], F32)
    sb_boc = nc.alloc_sbuf_tensor("sb_boc", [128, 8], F32)
    sb_s1c = nc.alloc_sbuf_tensor("sb_s1c", [128, 32], F32)
    sb_s2c = nc.alloc_sbuf_tensor("sb_s2c", [128, 32], F32)
    sb_hs1c = nc.alloc_sbuf_tensor("sb_hs1c", [128, 32], F32)
    sb_hs2c = nc.alloc_sbuf_tensor("sb_hs2c", [128, 32], F32)
    sb_s3c = nc.alloc_sbuf_tensor("sb_s3c", [128, 8], F32)
    sb_amall = nc.alloc_sbuf_tensor("sb_amall", [128, 128], F32)
    sb_scdbg = nc.alloc_sbuf_tensor("sb_scdbg", [128, 128], F32)
    sb_row = nc.alloc_sbuf_tensor("sb_row", [1, DM], F32)
    sb_row2 = nc.alloc_sbuf_tensor("sb_row2", [1, 512], F32)
    sb_mrow = nc.alloc_sbuf_tensor("sb_mrow", [1, 256], F32)
    sb_arow = nc.alloc_sbuf_tensor("sb_arow", [1, 256], F32)
    sb_drow = nc.alloc_sbuf_tensor("sb_drow", [1, 256], F32)
    sb_ones1 = nc.alloc_sbuf_tensor("sb_ones1", [1, 128], F32)
    sb_dsrc = nc.alloc_sbuf_tensor("sb_dsrc", [1, 16], mybir.dt.int32)
    sb_stat = nc.alloc_sbuf_tensor("sb_stat", [128, 24], F32)

    pA = nc.alloc_psum_tensor("pA", [128, 512], F32)
    pB = nc.alloc_psum_tensor("pB", [128, 512], F32)
    pD = nc.alloc_psum_tensor("pD", [128, 2048], F32)
    pT = nc.alloc_psum_tensor("pT", [128, 1024], BF16)
    pE = nc.alloc_psum_tensor("pE", [128, 512], F32)

    sX = Sem(nc, "sX")
    sG = Sem(nc, "sG")
    sT = Sem(nc, "sT")
    sV = Sem(nc, "sV")
    sA = Sem(nc, "sA")
    sO = Sem(nc, "sO")
    sP = Sem(nc, "sP")
    sW = Sem(nc, "sW")
    sXT = Sem(nc, "sXT")
    sQS = Sem(nc, "sQS")
    sAB = Sem(nc, "sAB")
    sZL = Sem(nc, "sZL")
    sFL = Sem(nc, "sFL")
    sGA = Sem(nc, "sGA")
    sGB = Sem(nc, "sGB")
    sAC = Sem(nc, "sAC")
    sAB2 = Sem(nc, "sAB2")

    ctx = dict(locals())
    _l1_p0(ctx)
    if dbg == "p0":
        ctx["own_tail"] = dict(v=(sV, sV.n), s=(sA, sA.n), t=(sT, sT.n), o=(sO, sO.n))
        _l1_dbg_tail(ctx, ["d_chi"])
        return nc, ctx
    _l1_p2a(ctx)
    if dbg == "p2a":
        ctx["own_tail"] = dict(v=(sV, sV.n), s=(sA, sA.n), t=(sT, sT.n), o=(sO, sO.n))
        _l1_dbg_tail(ctx, ["d_chi", "d_kfm", "d_vst"])
        return nc, ctx
    _l1_own(ctx)
    _l1_p2a_diag(ctx)
    if dbg == "own":
        _l1_dbg_tail(ctx, ["d_xn", "d_z", "d_q", "d_amask", "d_kfm", "d_vst", "d_sc", "d_chi"])
        return nc, ctx
    _l1_attn(ctx)
    if dbg == "attn":
        _l1_dbg_tail(ctx, ["d_attn"])
        return nc, ctx
    _l1_wo(ctx)
    _l1_moe(ctx)
    if dbg == "moe":
        _l1_dbg_tail(ctx, ["d_y", "d_acc", "d_sc", "d_attn"])
        return nc, ctx
    _l1_block1(ctx)
    _l1_ffn1(ctx)
    _l1_tail(ctx)
    return nc, ctx



class Prog:

    def __init__(self):
        self.ops = []

    def add(self, eng_name, emit, waits=(), sem=None, inc=1):
        if sem is not None:
            sem.n += inc
            tok = (sem, sem.n)
        else:
            tok = None
        self.ops.append((eng_name, sem, inc, emit, [w for w in waits if w], tok))
        return tok

    def run(self, nc, engines=("sync", "gpsimd", "tensor", "vector", "scalar"),
            serialize=False):
        if serialize:
            prev = None
            for (en, sem, inc, emit, waits, tok) in self.ops:
                if prev is not None:
                    waits.append(prev)
                if tok is not None:
                    prev = tok
        with nc.Block() as blk:
            for name in engines:
                def body(eng, name=name):
                    for (en, sem, inc, emit, waits, tok) in self.ops:
                        if en != name:
                            continue
                        for (ws, wv) in waits:
                            eng.wait_ge(ws.h, wv)
                        instr = emit(eng)
                        if sem is not None:
                            instr.then_inc(sem.h, inc)
                getattr(blk, name)(body)


A32 = dict(
    cos=0, sin=2048, x3=4096, xnf=7168, bq=8192, bkv=9216, kvb=9728,
    cown=10240, sown=10752, tmpa=11264, tmpb=11296,
    m_pl=0, m_dA=256, m_dB=512, tmpS0=768, tmpS1=1024, recb=1312,
    yst=0, x3e=4096, wotmp=7168,
    acc=0, abc=16384, abc2=16640,
    ysl=0, ysl2=2048, qnb=4096, knb=5120, bc1=6144, bc2=6656, sttmp=16896,
    facc=0, ftmp=4096, ftmp2=4608, ftmp3=5120,
)
H16 = dict(stage0=0, stage1=1024, qro=2048, kro=3072, chi=3328, clo=3392,
           qslot=0, kscr=0, vaug=1024, qscr=2176,
           hbuf=0, rt=2048, rm=2304,
           w3s=0, w3s2=1024)


def _l1_p0(c):
    a16, a32 = c["a16"], c["a32"]
    sX, sG, sT, sV, sA = c["sX"], c["sG"], c["sT"], c["sV"], c["sA"]
    H = A16_H
    p = Prog()

    loads = [
        (c["sb_id16"][:], "id16"), (c["sb_idf"][:], "idf"),
        (c["sb_maskA"][:], "maskA"), (c["sb_maskB"][:], "maskB"),
        (c["sb_flag"][:], "flagbias"), (c["sb_b1c"][:], "b1_c"),
        (c["sb_b2c"][:], "b2_c"), (c["sb_gmc"][:], "gm_c"),
        (c["sb_btc"][:], "bt_c"), (c["sb_boc"][:], "bo_c"),
        (c["sb_s1c"][:], "s1_c"), (c["sb_s2c"][:], "s2_c"),
        (c["sb_hs1c"][:], "hs1_c"), (c["sb_hs2c"][:], "hs2_c"),
        (c["sb_s3c"][:], "s3_c"), (c["sb_dsrc"][:], "diag_src"),
        (a32.view(A32["cos"], 2048), "cos_tm"),
        (a32.view(A32["sin"], 2048), "sin_tm"),
        (a32.view(A32["cown"], 512), "cos_own"),
        (a32.view(A32["sown"], 512), "sin_own"),
        (c["sb_row"][:], "bq_r"), (c["sb_row2"][:], "bkv_r"),
        (a32.view(11392, 512), "cos_diag"),
        (a32.view(11904, 512), "sin_diag"),
    ]
    for dst, nm in loads:
        p.add("sync", lambda e, d=dst, n=nm: e.dma_start(d, c[n][:]), sem=sX, inc=16)
    p.add("sync", lambda e: e.dma_start(
        a32.view(A32["x3"], 64).rearrange("p (c e) -> p c e", c=8),
        c["centT"][:].rearrange("(c p) e -> p c e", p=128)), sem=sX, inc=16)
    ldbar = (sX, sX.n)

    p.add("gpsimd", lambda e: e.dma_start(
        a16.view(A16_C, 8 * DM).rearrange("p (c o) -> p c o", c=8),
        c["wq"][:].rearrange("(c p) o -> p c o", p=128)), sem=sG, inc=16)
    p.add("gpsimd", lambda e: e.dma_start(
        a16.view(A16_C + 12288, 8 * 512).rearrange("p (c o) -> p c o", c=8),
        c["wkv"][:].rearrange("(c p) o -> p c o", p=128)), sem=sG, inc=16)
    p.add("gpsimd", lambda e: e.dma_start(
        a16.view(H + H16["chi"], 64).rearrange("p (c e) -> p c e", c=8),
        c["centT"][:].rearrange("(c p) e -> p c e", p=128)), sem=sG, inc=16)
    gbar = (sG, sG.n)
    tvone = p.add("gpsimd", lambda e: e.memset(
        a16.view(A16_A, 64 * 260).rearrange("p (b e) -> p b e", e=65)[:, :, 64:65], 1.0),
        sem=c["sP"])
    tones = p.add("gpsimd", lambda e: e.memset(c["sb_ones1"][:], 1.0), sem=c["sP"])
    p.add("vector", lambda e: (
        e.memset(c["sb_stat"][:, 18:19], 1e-5),
        e.memset(c["sb_stat"][:, 19:20], 1e-6))[-1], sem=c["sV"])

    tclo = p.add("vector", lambda e: e.tensor_sub(
        a16.view(H + H16["clo"], 64), a32.view(A32["x3"], 64), a16.view(H + H16["chi"], 64)),
        waits=[ldbar, gbar], sem=sV)
    tbq = p.add("scalar", lambda e: e.mul(c["sb_row"][:], c["sb_row"][:], 0.125),
                waits=[ldbar], sem=sA)
    tmm = p.add("tensor", lambda e: (
        e.matmul(c["pA"][:], c["sb_ones1"][:], c["sb_row"][0:1, 0:512], start=True, stop=True),
        e.matmul(c["pB"][:], c["sb_ones1"][:], c["sb_row"][0:1, 512:1024], start=True, stop=True))[-1],
        waits=[tbq, tones], sem=sT)
    tcp = p.add("scalar", lambda e: (
        e.copy(a32.view(A32["bq"], 512), c["pA"][:]),
        e.copy(a32.view(A32["bq"] + 512, 512), c["pB"][:]))[-1],
        waits=[tmm], sem=sA)
    tmm2 = p.add("tensor", lambda e: e.matmul(
        c["pA"][:], c["sb_ones1"][:], c["sb_row2"][:], start=True, stop=True),
        waits=[ldbar, tones, tcp], sem=sT)
    tbkv = p.add("scalar", lambda e: e.copy(a32.view(A32["bkv"], 512), c["pA"][:]),
                 waits=[tmm2], sem=sA)
    sW = c["sW"]
    for i in range(16):
        p.add("gpsimd", lambda e, i=i: e.dma_start(
            c["h1b"][i * 256:(i + 1) * 256, :], c["l1h1"][i * 256:(i + 1) * 256, :]),
            sem=sW, inc=16)
    for i in range(16):
        p.add("gpsimd", lambda e, i=i: e.dma_start(
            c["h2b"][i * 256:(i + 1) * 256, :], c["l1h2"][i * 256:(i + 1) * 256, :]),
            sem=sW, inc=16)
    for i in range(4):
        p.add("gpsimd", lambda e, i=i: e.dma_start(
            c["w1b"][i * 256:(i + 1) * 256, :], c["l1w1"][i * 256:(i + 1) * 256, :]),
            sem=sW, inc=16)
        p.add("gpsimd", lambda e, i=i: e.dma_start(
            c["w2b"][i * 256:(i + 1) * 256, :], c["l1w2"][i * 256:(i + 1) * 256, :]),
            sem=sW, inc=16)
    for i in range(4):
        p.add("gpsimd", lambda e, i=i: e.dma_start(
            c["w3b"][i * 1024:(i + 1) * 1024, :], c["l1w3"][i * 1024:(i + 1) * 1024, :]),
            sem=sW, inc=16)
    c["tok_conv"] = (sW, sW.n)
    p.run(c["nc"])
    c["tok_wq"] = gbar
    c["tok_wkv"] = gbar
    c["tok_vones"] = tvone
    c["tok_bkv"] = tbkv
    c["tok_bq"] = tcp
    c["tok_clo"] = tclo
    c["tok_ld"] = ldbar


def _stats_chain(p, c, x_ap, sq_ap, cb, wx, prev_read_tok, eps_col=18):
    st = c["sb_stat"]
    sV, sA = c["sV"], c["sA"]
    t_red = p.add("vector", lambda e: e.reduce_sum(st[:, cb:cb + 1], x_ap, axis=AX.X),
                  waits=[wx], sem=sV)
    t_mu = p.add("vector", lambda e: e.tensor_scalar_mul(
        st[:, cb + 1:cb + 2], st[:, cb:cb + 1], 1.0 / 1024.0), waits=[t_red], sem=sV)
    t_sq = p.add("scalar", lambda e: e.activation(
        sq_ap, x_ap, AF.Square, accum_out=st[:, cb + 2:cb + 3]),
        waits=[wx, prev_read_tok], sem=sA)
    t_va = p.add("vector", lambda e: (
        e.tensor_scalar_mul(st[:, cb + 3:cb + 4], st[:, cb + 2:cb + 3], 1.0 / 1024.0),
        e.tensor_tensor(st[:, 20:21], st[:, cb + 1:cb + 2], st[:, cb + 1:cb + 2],
                        ALU.mult))[-1],
        waits=[t_sq, t_mu], sem=sV)
    t_var = p.add("vector", lambda e: e.tensor_sub(
        st[:, cb + 4:cb + 5], st[:, cb + 3:cb + 4], st[:, 20:21]),
        waits=[t_va], sem=sV)
    t_sd = p.add("scalar", lambda e: e.activation(
        st[:, cb + 5:cb + 6], st[:, cb + 4:cb + 5], AF.Sqrt,
        bias=st[:, eps_col:eps_col + 1]), waits=[t_var], sem=sA)
    t_rc = p.add("vector", lambda e: e.reciprocal(st[:, cb + 6:cb + 7], st[:, cb + 5:cb + 6]),
                 waits=[t_sd], sem=sV)
    t_rstd = p.add("vector", lambda e: e.tensor_scalar(
        st[:, cb + 7:cb + 8], st[:, cb + 1:cb + 2], st[:, cb + 6:cb + 7],
        -1.0, ALU.mult, ALU.mult), waits=[t_rc], sem=sV)
    return t_sq, t_var, t_rstd


def _rope_emit(p, c, src_view, dst_view, nheads, cos_ap, sin_ap, wfirst, extra_waits=()):
    a32 = c["a32"]
    sV = c["sV"]
    last = wfirst
    toks = []
    for h in range(nheads):
        ta1 = a32.view(A32["tmpa"], 32)
        tb1 = a32.view(A32["tmpb"], 32)
        ta2 = a32.view(A32["tmpa"] + 64, 32)
        tb2 = a32.view(A32["tmpb"] + 64, 32)
        av = src_view[:, h * 64:h * 64 + 32]
        bv = src_view[:, h * 64 + 32:h * 64 + 64]
        g1 = p.add("vector", lambda e, av=av, bv=bv, ta1=ta1, tb1=tb1, ta2=ta2, tb2=tb2: (
            e.tensor_tensor(ta1, av, cos_ap, ALU.mult),
            e.tensor_tensor(tb1, bv, sin_ap, ALU.mult),
            e.tensor_tensor(ta2, bv, cos_ap, ALU.mult),
            e.tensor_tensor(tb2, av, sin_ap, ALU.mult))[-1],
            waits=[last] + list(extra_waits if h == 0 else []), sem=sV)
        g2 = p.add("vector", lambda e, h=h, ta1=ta1, tb1=tb1, ta2=ta2, tb2=tb2: (
            e.tensor_sub(dst_view[:, h * 64:h * 64 + 32], ta1, tb1),
            e.tensor_add(dst_view[:, h * 64 + 32:h * 64 + 64], ta2, tb2))[-1],
            waits=[g1], sem=sV)
        last = g2
        toks.append(g2)
    return toks[-1]


def _l1_p2a(c):
    a16, a32 = c["a16"], c["a32"]
    sX, sT, sV, sA = c["sX"], c["sT"], c["sV"], c["sA"]
    pA, pT = c["pA"], c["pT"]
    H = A16_H
    st = c["sb_stat"]
    x3o = [A32["x3"] + j * 1024 for j in range(3)]
    stage0 = a16.view(H + H16["stage0"], 1024)
    stage1 = a16.view(H + H16["stage1"], 1024)
    kro = a16.view(H + H16["kro"], 256)
    wkv_sb = lambda ic: a16.view(A16_C + 12288 + ic * 512, 512)

    p = Prog()
    xn_tok = [None] * NT_PRE
    tT_tok = [None] * NT_PRE
    ev_tok = [None] * NT_PRE
    kv_tok = [None] * NT_PRE
    bias_tok = [None] * NT_PRE
    vcp_tok = [None] * NT_PRE
    kT_tok = [None] * NT_PRE
    kev_tok = [None] * NT_PRE
    var_tok = [None] * NT_PRE
    prev = lambda lst, i: lst[i - 1] if i >= 1 else None

    for i in range(NT_PRE):
        xv = a32.view(x3o[i % 3], 1024)
        t_x = p.add("sync", lambda e, i=i, xv=xv: e.dma_start(
            xv, c["x_batch"][i * 128:(i + 1) * 128, :]),
            waits=[xn_tok[i - 3] if i >= 3 else c["tok_clo"]], sem=c["sXT"], inc=16)
        sq_tok, var_tok[i], t_rstd = _stats_chain(
            p, c, xv, a32.view(A32["xnf"], 1024), 0, t_x, prev(var_tok, i))
        xn_tok[i] = p.add("vector", lambda e, xv=xv: e.tensor_scalar(
            stage0, xv, st[:, 6:7], st[:, 7:8], ALU.mult, ALU.add),
            waits=[t_rstd, prev(tT_tok, i)], sem=sV)
        tT_tok[i] = p.add("tensor", lambda e: [
            e.transpose(pT[:, ic * 128:(ic + 1) * 128],
                        stage0[:, ic * 128:(ic + 1) * 128], c["sb_id16"][:])
            for ic in range(8)][-1],
            waits=[xn_tok[i], prev(ev_tok, i), prev(kev_tok, i)], sem=sT)
        ev_tok[i] = p.add("vector", lambda e: [
            e.tensor_copy(stage1[:, ic * 128:(ic + 1) * 128], pT[:, ic * 128:(ic + 1) * 128])
            for ic in range(8)][-1],
            waits=[tT_tok[i], prev(kv_tok, i)], sem=sV)
        kv_tok[i] = p.add("tensor", lambda e: [
            e.matmul(pA[:], stage1[:, ic * 128:(ic + 1) * 128], wkv_sb(ic),
                     start=(ic == 0), stop=(ic == 7)) for ic in range(8)][-1],
            waits=[ev_tok[i], prev(bias_tok, i), c["tok_wkv"] if i == 0 else None],
            sem=sT)
        bias_tok[i] = p.add("vector", lambda e: e.tensor_tensor(
            a32.view(A32["kvb"], 512), pA[:], a32.view(A32["bkv"], 512), ALU.add),
            waits=[kv_tok[i], prev(vcp_tok, i),
                   c["tok_bkv"] if i == 0 else None], sem=sV)
        rope_tok = _rope_emit(
            p, c, a32.view(A32["kvb"], 512), kro, 4,
            a32.view(A32["cos"] + i * 32, 32), a32.view(A32["sin"] + i * 32, 32),
            bias_tok[i], extra_waits=[prev(kT_tok, i)])
        vcp_tok[i] = p.add("scalar", lambda e, i=i: e.copy(
            a16.view(A16_A + i * 260, 260).rearrange("p (h x) -> p h x", h=4)[:, :, 0:64],
            a32.view(A32["kvb"] + 256, 256).rearrange("p (h x) -> p h x", h=4)),
            waits=[bias_tok[i], c["tok_vones"] if i == 0 else None], sem=sA)
        kT_tok[i] = p.add("tensor", lambda e: (
            e.transpose(pT[:, 0:128], kro[:, 0:128], c["sb_id16"][:]),
            e.transpose(pT[:, 128:256], kro[:, 128:256], c["sb_id16"][:]))[-1],
            waits=[rope_tok], sem=sT)
        kev_tok[i] = p.add("vector", lambda e, i=i: (
            e.tensor_copy(a16.view(A16_B + i * 128, 128), pT[:, 0:128]),
            e.tensor_copy(a16.view(A16_B + S_GLOB + i * 128, 128), pT[:, 128:256]))[-1],
            waits=[kT_tok[i]], sem=sV)
    p.run(c["nc"])
    c["p2a_last_v"] = (sV, sV.n)
    c["p2a_last_s"] = (sA, sA.n)


def _l1_own(c):
    a16, a32 = c["a16"], c["a32"]
    sX, sT, sV, sA, sO = c["sX"], c["sT"], c["sV"], c["sA"], c["sO"]
    pA, pT, pD, pE = c["pA"], c["pT"], c["pD"], c["pE"]
    H = A16_H
    st = c["sb_stat"]
    x3o = [A32["x3"] + j * 1024 for j in range(3)]
    stage0 = a16.view(H + H16["stage0"], 1024)
    stage1 = a16.view(H + H16["stage1"], 1024)
    qro = a16.view(H + H16["qro"], 1024)
    chi = a16.view(H + H16["chi"], 64)
    clo = a16.view(H + H16["clo"], 64)
    wq_sb = lambda ic: a16.view(A16_C + ic * DM, DM)
    xnf = a32.view(A32["xnf"], 1024)
    scp = pE[:, 256:264]

    p = Prog()
    T = NT_OWN
    xn_tok = [None] * T; tT_tok = [None] * T; ev_tok = [None] * T
    sc_tok = [None] * T; am_tok = [None] * T; zT_tok = [None] * T; scp_tok = [None] * T
    zev_tok = [None] * T; zdma_tok = [None] * T; q_tok = [None] * T
    qb_tok = [None] * T; qro_tok = [None] * T; qT_tok = [None] * T
    qev_tok = [None] * T; qdma_tok = [None] * T; xndma_tok = [None] * T
    loT_tok = [None] * T; loev_tok = [None] * T; sq2_tok = [None] * T
    var_tok = [None] * T; var2_tok = [None] * T; z_tok = [None] * T
    xlo_tok = [None] * T
    prev = lambda lst, i: lst[i - 1] if i >= 1 else None

    p2a_v = c["p2a_last_v"]
    p2a_s = c["p2a_last_s"]

    for i in range(T):
        xv = a32.view(x3o[i % 3], 1024)
        t_x = p.add("sync", lambda e, i=i, xv=xv: e.dma_start(
            xv, c["x_own"][i * 128:(i + 1) * 128, :]),
            waits=[sq2_tok[i - 3] if i >= 3 else p2a_s], sem=c["sXT"], inc=16)
        t_sq, var_tok[i], t_rstd = _stats_chain(
            p, c, xv, xnf, 0, t_x, prev(var_tok, i) or p2a_v)
        xn_tok[i] = p.add("vector", lambda e, xv=xv: (
            e.tensor_scalar(stage0, xv, st[:, 6:7], st[:, 7:8], ALU.mult, ALU.add),
            e.tensor_scalar(xnf, xv, st[:, 6:7], st[:, 7:8], ALU.mult, ALU.add))[-1],
            waits=[t_rstd, prev(loT_tok, i) or p2a_v], sem=sV)
        tT_tok[i] = p.add("tensor", lambda e: [
            e.transpose(pT[:, ic * 128:(ic + 1) * 128],
                        stage0[:, ic * 128:(ic + 1) * 128], c["sb_id16"][:])
            for ic in range(8)][-1],
            waits=[xn_tok[i], prev(ev_tok, i) or p2a_v], sem=sT)
        ev_tok[i] = p.add("vector", lambda e: [
            e.tensor_copy(stage1[:, ic * 128:(ic + 1) * 128], pT[:, ic * 128:(ic + 1) * 128])
            for ic in range(8)][-1],
            waits=[tT_tok[i], prev(q_tok, i), xndma_tok[i - 1] if i >= 1 else None],
            sem=sV)
        xndma_tok[i] = p.add("sync", lambda e, i=i: e.dma_start(
            c["xn_d"][:].rearrange("p (x t) -> p x t", x=8)[:, :, i * 128:(i + 1) * 128],
            stage1[:].rearrange("p (x t) -> p x t", x=8)),
            waits=[ev_tok[i]], sem=sO, inc=16)
        xlo_tok[i] = p.add("vector", lambda e: e.tensor_sub(qro, xnf, stage0),
                           waits=[xn_tok[i], zdma_tok[i - 1] if i >= 1 else None],
                           sem=sV)
        loT_tok[i] = p.add("tensor", lambda e: [
            e.transpose(pT[:, ic * 128:(ic + 1) * 128],
                        qro[:, ic * 128:(ic + 1) * 128], c["sb_id16"][:])
            for ic in range(8)][-1],
            waits=[xlo_tok[i], ev_tok[i]], sem=sT)
        loev_tok[i] = p.add("vector", lambda e: [
            e.tensor_copy(stage0[:, ic * 128:(ic + 1) * 128], pT[:, ic * 128:(ic + 1) * 128])
            for ic in range(8)][-1],
            waits=[loT_tok[i]], sem=sV)
        def sc_mms(e, i=i):
            last = None
            for n, (stat, mov) in enumerate([(stage1, chi), (stage0, chi), (stage1, clo)]):
                for ic in range(8):
                    last = e.matmul(pE[:, 256 + n * 8:264 + n * 8],
                                    stat[:, ic * 128:(ic + 1) * 128],
                                    mov[:, ic * 8:(ic + 1) * 8],
                                    start=(ic == 0), stop=(ic == 7))
            return last
        sc_tok[i] = p.add("tensor", sc_mms,
                          waits=[loev_tok[i], prev(scp_tok, i),
                                 c["tok_clo"] if i == 0 else None], sem=sT)
        scsb = c["sb_scdbg"][:, i * 8:(i + 1) * 8]
        t_sc0 = p.add("scalar", lambda e, scsb=scsb: e.copy(scsb, pE[:, 256:264]),
                      waits=[sc_tok[i]], sem=sA)
        t_sc1 = p.add("vector", lambda e, scsb=scsb: e.tensor_add(scsb, scsb, pE[:, 264:272]),
                      waits=[t_sc0], sem=sV)
        scp_tok[i] = p.add("vector", lambda e, scsb=scsb: e.tensor_add(scsb, scsb, pE[:, 272:280]),
                          waits=[t_sc1], sem=sV)
        t_am0 = p.add("vector", lambda e, scsb=scsb: e.reduce_max(st[:, 16:17], scsb, axis=AX.X),
                      waits=[scp_tok[i], prev(am_tok, i)], sem=sV)
        t_am0b = p.add("vector", lambda e: e.tensor_scalar_add(
            st[:, 21:22], st[:, 16:17], -1e-4), waits=[t_am0], sem=sV)
        t_am1 = p.add("vector", lambda e, i=i, scsb=scsb: e.tensor_scalar(
            c["sb_amall"][:, i * 8:(i + 1) * 8], scsb, st[:, 21:22], None, ALU.is_ge),
            waits=[t_am0b], sem=sV)
        t_alpha = p.add("scalar", lambda e: e.activation(
            st[:, 17:18], st[:, 16:17], AF.Sigmoid), waits=[t_am1], sem=sA)
        am_tok[i] = p.add("vector", lambda e, i=i: e.tensor_scalar(
            c["sb_amall"][:, i * 8:(i + 1) * 8], c["sb_amall"][:, i * 8:(i + 1) * 8],
            st[:, 17:18], None, ALU.mult), waits=[t_alpha], sem=sV)
        sq2_tok[i], var2_tok[i], t_rstd2 = _stats_chain(
            p, c, xnf, xv, 8, xn_tok[i], prev(var2_tok, i) or p2a_v)
        z_tok[i] = p.add("vector", lambda e: e.tensor_scalar(
            stage0, xnf, st[:, 14:15], st[:, 15:16], ALU.mult, ALU.add),
            waits=[t_rstd2, loT_tok[i], am_tok[i]], sem=sV)
        zT_tok[i] = p.add("tensor", lambda e: [
            e.transpose(pT[:, ic * 128:(ic + 1) * 128],
                        stage0[:, ic * 128:(ic + 1) * 128], c["sb_id16"][:])
            for ic in range(8)][-1],
            waits=[z_tok[i], loev_tok[i], sc_tok[i]], sem=sT)
        zev_tok[i] = p.add("vector", lambda e: [
            e.tensor_copy(qro[:, ic * 128:(ic + 1) * 128], pT[:, ic * 128:(ic + 1) * 128])
            for ic in range(8)][-1],
            waits=[zT_tok[i], loT_tok[i]], sem=sV)
        zdma_tok[i] = p.add("sync", lambda e, i=i: e.dma_start(
            c["z_d"][:].rearrange("p (x t) -> p x t", x=8)[:, :, i * 128:(i + 1) * 128],
            qro[:].rearrange("p (x t) -> p x t", x=8)),
            waits=[zev_tok[i]], sem=sO, inc=16)
        q_tok[i] = p.add("tensor", lambda e: [
            e.matmul(pD[:, oh * 512:(oh + 1) * 512], stage1[:, ic * 128:(ic + 1) * 128],
                     wq_sb(ic)[:, oh * 512:(oh + 1) * 512],
                     start=(ic == 0), stop=(ic == 7))
            for oh in range(2) for ic in range(8)][-1],
            waits=[ev_tok[i], qb_tok[i - 1] if i >= 1 else None,
                   c["tok_wq"] if i == 0 else None], sem=sT)
        qb_tok[i] = p.add("vector", lambda e: [
            e.scalar_tensor_tensor(xnf[:, oh * 512:(oh + 1) * 512],
                                   pD[:, oh * 512:(oh + 1) * 512], 0.125,
                                   a32.view(A32["bq"] + oh * 512, 512),
                                   ALU.mult, ALU.add) for oh in range(2)][-1],
            waits=[q_tok[i], sq2_tok[i], z_tok[i],
                   c["tok_bq"] if i == 0 else None], sem=sV)
        qro_tok[i] = _rope_emit(
            p, c, xnf, qro, 16,
            a32.view(A32["cown"] + i * 32, 32), a32.view(A32["sown"] + i * 32, 32),
            qb_tok[i], extra_waits=[z_tok[i], zdma_tok[i]])
        qT_tok[i] = p.add("tensor", lambda e: [
            e.transpose(pT[:, ic * 128:(ic + 1) * 128],
                        qro[:, ic * 128:(ic + 1) * 128], c["sb_id16"][:])
            for ic in range(8)][-1],
            waits=[qro_tok[i], zev_tok[i]], sem=sT)
        qev_tok[i] = p.add("vector", lambda e: [
            e.tensor_copy(stage1[:, ic * 128:(ic + 1) * 128], pT[:, ic * 128:(ic + 1) * 128])
            for ic in range(8)][-1],
            waits=[qT_tok[i], xndma_tok[i], q_tok[i]], sem=sV)
        qdma_tok[i] = p.add("sync", lambda e, i=i: e.dma_start(
            c["q_d"][:].rearrange("p (x t) -> p x t", x=8)[:, :, i * 128:(i + 1) * 128],
            stage1[:].rearrange("p (x t) -> p x t", x=8)),
            waits=[qev_tok[i]], sem=sO, inc=16)
    p.run(c["nc"])
    c["own_tail"] = dict(qdma=qdma_tok[-1], am=am_tok[-1], v=(sV, sV.n),
                         s=(sA, sA.n), t=(sT, sT.n), o=(sO, sO.n))


def _l1_dbg_tail(c, names):
    sO, sX = c["sO"], c["sX"]
    a16 = c["a16"]
    p = Prog()
    w = [c["own_tail"]["v"], c["own_tail"]["s"], c["own_tail"]["t"],
         c["own_tail"]["o"]] if "own_tail" in c else []
    if "moe_tail" in c:
        w += [c["moe_tail"]]
    for k in ("attn_tail", "attn_tail_v", "attn_tail_t", "attn_tail_s"):
        if k in c:
            w += [c[k]]
    srcs = {
        "d_xn": lambda: c["xn_d"][:], "d_z": lambda: c["z_d"][:],
        "d_q": lambda: c["q_d"][:], "d_amask": lambda: c["sb_amall"][:],
        "d_kfm": lambda: a16.view(A16_B, 2 * S_GLOB),
        "d_vst": lambda: a16.view(A16_A, NT_PRE * 260),
        "d_sc": lambda: c["sb_scdbg"][:],
        "d_chi": lambda: a16.view(A16_H + H16["chi"], 128),
        "d_attn": lambda: c["attn_d"][:], "d_y": lambda: c["y_d"][:],
        "d_acc": lambda: c["a32"].view(A32["acc"], 16384),
    }
    for nm in names:
        p.add("sync", lambda e, nm=nm: e.dma_start(c["dbg_outs"][nm][:], srcs[nm]()),
              waits=w, sem=sO, inc=16)
    for nm in ("ypart", "ffn1"):
        p.add("gpsimd", lambda e, nm=nm: e.dma_start(c[nm][:, 0:128], c["xn_d"][:, 0:128]),
              waits=w, sem=sO, inc=16)
    p.add("sync", lambda e: e.dma_start(c["qp_o"][:, 0:128], c["xn_d"][:, 0:128]),
          waits=w, sem=sO, inc=16)
    p.add("gpsimd", lambda e: e.dma_start(c["kvpart"][0:64, 0:128], c["xn_d"][0:64, 0:128]),
          waits=w, sem=sO, inc=16)
    fin = (sO, sO.n)
    p.add("gpsimd", lambda e: e.wait_ge(fin[0].h, fin[1]), sem=None)
    p.run(c["nc"])


def _l1_attn(c):
    a16, a32 = c["a16"], c["a32"]
    sX, sG, sT, sV, sA, sO = c["sX"], c["sG"], c["sT"], c["sV"], c["sA"], c["sO"]
    pA, pB, pE = c["pA"], c["pB"], c["pE"]
    st = c["sb_stat"]
    H = A16_H
    KD = A16_C
    VD = A16_C + 4096
    AST = A16_C + 8256
    PT0 = A16_C + 10304
    PT1 = A16_C + 10560
    ot = c["diag_tail"] | {"o": c["own_tail"]["o"]}

    p = Prog()
    dt = c["diag_tail"]
    dg_tok = dt["v"]

    qs_dma = [None] * NSLOT
    qs_done = [None] * NSLOT
    stg_dma = [None] * NSLOT
    epi_tok = [None] * NSLOT
    last_recb = None
    last_t_out = None

    for s in range(NSLOT):
        qsl = H + (s % 2) * 2048
        R = _R(s)
        qs_dma[s] = p.add("sync", lambda e, s=s, qsl=qsl: e.dma_start(
            a16.t[:, qsl:qsl + 2048].rearrange("p (x t) -> p x t", x=8),
            c["q_d"][:].rearrange("p (x t) -> p x t", x=8)[:, :, s * 256:(s + 1) * 256]),
            waits=[ot["o"], qs_done[s - 2] if s >= 2 else None], sem=c["sQS"], inc=16)
        hd_toks = []
        for h in range(16):
            kvh = h // 4
            slot_h = INV[h]
            qrow = (slot_h % 2) * 64
            qcol = qsl + (slot_h // 2) * 256
            krow = (kvh % 2) * 64
            kbase = A16_B + (kvh // 2) * S_GLOB
            red_toks = []
            for qs2 in range(2):
                mcol = 16 + qs2
                for kb4 in range(R // 4):
                    pS = pA if kb4 % 2 == 0 else pB
                    t_mm = p.add("tensor", lambda e, pS=pS, qrow=qrow, qcol=qcol,
                                 qs2=qs2, krow=krow, kbase=kbase, kb4=kb4: e.matmul(
                        pS[:], a16.t[qrow:qrow + 64, qcol + qs2 * 128:qcol + qs2 * 128 + 128],
                        a16.t[krow:krow + 64, kbase + kb4 * 512:kbase + (kb4 + 1) * 512],
                        start=True, stop=True),
                        waits=[qs_dma[s],
                               red_toks[-2] if len(red_toks) >= 2 else last_t_out],
                        sem=sT)
                    if kb4 == 0:
                        t_red = p.add("vector", lambda e, pS=pS, mcol=mcol: e.reduce_max(
                            st[:, mcol:mcol + 1], pS[:], axis=AX.X),
                            waits=[t_mm], sem=sV)
                    else:
                        t_r0 = p.add("vector", lambda e, pS=pS: e.reduce_max(
                            st[:, 20:21], pS[:], axis=AX.X), waits=[t_mm], sem=sV)
                        t_red = p.add("vector", lambda e, mcol=mcol: e.tensor_max(
                            st[:, mcol:mcol + 1], st[:, mcol:mcol + 1], st[:, 20:21]),
                            waits=[t_r0], sem=sV)
                    red_toks.append(t_red)
            tT = p.add("tensor", lambda e: (
                e.transpose(pE[0:1, 256:384], st[:, 16:17], c["sb_idf"][:]),
                e.transpose(pE[0:1, 384:512], st[:, 17:18], c["sb_idf"][:]))[-1],
                waits=[red_toks[-1], last_recb], sem=sT)
            tmr = p.add("scalar", lambda e: (
                e.copy(c["sb_mrow"][0:1, 0:128], pE[0:1, 256:384]),
                e.copy(c["sb_mrow"][0:1, 128:256], pE[0:1, 384:512]))[-1],
                waits=[tT], sem=sA)
            tmb = p.add("tensor", lambda e: e.matmul(
                pE[:, 0:256], c["sb_ones1"][:], c["sb_mrow"][:], start=True, stop=True),
                waits=[tmr, last_t_out], sem=sT)
            tmp_cp = p.add("scalar", lambda e: e.copy(
                a32.view(A32["m_pl"], 256), pE[:, 0:256]), waits=[tmb], sem=sA)
            tdA = p.add("vector", lambda e: e.tensor_add(
                a32.view(A32["m_dA"], 256), a32.view(A32["m_pl"], 256), c["sb_maskA"][:]),
                waits=[tmp_cp], sem=sV)
            tdB = p.add("vector", lambda e: e.tensor_add(
                a32.view(A32["m_dB"], 256), a32.view(A32["m_pl"], 256), c["sb_maskB"][:]),
                waits=[tmp_cp], sem=sV)
            fb_base = sum(_R(s2) - 2 for s2 in range(s))
            av_toks = []
            exp_toks = [None, None]
            for j in range(R - 2 + 2):
                diag = j >= R - 2
                dd = j - (R - 2)
                par = j % 2
                pS = pA if par == 0 else pB
                ptile = PT0 if par == 0 else PT1
                tmps = A32["tmpS0"] if par == 0 else A32["tmpS1"]
                if not diag:
                    stat = (krow, kbase + j * 128)
                    mop = A32["m_pl"]
                else:
                    stat = (krow, KD + (kvh // 2) * 2048 + (s * 2 + dd) * 128)
                    mop = A32["m_dA"] if dd == 0 else A32["m_dB"]
                t_mm = p.add("tensor", lambda e, pS=pS, stat=stat, qrow=qrow, qcol=qcol: e.matmul(
                    pS[:, 0:256], a16.t[stat[0]:stat[0] + 64, stat[1]:stat[1] + 128],
                    a16.t[qrow:qrow + 64, qcol:qcol + 256], start=True, stop=True),
                    waits=[tmb if j == 0 else None,
                           red_toks[-1] if j < 2 else None,
                           exp_toks[par] if j >= 2 else None,
                           tdA if diag and dd == 0 else None,
                           tdB if diag and dd == 1 else None,
                           dg_tok if (s == 0 and h == 0 and diag) else None], sem=sT)
                t_sub = p.add("vector", lambda e, pS=pS, tmps=tmps, mop=mop: e.tensor_sub(
                    a32.view(tmps, 256), pS[:, 0:256], a32.view(mop, 256)),
                    waits=[t_mm], sem=sV)
                if not diag:
                    fb = c["sb_flag"][:, fb_base + j:fb_base + j + 1]
                    t_exp = p.add("scalar", lambda e, tmps=tmps, ptile=ptile, fb=fb:
                                  e.activation(a16.view(ptile, 256), a32.view(tmps, 256),
                                               AF.Exp, bias=fb),
                                  waits=[t_sub], sem=sA)
                else:
                    t_exp = p.add("scalar", lambda e, tmps=tmps, ptile=ptile:
                                  e.activation(a16.view(ptile, 256), a32.view(tmps, 256),
                                               AF.Exp),
                                  waits=[t_sub], sem=sA)
                exp_toks[par] = t_exp
                vstat = (VD + (s * 2 + dd) * 260 + 65 * kvh) if diag else (j * 260 + 65 * kvh)
                t_av = p.add("tensor", lambda e, vstat=vstat, ptile=ptile, j=j: e.matmul(
                    pE[0:65, 0:256], a16.t[:, vstat:vstat + 65], a16.view(ptile, 256),
                    start=(j == 0), stop=(j == R - 1)),
                    waits=[t_exp], sem=sT)
                av_toks.append(t_av)
            t_dr = p.add("scalar", lambda e: e.copy(c["sb_drow"][:], pE[64:65, 0:256]),
                         waits=[av_toks[-1]], sem=sA)
            t_rc = p.add("vector", lambda e: e.reciprocal(c["sb_drow"][:], c["sb_drow"][:]),
                         waits=[t_dr], sem=sV)
            t_db = p.add("tensor", lambda e: e.matmul(
                pE[0:64, 256:512], c["sb_ones1"][0:1, 0:64], c["sb_drow"][:],
                start=True, stop=True), waits=[t_rc, tmr], sem=sT)
            t_rb = p.add("scalar", lambda e: e.copy(
                a32.view(A32["recb"], 256)[0:64, :], pE[0:64, 256:512]),
                waits=[t_db], sem=sA)
            last_recb = t_rb
            t_out = p.add("vector", lambda e, qrow=qrow, slot_h=slot_h, s=s: e.tensor_tensor(
                a16.t[qrow:qrow + 64, AST + (slot_h // 2) * 256:AST + (slot_h // 2) * 256 + 256],
                pE[0:64, 0:256], a32.view(A32["recb"], 256)[0:64, :], ALU.mult),
                waits=[t_rb, stg_dma[s - 1] if (s >= 1 and h <= 1) else None], sem=sV)
            hd_toks.append(t_out)
            epi_tok[s] = t_out
            last_t_out = t_out
        qs_done[s] = av_toks[-1]
        stg_dma[s] = p.add("sync", lambda e, s=s: e.dma_start(
            c["attn_d"][:].rearrange("p (x t) -> p x t", x=8)[:, :, s * 256:(s + 1) * 256],
            a16.t[:, AST:AST + 2048].rearrange("p (x t) -> p x t", x=8)),
            waits=hd_toks, sem=sO, inc=16)
    p.run(c["nc"])
    c["attn_tail"] = stg_dma[-1]
    c["attn_tail_v"] = (sV, sV.n)
    c["attn_tail_t"] = (sT, sT.n)
    c["attn_tail_s"] = (sA, sA.n)


def _l1_p2a_diag(c):
    a16, a32 = c["a16"], c["a32"]
    sX, sT, sV, sA = c["sX"], c["sT"], c["sV"], c["sA"]
    pA, pT = c["pA"], c["pT"]
    H = A16_H
    st = c["sb_stat"]
    x3o = [A32["x3"] + j * 1024 for j in range(3)]
    stage0 = a16.view(H + H16["stage0"], 1024)
    stage1 = a16.view(H + H16["stage1"], 1024)
    kro = a16.view(H + H16["kro"], 256)
    wkv_sb = lambda ic: a16.view(A16_C + 12288 + ic * 512, 512)
    KD = A16_C
    VD = A16_C + 4096
    ot = c["own_tail"]

    p = Prog()
    N = 16
    xn_tok = [None] * N; tT_tok = [None] * N; ev_tok = [None] * N
    kv_tok = [None] * N; bias_tok = [None] * N; vcp_tok = [None] * N
    kT_tok = [None] * N; kev_tok = [None] * N; var_tok = [None] * N
    prev = lambda lst, i: lst[i - 1] if i >= 1 else None
    t_ones = p.add("vector", lambda e: e.memset(
        a16.view(VD, 16 * 260).rearrange("p (b x) -> p b x", x=65)[:, :, 64:65], 1.0),
        waits=[ot["t"], ot["v"], ot["s"]], sem=c["sV"])

    for i in range(N):
        xv = a32.view(x3o[i % 3], 1024)
        t_x = p.add("sync", lambda e, i=i, xv=xv: e.dma_start(
            xv, c["x_diag"][i * 128:(i + 1) * 128, :]),
            waits=[xn_tok[i - 3] if i >= 3 else ot["s"]], sem=c["sXT"], inc=16)
        sq_tok, var_tok[i], t_rstd = _stats_chain(
            p, c, xv, a32.view(A32["xnf"], 1024), 0, t_x, prev(var_tok, i) or ot["v"])
        xn_tok[i] = p.add("vector", lambda e, xv=xv: e.tensor_scalar(
            stage0, xv, st[:, 6:7], st[:, 7:8], ALU.mult, ALU.add),
            waits=[t_rstd, prev(tT_tok, i) or ot["t"]], sem=sV)
        tT_tok[i] = p.add("tensor", lambda e: [
            e.transpose(pT[:, ic * 128:(ic + 1) * 128],
                        stage0[:, ic * 128:(ic + 1) * 128], c["sb_id16"][:])
            for ic in range(8)][-1],
            waits=[xn_tok[i], prev(ev_tok, i), prev(kev_tok, i)], sem=sT)
        ev_tok[i] = p.add("vector", lambda e: [
            e.tensor_copy(stage1[:, ic * 128:(ic + 1) * 128], pT[:, ic * 128:(ic + 1) * 128])
            for ic in range(8)][-1],
            waits=[tT_tok[i], prev(kv_tok, i)], sem=sV)
        kv_tok[i] = p.add("tensor", lambda e: [
            e.matmul(pA[:], stage1[:, ic * 128:(ic + 1) * 128], wkv_sb(ic),
                     start=(ic == 0), stop=(ic == 7)) for ic in range(8)][-1],
            waits=[ev_tok[i], prev(bias_tok, i)], sem=sT)
        bias_tok[i] = p.add("vector", lambda e: e.tensor_tensor(
            a32.view(A32["kvb"], 512), pA[:], a32.view(A32["bkv"], 512), ALU.add),
            waits=[kv_tok[i], prev(vcp_tok, i)], sem=sV)
        rope_tok = _rope_emit(
            p, c, a32.view(A32["kvb"], 512), kro, 4,
            a32.view(11392 + i * 32, 32), a32.view(11904 + i * 32, 32),
            bias_tok[i], extra_waits=[prev(kT_tok, i)])
        vcp_tok[i] = p.add("scalar", lambda e, i=i: e.copy(
            a16.view(VD + i * 260, 260).rearrange("p (h x) -> p h x", h=4)[:, :, 0:64],
            a32.view(A32["kvb"] + 256, 256).rearrange("p (h x) -> p h x", h=4)),
            waits=[bias_tok[i], t_ones if i == 0 else None], sem=sA)
        kT_tok[i] = p.add("tensor", lambda e: (
            e.transpose(pT[:, 0:128], kro[:, 0:128], c["sb_id16"][:]),
            e.transpose(pT[:, 128:256], kro[:, 128:256], c["sb_id16"][:]))[-1],
            waits=[rope_tok], sem=sT)
        kev_tok[i] = p.add("vector", lambda e, i=i: (
            e.tensor_copy(a16.view(KD + i * 128, 128), pT[:, 0:128]),
            e.tensor_copy(a16.view(KD + 2048 + i * 128, 128), pT[:, 128:256]))[-1],
            waits=[kT_tok[i]], sem=sV)
    p.run(c["nc"])
    c["diag_tail"] = dict(v=(sV, sV.n), s=(sA, sA.n), t=(sT, sT.n))


def _l1_wo(c):
    a16, a32 = c["a16"], c["a32"]
    sX, sG, sT, sV, sA, sO = c["sX"], c["sG"], c["sT"], c["sV"], c["sA"], c["sO"]
    pA, pB, pD, pT = c["pA"], c["pB"], c["pD"], c["pT"]
    at = c["attn_tail"]
    WO = A16_C
    AB0 = A16_C + 8192
    AB1 = A16_C + 12288
    x3o = [A32["x3"] + j * 1024 for j in range(3)]
    p = Prog()

    two = p.add("gpsimd", lambda e: e.dma_start(
        a16.view(WO, 8 * DM).rearrange("p (c o) -> p c o", c=8),
        c["wo"][:].rearrange("(c p) o -> p c o", p=128)),
        waits=[c["attn_tail_t"]], sem=c["sW"], inc=16)
    ab_dma = [None] * 4
    ab_done = [None] * 4
    ystg_dma = [None] * 4
    xT_tok = [None, None, None, None]
    last_ev = c["attn_tail_v"]
    last_sq = c["attn_tail_s"]
    for tc in range(4):
        ab = AB0 if tc % 2 == 0 else AB1
        ab_dma[tc] = p.add("sync", lambda e, tc=tc, ab=ab: e.dma_start(
            a16.t[:, ab:ab + 4096].rearrange("p (x t) -> p x t", x=8),
            c["attn_d"][:].rearrange("p (x t) -> p x t", x=8)[:, :, tc * 512:(tc + 1) * 512]),
            waits=[at if tc == 0 else None, ab_done[tc - 2] if tc >= 2 else None],
            sem=c["sAB"] if tc % 2 == 0 else c["sAB2"], inc=16)
        ev_toks = []
        for t4 in range(4):
            i = tc * 4 + t4
            xv = a32.view(x3o[i % 3], 1024)
            t_x = p.add("sync", lambda e, i=i, xv=xv: e.dma_start(
                xv, c["x_own"][i * 128:(i + 1) * 128, :]),
                waits=[xT_tok[(i - 3) % 4] if i >= 3 else last_sq], sem=c["sXT"], inc=16)
            tT = p.add("tensor", lambda e, xv=xv: [
                e.transpose(pT[:, 0:1024].bitcast(F32)[:, ic * 128:(ic + 1) * 128]
                            if False else pD[:, ic * 128:(ic + 1) * 128],
                            xv[:, ic * 128:(ic + 1) * 128], c["sb_idf"][:])
                for ic in range(8)][-1],
                waits=[t_x, last_ev], sem=sT)
            xT_tok[i % 4] = tT
            t_ev = p.add("vector", lambda e, t4=t4: [
                e.tensor_copy(a32.view(A32["yst"] + ic * 512 + t4 * 128, 128),
                              pD[:, ic * 128:(ic + 1) * 128]) for ic in range(8)][-1],
                waits=[tT, ystg_dma[tc - 1] if (tc >= 1 and t4 == 0) else None], sem=sV)
            last_ev = t_ev
            ev_toks.append(t_ev)
        oc_toks = []
        last_comb = None
        for oc in range(8):
            pS = pD[:, 1024:1536] if oc % 2 == 0 else pD[:, 1536:2048]
            t_mm = p.add("tensor", lambda e, pS=pS, oc=oc, ab=ab: [
                e.matmul(pS, a16.t[:, WO + ic * DM + oc * 128:WO + ic * DM + oc * 128 + 128],
                         a16.t[:, ab + ic * 512:ab + (ic + 1) * 512],
                         start=(ic == 0), stop=(ic == 7)) for ic in range(8)][-1],
                waits=[ab_dma[tc], two if (tc == 0 and oc == 0) else None,
                       ev_toks[-1] if oc == 0 else None,
                       oc_toks[-1] if oc_toks else
                       (ab_done[tc - 1] if tc >= 1 else None)], sem=sT)
            t_comb = p.add("vector", lambda e, pS=pS, oc=oc: e.scalar_tensor_tensor(
                a32.view(A32["yst"] + oc * 512, 512), pS, c["sb_boc"][:, oc:oc + 1],
                a32.view(A32["yst"] + oc * 512, 512), ALU.add, ALU.add),
                waits=[t_mm, ev_toks[-1] if oc == 0 else None], sem=sV)
            oc_toks.append(t_comb)
            last_comb = t_comb
        ab_done[tc] = last_comb
        ystg_dma[tc] = p.add("sync", lambda e, tc=tc: e.dma_start(
            c["y_d"][:].rearrange("p (x t) -> p x t", x=8)[:, :, tc * 512:(tc + 1) * 512],
            a32.view(A32["yst"], 4096).rearrange("p (x t) -> p x t", x=8)),
            waits=oc_toks, sem=sO, inc=16)
    p.run(c["nc"], serialize=True)
    c["wo_tail"] = dict(o=ystg_dma[-1], v=(sV, sV.n), s=(sA, sA.n), t=(sT, sT.n))


def _l1_moe(c):
    a16, a32 = c["a16"], c["a32"]
    sX, sG, sT, sV, sA, sO, sP = (c["sX"], c["sG"], c["sT"], c["sV"], c["sA"],
                                  c["sO"], c["sP"])
    pA, pD, pE = c["pA"], c["pD"], c["pE"]
    H = A16_H
    wt = c["wo_tail"]
    p = Prog()

    t_z = p.add("sync", lambda e: e.dma_start(
        a16.view(A16_C, 16384), c["z_d"][:]), waits=[wt["t"], wt["v"]], sem=c["sZL"], inc=16)
    t_acc = p.add("gpsimd", lambda e: e.dma_start(
        a32.view(A32["acc"], 16384), c["xn_d"][:]),
        waits=[wt["o"], wt["v"]], sem=c["sAC"], inc=16)

    t_dbg0 = p.add("scalar", lambda e: e.copy(
        c["sb_scdbg"][:, 0:64], a32.view(A32["acc"], 64)), waits=[t_acc], sem=c["sA"])
    c["_dbg0"] = t_dbg0
    wbuf_done = [wt["t"], wt["t"]]
    wdma = [None, None]
    prev_evac = None
    prev_rm = None
    prev_rt = None
    prev_hb = None
    rm_war = [None] * 8
    prev_bc = None
    bc_cp = None
    first_bc = True
    first_h = True
    for e8 in range(8):
        for qq in range(4):
            par = (e8 * 4 + qq) % 2
            wb = A16_A if par == 0 else A16_B
            sGP = c["sGA"] if par == 0 else c["sGB"]
            p.add("gpsimd", lambda e, e8=e8, qq=qq, wb=wb: e.dma_start(
                a16.t[:, wb:wb + 8192].rearrange("p (c o) -> p c o", c=8),
                c["w1t"][e8][:, qq * 1024:(qq + 1) * 1024]
                .rearrange("(c p) o -> p c o", p=128)),
                waits=[wbuf_done[par]], sem=sGP, inc=16)
            wdma[par] = p.add("gpsimd", lambda e, e8=e8, qq=qq, wb=wb: e.dma_start(
                a16.t[:, wb + 8192:wb + 16384].rearrange("p (c o) -> p c o", c=8),
                c["w2t"][e8][qq * 1024:(qq + 1) * 1024, :]
                .rearrange("(c p) o -> p c o", p=128)),
                sem=sGP, inc=16)
            for tt in range(8):
                tt1 = p.add("tensor", lambda e, e8=e8, tt=tt: (
                    e.transpose(pE[0:1, 256:384],
                                c["sb_amall"][:, (2 * tt) * 8 + e8:(2 * tt) * 8 + e8 + 1],
                                c["sb_idf"][:]),
                    e.transpose(pE[0:1, 384:512],
                                c["sb_amall"][:, (2 * tt + 1) * 8 + e8:(2 * tt + 1) * 8 + e8 + 1],
                                c["sb_idf"][:]))[-1],
                    waits=[bc_cp, c["own_tail"]["v"] if first_bc else None,
                           wt["s"] if first_bc else None], sem=sT)
                tt2 = p.add("scalar", lambda e: (
                    e.copy(c["sb_arow"][0:1, 0:128], pE[0:1, 256:384]),
                    e.copy(c["sb_arow"][0:1, 128:256], pE[0:1, 384:512]))[-1],
                    waits=[tt1], sem=sA)
                tt3 = p.add("tensor", lambda e: e.matmul(
                    pE[:, 0:256], c["sb_ones1"][:], c["sb_arow"][:], start=True, stop=True),
                    waits=[tt2], sem=sT)
                bc_cp = p.add("scalar", lambda e: e.copy(
                    a32.view(A32["abc"], 256), pE[:, 0:256]),
                    waits=[tt3, prev_bc], sem=sA)
                first_bc = False
                t_h = p.add("vector", lambda e, e8=e8, tt=tt: [
                    e.tensor_scalar(a16.view(H + H16["hbuf"] + ic * 256, 256),
                                    a16.view(A16_C + ic * 2048 + tt * 256, 256),
                                    c["sb_gmc"][:, e8 * 8 + ic:e8 * 8 + ic + 1],
                                    c["sb_btc"][:, e8 * 8 + ic:e8 * 8 + ic + 1],
                                    ALU.mult, ALU.add) for ic in range(8)][-1],
                    waits=[t_z if first_h else None, prev_hb], sem=sV)
                first_h = False
                last_y1 = None
                last_y2 = None
                rt_v = a16.view(A16_B + 16384, 256)
                rm_all = lambda fc: a16.view(H + 2048 + fc * 256, 256)
                rm_toks = []
                for fc in range(8):
                    t_y1 = p.add("tensor", lambda e, wb=wb, fc=fc: [
                        e.matmul(pA[:, 0:256],
                                 a16.t[:, wb + ic * 1024 + fc * 128:wb + ic * 1024 + fc * 128 + 128],
                                 a16.view(H + H16["hbuf"] + ic * 256, 256),
                                 start=(ic == 0), stop=(ic == 7)) for ic in range(8)][-1],
                        waits=[t_h if fc == 0 else None,
                               wdma[par] if tt == 0 and fc == 0 else None,
                               prev_rt, prev_evac if fc == 0 else None], sem=sT)
                    last_y1 = t_y1
                    t_r = p.add("scalar", lambda e, e8=e8, qq=qq, fc=fc: e.activation(
                        rt_v, pA[:, 0:256], AF.Relu,
                        bias=c["sb_b1c"][:, e8 * 32 + qq * 8 + fc:e8 * 32 + qq * 8 + fc + 1]),
                        waits=[t_y1, prev_rm], sem=sA)
                    prev_rt = t_r
                    t_rm = p.add("vector", lambda e, fc=fc: e.tensor_tensor(
                        rm_all(fc), rt_v, a32.view(A32["abc"], 256), ALU.mult),
                        waits=[t_r, bc_cp if fc == 0 else None,
                               rm_war[fc]], sem=sV)
                    prev_rm = t_rm
                    rm_toks.append(t_rm)
                for oc in range(8):
                    t_y2 = p.add("tensor", lambda e, wb=wb, oc=oc: [
                        e.matmul(pD[:, oc * 256:(oc + 1) * 256],
                                 a16.t[:, wb + 8192 + fc * 1024 + oc * 128:wb + 8192 + fc * 1024 + oc * 128 + 128],
                                 rm_all(fc), start=(fc == 0), stop=(fc == 7))
                        for fc in range(8)][-1],
                        waits=[rm_toks[-1] if oc == 0 else None,
                               prev_evac if oc == 0 else None], sem=sT)
                    last_y2 = t_y2
                for fc in range(8):
                    rm_war[fc] = last_y2
                prev_hb = last_y1
                if e8 == 0 and qq == 0 and tt == 0:
                    p.add("scalar", lambda e: e.copy(
                        c["sb_scdbg"][:, 64:128], pD[:, 0:64]),
                        waits=[last_y2], sem=c["sA"])
                ev = None
                for oc in range(8):
                    if qq == 0:
                        t1 = p.add("vector", lambda e, oc=oc, e8=e8: e.scalar_tensor_tensor(
                            a32.view(A32["sttmp"], 256), a32.view(A32["abc"], 256),
                            c["sb_b2c"][:, e8 * 8 + oc:e8 * 8 + oc + 1],
                            pD[:, oc * 256:(oc + 1) * 256], ALU.mult, ALU.add),
                            waits=[last_y2, ev, bc_cp,
                                   t_acc if e8 == 0 and tt == 0 and oc == 0 else None,
                                   c["_dbg0"] if e8 == 0 and tt == 0 and oc == 0 else None,
                                   (c["sA"], c["sA"].n) if e8 == 0 and qq == 0 and tt == 0 and oc == 0 else None],
                            sem=sV)
                        ev = p.add("vector", lambda e, oc=oc, tt=tt: e.tensor_add(
                            a32.view(A32["acc"] + oc * 2048 + tt * 256, 256),
                            a32.view(A32["acc"] + oc * 2048 + tt * 256, 256),
                            a32.view(A32["sttmp"], 256)),
                            waits=[t1], sem=sV)
                    else:
                        ev = p.add("vector", lambda e, oc=oc, tt=tt: e.tensor_add(
                            a32.view(A32["acc"] + oc * 2048 + tt * 256, 256),
                            a32.view(A32["acc"] + oc * 2048 + tt * 256, 256),
                            pD[:, oc * 256:(oc + 1) * 256]),
                            waits=[last_y2, t_acc if e8 == 0 and tt == 0 and oc == 0 else None,
                                   ev], sem=sV)
                prev_evac = ev
                prev_bc = ev if qq == 0 else prev_rt
            wbuf_done[par] = prev_evac
    fold_dma = []
    last_fold = None
    for k in range(32):
        slot = 16384 if k % 2 == 0 else 16896
        t_l = p.add("sync", lambda e, k=k, slot=slot: e.dma_start(
            a32.view(slot, 512), c["y_d"][:, k * 512:(k + 1) * 512]),
            waits=[prev_evac if k == 0 else None,
                   fold_dma[k - 2] if k >= 2 else None], sem=c["sFL"], inc=16)
        t_a = p.add("vector", lambda e, k=k, slot=slot: e.tensor_add(
            a32.view(slot, 512), a32.view(slot, 512), a32.view(A32["acc"] + k * 512, 512)),
            waits=[t_l, last_fold], sem=sV)
        last_fold = t_a
        t_o = p.add("sync", lambda e, k=k, slot=slot: e.dma_start(
            c["ypart"][:, k * 512:(k + 1) * 512], a32.view(slot, 512)),
            waits=[t_a], sem=sO, inc=16)
        fold_dma.append(t_o)
    p.run(c["nc"])
    c["moe_tail"] = fold_dma[-1]
    c["moe_tail_v"] = (sV, sV.n)
    c["moe_tail_t"] = (sT, sT.n)
    c["moe_tail_s"] = (sA, sA.n)
    c["moe_tail_g"] = (sG, sG.n)



def _l1_block1(c):
    a16, a32 = c["a16"], c["a32"]
    sX, sT, sV, sA, sO = c["sX"], c["sT"], c["sV"], c["sA"], c["sO"]
    pA, pB, pD, pT = c["pA"], c["pB"], c["pD"], c["pT"]
    st = c["sb_stat"]
    H = A16_H
    mt = c["moe_tail"]
    p = Prog()
    w = lambda *t: [x for x in t if x]

    t0 = p.add("sync", lambda e: e.dma_start(c["sb_row"][:], c["qn_r"][:]),
               waits=[mt, c["moe_tail_v"], c["moe_tail_t"], c["moe_tail_s"]],
               sem=sX, inc=16)
    for half in range(2):
        tm = p.add("tensor", lambda e, half=half: e.matmul(
            pA[:], c["sb_ones1"][:], c["sb_row"][0:1, half * 512:(half + 1) * 512],
            start=True, stop=True), sem=sT)
        p.add("scalar", lambda e, half=half: e.copy(
            a32.view(A32["qnb"] + half * 512, 512), pA[:]), sem=sA)
    p.add("sync", lambda e: e.dma_start(c["sb_row"][:], c["kn_r"][:]), sem=sX, inc=16)
    for half in range(2):
        p.add("tensor", lambda e, half=half: e.matmul(
            pA[:], c["sb_ones1"][:], c["sb_row"][0:1, half * 512:(half + 1) * 512],
            start=True, stop=True), sem=sT)
        p.add("scalar", lambda e, half=half: e.copy(
            a32.view(A32["knb"] + half * 512, 512), pA[:]), sem=sA)
    p.add("gpsimd", lambda e: e.dma_start(
        a16.view(0, 24576).rearrange("p (c o) -> p c o", c=8),
        c["wqkv"][:].rearrange("(c p) o -> p c o", p=128)), sem=c["sAC"], inc=16)
    p.add("vector", lambda e: e.memset(st[:, 23:24], 1.0), sem=sV)
    p.add("vector", lambda e: e.memset(a32.view(6144, 1040), 0.0), sem=sV)

    for sub in range(4):
        p.add("vector", lambda e: (
            e.memset(a32.view(A32["bc1"], 512)[0:1, :], 0.0),
            e.memset(a32.view(A32["bc2"], 512)[0:1, :], 0.0))[-1], sem=sV)
        for c8 in range(8):
            p.add("sync", lambda e, sub=sub, c8=c8: e.dma_start(
                a32.view(A32["ysl"] + (c8 % 2) * 1024, 512),
                c["ypart"][:, c8 * 2048 + sub * 512:c8 * 2048 + sub * 512 + 512]),
                sem=c["sFL"], inc=16)
            p.add("tensor", lambda e, c8=c8: e.matmul(
                pA[0:1, 0:512], st[:, 23:24], a32.view(A32["ysl"] + (c8 % 2) * 1024, 512),
                start=True, stop=True), sem=sT)
            p.add("vector", lambda e: e.tensor_add(
                a32.view(A32["bc2"], 512)[0:1, :], a32.view(A32["bc2"], 512)[0:1, :],
                pA[0:1, 0:512]), sem=sV)
            p.add("scalar", lambda e, c8=c8: e.activation(
                a32.view(A32["sttmp"], 512), a32.view(A32["ysl"] + (c8 % 2) * 1024, 512),
                AF.Square), sem=sA)
            p.add("tensor", lambda e: e.matmul(
                pA[0:1, 0:512], st[:, 23:24], a32.view(A32["sttmp"], 512),
                start=True, stop=True), sem=sT)
            p.add("vector", lambda e: e.tensor_add(
                a32.view(A32["bc1"], 512)[0:1, :], a32.view(A32["bc1"], 512)[0:1, :],
                pA[0:1, 0:512]), sem=sV)
        p.add("vector", lambda e: e.tensor_scalar_mul(
            a32.view(A32["bc2"], 512)[0:1, :], a32.view(A32["bc2"], 512)[0:1, :],
            1.0 / 1024.0), sem=sV)
        p.add("vector", lambda e: e.tensor_scalar_mul(
            a32.view(A32["bc1"], 512)[0:1, :], a32.view(A32["bc1"], 512)[0:1, :],
            1.0 / 1024.0), sem=sV)
        p.add("vector", lambda e: e.tensor_tensor(
            a32.view(A32["sttmp"], 512)[0:1, :], a32.view(A32["bc2"], 512)[0:1, :],
            a32.view(A32["bc2"], 512)[0:1, :], ALU.mult), sem=sV)
        p.add("vector", lambda e: e.tensor_sub(
            a32.view(A32["bc1"], 512)[0:1, :], a32.view(A32["bc1"], 512)[0:1, :],
            a32.view(A32["sttmp"], 512)[0:1, :]), sem=sV)
        p.add("scalar", lambda e: e.activation(
            a32.view(A32["bc1"], 512)[0:1, :], a32.view(A32["bc1"], 512)[0:1, :],
            AF.Sqrt, bias=st[0:1, 18:19]), sem=sA)
        p.add("vector", lambda e: e.reciprocal(
            a32.view(A32["bc1"], 512)[0:1, :], a32.view(A32["bc1"], 512)[0:1, :]), sem=sV)
        p.add("tensor", lambda e: e.matmul(
            pA[:, 0:512], c["sb_ones1"][:], a32.view(A32["bc1"], 512)[0:1, :],
            start=True, stop=True), sem=sT)
        p.add("tensor", lambda e: e.matmul(
            pB[:, 0:512], c["sb_ones1"][:], a32.view(A32["bc2"], 512)[0:1, :],
            start=True, stop=True), sem=sT)
        for c8 in range(8):
            p.add("sync", lambda e, sub=sub, c8=c8: e.dma_start(
                a32.view(A32["ysl"] + (c8 % 2) * 1024, 512),
                c["ypart"][:, c8 * 2048 + sub * 512:c8 * 2048 + sub * 512 + 512]),
                sem=c["sFL"], inc=16)
            p.add("vector", lambda e, c8=c8: e.tensor_tensor(
                a32.view(A32["sttmp"], 512), a32.view(A32["ysl"] + (c8 % 2) * 1024, 512),
                pB[:, 0:512], ALU.subtract), sem=sV)
            p.add("vector", lambda e, sub=sub, c8=c8: e.tensor_tensor(
                a16.view(A16_C + c8 * 2048 + sub * 512, 512), a32.view(A32["sttmp"], 512),
                pA[:, 0:512], ALU.mult), sem=sV)

    for stt_ in range(16):
        xtile = lambda ic, stt_=stt_: a16.view(A16_C + ic * 2048 + stt_ * 128, 128)
        for nm, obase, dstH, nbofs in (("k", DM, 0, A32["knb"]), ("q", 0, 2176, A32["qnb"])):
            for half in range(2):
                p.add("tensor", lambda e, obase=obase, half=half, xtile=xtile: [
                    e.matmul(pA[:, 0:512] if half == 0 else pB[:, 0:512], xtile(ic),
                             a16.view(ic * 3072 + obase + half * 512, 512),
                             start=(ic == 0), stop=(ic == 7)) for ic in range(8)][-1],
                    sem=sT)
                p.add("scalar", lambda e, half=half: e.activation(
                    a32.view(A32["sttmp"], 512), (pA if half == 0 else pB)[:, 0:512],
                    AF.Square, accum_out=st[:, 20 + half:21 + half]), sem=sA)
            p.add("vector", lambda e: e.tensor_add(st[:, 22:23], st[:, 20:21], st[:, 21:22]), sem=sV)
            p.add("vector", lambda e: e.tensor_scalar_mul(st[:, 22:23], st[:, 22:23], 1.0 / 1024.0), sem=sV)
            p.add("scalar", lambda e: e.activation(st[:, 22:23], st[:, 22:23], AF.Sqrt,
                                                   bias=st[:, 19:20]), sem=sA)
            p.add("vector", lambda e: e.reciprocal(st[:, 22:23], st[:, 22:23]), sem=sV)
            for half in range(2):
                p.add("vector", lambda e, half=half, nbofs=nbofs: e.tensor_tensor(
                    a32.view(A32["sttmp"], 512), (pA if half == 0 else pB)[:, 0:512],
                    a32.view(nbofs + half * 512, 512), ALU.mult), sem=sV)
                p.add("scalar", lambda e, half=half, dstH=dstH: e.activation(
                    a16.view(H + dstH + half * 512, 512), a32.view(A32["sttmp"], 512),
                    AF.Relu, scale=st[:, 22:23]), sem=sA)
        for half in range(2):
            p.add("tensor", lambda e, half=half, xtile=xtile: [
                e.matmul(pA[:, 0:512], xtile(ic),
                         a16.view(ic * 3072 + 2048 + half * 512, 512),
                         start=(ic == 0), stop=(ic == 7)) for ic in range(8)][-1],
                sem=sT)
            p.add("scalar", lambda e, half=half: e.copy(
                a16.view(H + 1024, 1040).rearrange("p (h x) -> p h x", x=65)
                [:, half * 8:(half + 1) * 8, 0:64],
                pA[:, 0:512].rearrange("p (h x) -> p h x", x=64)), sem=sA)
        p.add("vector", lambda e: e.memset(
            a16.view(H + 1024, 1040).rearrange("p (h x) -> p h x", x=65)[:, :, 64:65], 1.0),
            sem=sV)
        p.add("tensor", lambda e: [
            e.matmul(pD[0:64, h * 128:h * 128 + 65], a16.view(H + h * 64, 64),
                     a16.view(H + 1024 + h * 65, 65), start=True, stop=True)
            for h in range(16)][-1], sem=sT)
        p.add("vector", lambda e: e.tensor_add(
            a32.view(6144, 1040).rearrange("p (h x) -> p h x", x=65)[0:64],
            a32.view(6144, 1040).rearrange("p (h x) -> p h x", x=65)[0:64],
            pD[0:64, 0:2048].rearrange("p (h x) -> p h x", x=128)[:, :, 0:65]), sem=sV)
        p.add("tensor", lambda e: [
            e.transpose(pT[:, ic * 128:(ic + 1) * 128],
                        a16.view(H + 2176 + ic * 128, 128), c["sb_id16"][:])
            for ic in range(8)][-1], sem=sT)
        p.add("vector", lambda e: [
            e.tensor_copy(a16.view(H, 1024)[:, ic * 128:(ic + 1) * 128],
                          pT[:, ic * 128:(ic + 1) * 128]) for ic in range(8)][-1], sem=sV)
        p.add("sync", lambda e, stt_=stt_: e.dma_start(
            c["qp_o"][:].rearrange("p (x t) -> p x t", x=8)[:, :, stt_ * 128:(stt_ + 1) * 128],
            a16.view(H, 1024).rearrange("p (x t) -> p x t", x=8)),
            sem=sO, inc=16)
    p.add("sync", lambda e: e.dma_start(c["kvpart"][:], a32.view(6144, 1040)[0:64, :]),
          sem=sO, inc=16)
    p.run(c["nc"], serialize=True)
    c["b1_tail"] = dict(o=(sO, sO.n), v=(sV, sV.n), s=(sA, sA.n), t=(sT, sT.n))


def _l1_ffn1(c):
    a16, a32 = c["a16"], c["a32"]
    sX, sT, sV, sA, sO = c["sX"], c["sT"], c["sV"], c["sA"], c["sO"]
    pA, pB = c["pA"], c["pB"]
    H = A16_H
    bt = c["b1_tail"]
    p = Prog()
    t_conv = p.add("gpsimd", lambda e: e.wait_ge(c["sW"].h, c["tok_conv"][1]),
                   waits=[bt["o"], bt["v"], bt["t"], bt["s"]], sem=c["sP"])

    hw_war = None
    gw_war = None
    pa_war = None
    ffdma = [None] * 4
    for tt in range(4):
        sl_user = [None, None]
        silu_tok = None; ts2_tok = None; hm_tok_prev = None
        hm_tok = None
        for fg in range(32):
            par = fg % 2
            slab = H + par * 2048
            d1 = p.add("sync", lambda e, fg=fg, slab=slab: e.dma_start(
                a16.view(slab, 1024).rearrange("p (c o) -> p c o", c=8),
                c["w1b"][:, fg * 128:(fg + 1) * 128].rearrange("(c p) o -> p c o", p=128)),
                waits=[sl_user[par], t_conv if fg < 2 else None,
                       hw_war if fg < 2 else None], sem=c["sFL"], inc=16)
            d2 = p.add("sync", lambda e, fg=fg, slab=slab: e.dma_start(
                a16.view(slab + 1024, 1024).rearrange("p (c o) -> p c o", c=8),
                c["w2b"][:, fg * 128:(fg + 1) * 128].rearrange("(c p) o -> p c o", p=128)),
                sem=c["sFL"], inc=16)
            t1 = p.add("tensor", lambda e, tt=tt, slab=slab: [
                e.matmul(pA[:, 0:512], a16.view(slab + ic * 128, 128),
                         a16.view(A16_C + ic * 2048 + tt * 512, 512),
                         start=(ic == 0), stop=(ic == 7)) for ic in range(8)][-1],
                waits=[d2, silu_tok, pa_war if fg == 0 else None], sem=sT)
            t2 = p.add("tensor", lambda e, tt=tt, slab=slab: [
                e.matmul(pB[:, 0:512], a16.view(slab + 1024 + ic * 128, 128),
                         a16.view(A16_C + ic * 2048 + tt * 512, 512),
                         start=(ic == 0), stop=(ic == 7)) for ic in range(8)][-1],
                waits=[ts2_tok], sem=sT)
            sl_user[par] = t2
            silu_tok = p.add("scalar", lambda e, fg=fg: e.activation(
                a32.view(A32["ftmp"], 512), pA[:, 0:512], AF.Silu,
                scale=c["sb_s1c"][:, fg:fg + 1]),
                waits=[t1, hm_tok], sem=sA)
            ts2_tok = p.add("vector", lambda e, fg=fg: e.tensor_scalar(
                a32.view(A32["ftmp2"], 512), pB[:, 0:512], c["sb_s2c"][:, fg:fg + 1],
                None, ALU.mult), waits=[t2, hm_tok], sem=sV)
            hm_tok = p.add("vector", lambda e, fg=fg: e.tensor_tensor(
                a16.view(A16_A + fg * 512, 512), a32.view(A32["ftmp"], 512),
                a32.view(A32["ftmp2"], 512), ALU.mult),
                waits=[silu_tok, ts2_tok, hw_war if fg == 0 else None], sem=sV)
        sl_user = [None, None]
        act_tok = None
        last_b = None
        for oc in range(32):
            for ih in range(2):
                par = ih
                slab = H + par * 2048
                d = p.add("sync", lambda e, oc=oc, ih=ih, slab=slab: e.dma_start(
                    a16.view(slab, 2048).rearrange("p (c o) -> p c o", c=16),
                    c["h1b"][ih * 2048:(ih + 1) * 2048, oc * 128:(oc + 1) * 128]
                    .rearrange("(c p) o -> p c o", p=128)),
                    waits=[sl_user[par], hm_tok if oc == 0 and ih == 0 else None],
                    sem=c["sFL"], inc=16)
                t = p.add("tensor", lambda e, ih=ih, slab=slab: [
                    e.matmul(pA[:, 0:512], a16.view(slab + ic * 128, 128),
                             a16.view(A16_A + (ih * 16 + ic) * 512, 512),
                             start=(ih == 0 and ic == 0), stop=(ih == 1 and ic == 15))
                    for ic in range(16)][-1],
                    waits=[d, hm_tok if oc == 0 and ih == 0 else None,
                           act_tok if ih == 0 else None], sem=sT)
                sl_user[par] = t
                last_b = t
            act_tok = p.add("scalar", lambda e, oc=oc: e.activation(
                a16.view(A16_B + oc * 512, 512), pA[:, 0:512], AF.Gelu,
                scale=c["sb_hs1c"][:, oc:oc + 1]),
                waits=[last_b, gw_war if oc == 0 else None], sem=sA)
        hw_war_b = last_b
        sl_user = [None, None]
        act2_tok = None
        for oc in range(32):
            for ih in range(2):
                par = ih
                slab = H + par * 2048
                d = p.add("sync", lambda e, oc=oc, ih=ih, slab=slab: e.dma_start(
                    a16.view(slab, 2048).rearrange("p (c o) -> p c o", c=16),
                    c["h2b"][ih * 2048:(ih + 1) * 2048, oc * 128:(oc + 1) * 128]
                    .rearrange("(c p) o -> p c o", p=128)),
                    waits=[sl_user[par], act_tok if oc == 0 and ih == 0 else None],
                    sem=c["sFL"], inc=16)
                t = p.add("tensor", lambda e, ih=ih, slab=slab: [
                    e.matmul(pA[:, 0:512], a16.view(slab + ic * 128, 128),
                             a16.view(A16_B + (ih * 16 + ic) * 512, 512),
                             start=(ih == 0 and ic == 0), stop=(ih == 1 and ic == 15))
                    for ic in range(16)][-1],
                    waits=[d, act_tok if oc == 0 and ih == 0 else None,
                           act2_tok if ih == 0 else None], sem=sT)
                sl_user[par] = t
                last_c = t
            act2_tok = p.add("scalar", lambda e, oc=oc: e.activation(
                a16.view(A16_A + oc * 512, 512), pA[:, 0:512], AF.Gelu,
                scale=c["sb_hs2c"][:, oc:oc + 1]),
                waits=[last_c, hw_war_b if oc == 0 else None], sem=sA)
        gw_war_c = last_c
        ev_tok = None
        wdma_tok = None
        for half in range(2):
            wdma_tok = p.add("sync", lambda e, half=half: e.dma_start(
                a16.view(A16_B, 16384).rearrange("p (c o) -> p c o", c=16),
                c["w3b"][half * 2048:(half + 1) * 2048, :]
                .rearrange("(c p) o -> p c o", p=128)),
                waits=[gw_war_c if half == 0 else last_c2, act2_tok if half == 0 else None],
                sem=c["sFL"], inc=16)
            for oc in range(8):
                t = p.add("tensor", lambda e, half=half, oc=oc: [
                    e.matmul(pA[:, 0:512], a16.view(A16_B + fc * 1024 + oc * 128, 128),
                             a16.view(A16_A + (half * 16 + fc) * 512, 512),
                             start=(fc == 0), stop=(fc == 15)) for fc in range(16)][-1],
                    waits=[wdma_tok if oc == 0 else None,
                           act2_tok if half == 0 and oc == 0 else None,
                           ev_tok], sem=sT)
                last_c2 = t
                if half == 0:
                    ev_tok = p.add("vector", lambda e, oc=oc: e.tensor_scalar(
                        a32.view(A32["facc"] + oc * 512, 512), pA[:, 0:512],
                        c["sb_s3c"][:, oc:oc + 1], None, ALU.mult),
                        waits=[t, ev_tok,
                               ffdma[tt - 1] if tt >= 1 and oc == 0 else None], sem=sV)
                else:
                    ev_tok = p.add("vector", lambda e, oc=oc: e.scalar_tensor_tensor(
                        a32.view(A32["facc"] + oc * 512, 512), pA[:, 0:512],
                        c["sb_s3c"][:, oc:oc + 1], a32.view(A32["facc"] + oc * 512, 512),
                        ALU.mult, ALU.add), waits=[t, ev_tok], sem=sV)
        ffdma[tt] = p.add("sync", lambda e, tt=tt: e.dma_start(
            c["ffn1"][:].rearrange("p (x t) -> p x t", x=8)[:, :, tt * 512:(tt + 1) * 512],
            a32.view(A32["facc"], 4096).rearrange("p (x t) -> p x t", x=8)),
            waits=[ev_tok], sem=sO, inc=16)
        hw_war = last_c2
        gw_war = last_c2
        pa_war = ev_tok
    p.run(c["nc"])
    c["ffn_tail"] = (sO, sO.n)

def _l1_tail(c):
    p = Prog()
    p.add("gpsimd", lambda e: e.wait_ge(c["sO"].h, c["sO"].n),
          waits=[c["ffn_tail"], c["b1_tail"]["o"]], sem=None)
    p.run(c["nc"])



def build_launch2():
    nc = bass.Bass("TRN2", target_bir_lowering=False)
    yp_i = nc.dram_tensor("yp_i", [128, 16384], F32, kind="ExternalInput")
    ff_i = nc.dram_tensor("ff_i", [128, 16384], F32, kind="ExternalInput")
    qp_i = nc.dram_tensor("qp_i", [128, 16384], BF16, kind="ExternalInput")
    kvall = nc.dram_tensor("kvall", [4, 64, 1040], F32, kind="ExternalInput")
    wout = nc.dram_tensor("wout", [DM, DM], F32, kind="ExternalInput")
    bout_c = nc.dram_tensor("bout_c", [128, 8], F32, kind="ExternalInput")
    idf = nc.dram_tensor("idf", [128, 128], F32, kind="ExternalInput")
    out_tm = nc.dram_tensor("out_tm", [2048, DM], F32, kind="ExternalOutput")

    s_kv4 = nc.alloc_sbuf_tensor("s_kv4", [64, 4 * 1040], F32)
    s_kvb = nc.alloc_sbuf_tensor("s_kvb", [128, 1040], BF16)
    s_wout = nc.alloc_sbuf_tensor("s_wout", [128, 8192], BF16)
    s_qp = nc.alloc_sbuf_tensor("s_qp", [128, 4096], BF16)
    s_att = nc.alloc_sbuf_tensor("s_att", [128, 4096], BF16)
    s_yst = nc.alloc_sbuf_tensor("s_yst", [128, 4096], F32)
    s_y1 = nc.alloc_sbuf_tensor("s_y1", [128, 512], F32)
    s_y2 = nc.alloc_sbuf_tensor("s_y2", [128, 512], F32)
    s_rec = nc.alloc_sbuf_tensor("s_rec", [64, 512], F32)
    s_drow = nc.alloc_sbuf_tensor("s_drow", [1, 512], F32)
    s_ones = nc.alloc_sbuf_tensor("s_ones", [1, 128], F32)
    s_idf = nc.alloc_sbuf_tensor("s_idf", [128, 128], F32)
    s_boc = nc.alloc_sbuf_tensor("s_boc", [128, 8], F32)
    s_out = nc.alloc_sbuf_tensor("s_out", [128, 1024], F32)
    pA = nc.alloc_psum_tensor("pA", [128, 512], F32)
    pB = nc.alloc_psum_tensor("pB", [128, 512], F32)
    pC = nc.alloc_psum_tensor("pC", [128, 1024], F32)
    sX = Sem(nc, "sX"); sG = Sem(nc, "sG"); sT = Sem(nc, "sT")
    sV = Sem(nc, "sV"); sA = Sem(nc, "sA"); sO = Sem(nc, "sO")

    p = Prog()
    p.add("sync", lambda e: e.dma_start(s_idf[:], idf[:]), sem=sX, inc=16)
    p.add("sync", lambda e: e.dma_start(s_boc[:], bout_c[:]), sem=sX, inc=16)
    p.add("sync", lambda e: e.dma_start(
        s_kv4[:].rearrange("p (r n) -> p r n", r=4),
        kvall[:].rearrange("r p n -> p r n")), sem=sX, inc=16)
    p.add("gpsimd", lambda e: e.dma_start(
        s_wout[:].rearrange("p (c o) -> p c o", c=8),
        wout[:].rearrange("(c p) o -> p c o", p=128)), sem=sG, inc=16)
    p.add("vector", lambda e: e.memset(s_ones[:], 1.0), sem=sV)
    for r in range(1, 4):
        p.add("vector", lambda e, r=r: e.tensor_add(
            s_kv4[:, 0:1040], s_kv4[:, 0:1040], s_kv4[:, r * 1040:(r + 1) * 1040]), sem=sV)
    p.add("scalar", lambda e: e.copy(s_kvb[0:64, :], s_kv4[:, 0:1040]), sem=sA)
    p.add("vector", lambda e: e.tensor_copy(s_kvb[64:128, :], s_kvb[0:64, :]), sem=sV)

    for tc in range(4):
        p.add("sync", lambda e, tc=tc: e.dma_start(
            s_qp[:].rearrange("p (x t) -> p x t", x=8),
            qp_i[:].rearrange("p (x t) -> p x t", x=8)[:, :, tc * 512:(tc + 1) * 512]),
            sem=sX, inc=16)
        for h in range(16):
            qrow = (h % 2) * 64
            p.add("tensor", lambda e, h=h, qrow=qrow: e.matmul(
                pA[0:65, 0:512], s_kvb[qrow:qrow + 64, 65 * h:65 * h + 65],
                s_qp[qrow:qrow + 64, (h // 2) * 512:(h // 2) * 512 + 512],
                start=True, stop=True), sem=sT)
            p.add("scalar", lambda e: e.copy(s_drow[:], pA[64:65, 0:512]), sem=sA)
            p.add("vector", lambda e: e.tensor_scalar_add(s_drow[:], s_drow[:], 1e-6), sem=sV)
            p.add("vector", lambda e: e.reciprocal(s_drow[:], s_drow[:]), sem=sV)
            p.add("tensor", lambda e: e.matmul(
                pB[0:64, 0:512], s_ones[0:1, 0:64], s_drow[:], start=True, stop=True),
                sem=sT)
            p.add("scalar", lambda e: e.copy(s_rec[:], pB[0:64, 0:512]), sem=sA)
            p.add("vector", lambda e, h=h, qrow=qrow: e.tensor_tensor(
                s_att[qrow:qrow + 64, (h // 2) * 512:(h // 2) * 512 + 512],
                pA[0:64, 0:512], s_rec[:], ALU.mult), sem=sV)
        for oc in range(8):
            p.add("tensor", lambda e, oc=oc: [
                e.matmul(pA[:, 0:512], s_wout[:, ic * DM + oc * 128:ic * DM + oc * 128 + 128],
                         s_att[:, ic * 512:(ic + 1) * 512],
                         start=(ic == 0), stop=(ic == 7)) for ic in range(8)][-1], sem=sT)
            p.add("sync", lambda e, oc=oc, tc=tc: e.dma_start(
                s_y1[:], yp_i[:, oc * 2048 + tc * 512:oc * 2048 + tc * 512 + 512]),
                sem=sX, inc=16)
            p.add("sync", lambda e, oc=oc, tc=tc: e.dma_start(
                s_y2[:], ff_i[:, oc * 2048 + tc * 512:oc * 2048 + tc * 512 + 512]),
                sem=sX, inc=16)
            p.add("vector", lambda e, oc=oc: e.scalar_tensor_tensor(
                s_y1[:], pA[:, 0:512], s_boc[:, oc:oc + 1], s_y1[:], ALU.add, ALU.add),
                sem=sV)
            p.add("vector", lambda e, oc=oc: e.tensor_add(
                s_yst[:, oc * 512:(oc + 1) * 512], s_y1[:], s_y2[:]), sem=sV)
        for t4 in range(4):
            p.add("tensor", lambda e, t4=t4: [
                e.transpose(pC[:, ic * 128:(ic + 1) * 128],
                            s_yst[:, ic * 512 + t4 * 128:ic * 512 + t4 * 128 + 128],
                            s_idf[:]) for ic in range(8)][-1], sem=sT)
            p.add("vector", lambda e: [
                e.tensor_copy(s_out[:, ic * 128:(ic + 1) * 128], pC[:, ic * 128:(ic + 1) * 128])
                for ic in range(8)][-1], sem=sV)
            p.add("sync", lambda e, tc=tc, t4=t4: e.dma_start(
                out_tm[tc * 512 + t4 * 128:tc * 512 + t4 * 128 + 128, :], s_out[:]),
                sem=sO, inc=16)
    p.add("gpsimd", lambda e: e.wait_ge(sO.h, sO.n), sem=None)
    p.run(nc, serialize=True)
    return nc



def kernel(**inputs):
    sh, per, rows_all = _host_prep(inputs)
    nc1, _ = build_launch1()
    in_maps = [dict(sh, **per[c]) for c in range(NCORE)]
    res1 = run_bass_kernel_spmd(nc1, in_maps, core_ids=list(range(NCORE)))

    nc2 = build_launch2()
    kvs = [np.asarray(res1.results[c]["kvpart"]) for c in range(NCORE)]
    in2 = []
    for c in range(NCORE):
        b = c // 4
        in2.append({
            "yp_i": np.asarray(res1.results[c]["ypart"]),
            "ff_i": np.asarray(res1.results[c]["ffn1"]),
            "qp_i": np.asarray(res1.results[c]["qp_o"]),
            "kvall": np.ascontiguousarray(np.stack([kvs[4 * b + j] for j in range(4)])),
            "wout": sh["wout"], "bout_c": sh["bout_c"], "idf": sh["idf"],
        })
    res2 = run_bass_kernel_spmd(nc2, in2, core_ids=list(range(NCORE)))
    out = np.empty((2, S_GLOB, DM), np.float32)
    for c in range(NCORE):
        b = c // 4
        out[b][rows_all[c]] = np.asarray(res2.results[c]["out_tm"])
    return out


# version 03 (final)
# speedup vs baseline: 1.1215x; 1.0142x over previous
import numpy as np
import ml_dtypes
import concourse.bass as bass
import concourse.mybir as mybir
from concourse.bass_utils import run_bass_kernel_spmd

F32 = mybir.dt.float32
BF16 = mybir.dt.bfloat16
AF = mybir.ActivationFunctionType
ALU = mybir.AluOpType
AX = mybir.AxisListType
BF = ml_dtypes.bfloat16

NCORE = 8
S_GLOB = 8192
DM = 1024
C = 8
NT_PRE = 64
NT_OWN = 16
NSLOT = 8
NEG = -1e30
POSINF = 1e30

def _ranges_for(cb):
    g = sorted([cb, 7 - cb, 8 + cb, 15 - cb, 16 + cb, 23 - cb, 24 + cb, 31 - cb],
               reverse=True)
    tr = [2 * (gi + 1) for gi in g]
    return g, tr

def _R(s):
    return 64 - 8 * s

N_MAIN = sum(_R(s) - 2 for s in range(NSLOT))
ORDER = [0, 4, 1, 5, 2, 6, 3, 7, 8, 12, 9, 13, 10, 14, 11, 15]
INV = [ORDER.index(h) for h in range(16)]



def _host_prep(inp):
    f32 = np.float32
    sh = {}
    x0 = np.ascontiguousarray(inp["L0_attn_x"], f32)

    sh["wq"] = np.ascontiguousarray(
        np.asarray(inp["L0_attn_Wq"], f32).reshape(DM, 16, 64)[:, ORDER, :].reshape(DM, DM))
    sh["wkv"] = np.ascontiguousarray(
        np.concatenate([inp["L0_attn_Wk"], inp["L0_attn_Wv"]], axis=1), f32)
    sh["wo"] = np.ascontiguousarray(
        np.asarray(inp["L0_attn_Wo"], f32).reshape(16, 64, DM)[ORDER].reshape(DM, DM))
    sh["bq_r"] = np.ascontiguousarray(
        np.asarray(inp["L0_attn_bq"], f32).reshape(16, 64)[ORDER].reshape(1, DM))
    sh["bkv_r"] = np.ascontiguousarray(
        np.concatenate([inp["L0_attn_bk"], inp["L0_attn_bv"]])[None, :].astype(f32))
    sh["bo_c"] = np.ascontiguousarray(np.asarray(inp["L0_attn_bo"], f32).reshape(8, 128).T)

    sh["centT"] = np.ascontiguousarray(np.asarray(inp["L0_ffn_centroids"], f32).T)
    sh["gm_c"] = np.ascontiguousarray(
        np.asarray(inp["L0_ffn_gamma"], f32).reshape(8, 8, 128).transpose(2, 0, 1).reshape(128, 64))
    sh["bt_c"] = np.ascontiguousarray(
        np.asarray(inp["L0_ffn_beta"], f32).reshape(8, 8, 128).transpose(2, 0, 1).reshape(128, 64))
    sh["w1t"] = np.ascontiguousarray(np.asarray(inp["L0_ffn_W1"], f32).transpose(0, 2, 1))
    sh["b1_c"] = np.ascontiguousarray(
        np.asarray(inp["L0_ffn_b1"], f32).reshape(8, 32, 128).transpose(2, 0, 1).reshape(128, 256))
    sh["w2t"] = np.ascontiguousarray(np.asarray(inp["L0_ffn_W2"], f32).transpose(0, 2, 1))
    sh["b2_c"] = np.ascontiguousarray(
        np.asarray(inp["L0_ffn_b2"], f32).reshape(8, 8, 128).transpose(2, 0, 1).reshape(128, 64))

    sh["wqkv"] = np.ascontiguousarray(inp["L1_attn_W_qkv"], f32)
    sh["qn_r"] = np.ascontiguousarray(np.asarray(inp["L1_attn_qn_w"], f32)[None, :])
    sh["kn_r"] = np.ascontiguousarray(np.asarray(inp["L1_attn_kn_w"], f32)[None, :])
    sh["wout"] = np.ascontiguousarray(inp["L1_attn_W_out"], f32)
    sh["bout_c"] = np.ascontiguousarray(np.asarray(inp["L1_attn_b_out"], f32).reshape(8, 128).T)

    sh["l1w1"] = np.ascontiguousarray(np.asarray(inp["L1_ffn_w1_w"], f32).T)
    sh["l1w2"] = np.ascontiguousarray(np.asarray(inp["L1_ffn_w2_w"], f32).T)
    sh["l1h1"] = np.ascontiguousarray(np.asarray(inp["L1_ffn_h1_w"], f32).T)
    sh["l1h2"] = np.ascontiguousarray(np.asarray(inp["L1_ffn_h2_w"], f32).T)
    sh["l1w3"] = np.ascontiguousarray(np.asarray(inp["L1_ffn_w3_w"], f32).T)
    sh["s1_c"] = np.ascontiguousarray(np.asarray(inp["L1_ffn_w1_s"], f32).reshape(32, 128).T)
    sh["s2_c"] = np.ascontiguousarray(np.asarray(inp["L1_ffn_w2_s"], f32).reshape(32, 128).T)
    sh["hs1_c"] = np.ascontiguousarray(np.asarray(inp["L1_ffn_h1_s"], f32).reshape(32, 128).T)
    sh["hs2_c"] = np.ascontiguousarray(np.asarray(inp["L1_ffn_h2_s"], f32).reshape(32, 128).T)
    sh["s3_c"] = np.ascontiguousarray(np.asarray(inp["L1_ffn_w3_s"], f32).reshape(8, 128).T)

    inv = 1.0 / 10000.0 ** (np.arange(0, 64, 2, dtype=f32) / 64.0)
    ang = np.arange(S_GLOB, dtype=f32)[:, None] * inv[None, :]
    cos_full = np.cos(ang).astype(f32)
    sin_full = np.sin(ang).astype(f32)
    sh["cos_tm"] = np.ascontiguousarray(cos_full.reshape(64, 128, 32).transpose(1, 0, 2).reshape(128, 64 * 32))
    sh["sin_tm"] = np.ascontiguousarray(sin_full.reshape(64, 128, 32).transpose(1, 0, 2).reshape(128, 64 * 32))
    kk = np.arange(128)[:, None]
    qq = np.arange(256)[None, :]
    sh["maskA"] = np.where(kk <= qq, 0.0, POSINF).astype(f32)
    sh["maskB"] = np.where(kk <= qq - 128, 0.0, POSINF).astype(f32)
    sh["id16"] = np.eye(128, dtype=BF)
    sh["idf"] = np.eye(128, dtype=f32)

    per = []
    rows_all = []
    for c in range(NCORE):
        b, cb = divmod(c, 4)
        g, tr = _ranges_for(cb)
        rows = np.concatenate([np.arange(gi * 256, (gi + 1) * 256) for gi in g])
        rows_all.append(rows)
        d = {}
        d["x_own"] = np.ascontiguousarray(x0[b][rows])
        d["x_batch"] = x0[b]
        d["cos_own"] = np.ascontiguousarray(
            cos_full[rows].reshape(16, 128, 32).transpose(1, 0, 2).reshape(128, 16 * 32))
        d["sin_own"] = np.ascontiguousarray(
            sin_full[rows].reshape(16, 128, 32).transpose(1, 0, 2).reshape(128, 16 * 32))
        fb = np.zeros((128, N_MAIN), f32)
        idx = 0
        for s in range(NSLOT):
            for j in range(_R(s) - 2):
                if j >= tr[s] - 2:
                    fb[:, idx] = NEG
                idx += 1
        d["flagbias"] = fb
        dsrc = [tr[s] - 2 + dd for s in range(NSLOT) for dd in (0, 1)]
        d["diag_src"] = np.asarray(dsrc, np.int32)[None, :]
        drows = np.concatenate([np.arange(bk * 128, (bk + 1) * 128) for bk in dsrc])
        d["x_diag"] = np.ascontiguousarray(x0[b][drows])
        d["cos_diag"] = np.ascontiguousarray(
            cos_full[drows].reshape(16, 128, 32).transpose(1, 0, 2).reshape(128, 512))
        d["sin_diag"] = np.ascontiguousarray(
            sin_full[drows].reshape(16, 128, 32).transpose(1, 0, 2).reshape(128, 512))
        per.append(d)
    return sh, per, rows_all



class Sem:
    def __init__(self, nc, name):
        self.h = nc.alloc_semaphore(name)
        self.n = 0

    def inc(self, instr, k=1):
        instr.then_inc(self.h, k)
        self.n += k
        return self.n


class Arena:

    def __init__(self, nc, name, cols, dtype):
        self.t = nc.alloc_sbuf_tensor(name, [128, cols], dtype)
        self.cols = cols
        self.off = 0

    def take(self, n):
        assert self.off + n <= self.cols, (self.off, n, self.cols)
        o = self.off
        self.off += n
        return o

    def view(self, off, n):
        return self.t[:, off:off + n]



A16_A = 0
A16_B = 16640
A16_C = 33280
A16_H = 49664
A16_TOT = 53760

A32_M = 0
A32_T = 16384
A32_TOT = 17408


def build_launch1(dbg=""):
    nc = bass.Bass("TRN2", target_bir_lowering=False)

    x_own = nc.dram_tensor("x_own", [2048, DM], F32, kind="ExternalInput")
    x_batch = nc.dram_tensor("x_batch", [S_GLOB, DM], F32, kind="ExternalInput")
    cos_own = nc.dram_tensor("cos_own", [128, 512], F32, kind="ExternalInput")
    x_diag = nc.dram_tensor("x_diag", [2048, DM], F32, kind="ExternalInput")
    cos_diag = nc.dram_tensor("cos_diag", [128, 512], F32, kind="ExternalInput")
    sin_diag = nc.dram_tensor("sin_diag", [128, 512], F32, kind="ExternalInput")
    sin_own = nc.dram_tensor("sin_own", [128, 512], F32, kind="ExternalInput")
    flagbias = nc.dram_tensor("flagbias", [128, N_MAIN], F32, kind="ExternalInput")
    diag_src = nc.dram_tensor("diag_src", [1, 16], mybir.dt.int32, kind="ExternalInput")
    wq = nc.dram_tensor("wq", [DM, DM], F32, kind="ExternalInput")
    wkv = nc.dram_tensor("wkv", [DM, 512], F32, kind="ExternalInput")
    wo = nc.dram_tensor("wo", [DM, DM], F32, kind="ExternalInput")
    bq_r = nc.dram_tensor("bq_r", [1, DM], F32, kind="ExternalInput")
    bkv_r = nc.dram_tensor("bkv_r", [1, 512], F32, kind="ExternalInput")
    bo_c = nc.dram_tensor("bo_c", [128, 8], F32, kind="ExternalInput")
    centT = nc.dram_tensor("centT", [DM, 8], F32, kind="ExternalInput")
    gm_c = nc.dram_tensor("gm_c", [128, 64], F32, kind="ExternalInput")
    bt_c = nc.dram_tensor("bt_c", [128, 64], F32, kind="ExternalInput")
    w1t = nc.dram_tensor("w1t", [8, DM, 4096], F32, kind="ExternalInput")
    b1_c = nc.dram_tensor("b1_c", [128, 256], F32, kind="ExternalInput")
    w2t = nc.dram_tensor("w2t", [8, 4096, DM], F32, kind="ExternalInput")
    b2_c = nc.dram_tensor("b2_c", [128, 64], F32, kind="ExternalInput")
    wqkv = nc.dram_tensor("wqkv", [DM, 3072], F32, kind="ExternalInput")
    qn_r = nc.dram_tensor("qn_r", [1, DM], F32, kind="ExternalInput")
    kn_r = nc.dram_tensor("kn_r", [1, DM], F32, kind="ExternalInput")
    l1w1 = nc.dram_tensor("l1w1", [DM, 4096], F32, kind="ExternalInput")
    l1w2 = nc.dram_tensor("l1w2", [DM, 4096], F32, kind="ExternalInput")
    l1h1 = nc.dram_tensor("l1h1", [4096, 4096], F32, kind="ExternalInput")
    l1h2 = nc.dram_tensor("l1h2", [4096, 4096], F32, kind="ExternalInput")
    l1w3 = nc.dram_tensor("l1w3", [4096, DM], F32, kind="ExternalInput")
    s1_c = nc.dram_tensor("s1_c", [128, 32], F32, kind="ExternalInput")
    s2_c = nc.dram_tensor("s2_c", [128, 32], F32, kind="ExternalInput")
    hs1_c = nc.dram_tensor("hs1_c", [128, 32], F32, kind="ExternalInput")
    hs2_c = nc.dram_tensor("hs2_c", [128, 32], F32, kind="ExternalInput")
    s3_c = nc.dram_tensor("s3_c", [128, 8], F32, kind="ExternalInput")
    cos_tm = nc.dram_tensor("cos_tm", [128, 2048], F32, kind="ExternalInput")
    sin_tm = nc.dram_tensor("sin_tm", [128, 2048], F32, kind="ExternalInput")
    maskA = nc.dram_tensor("maskA", [128, 256], F32, kind="ExternalInput")
    maskB = nc.dram_tensor("maskB", [128, 256], F32, kind="ExternalInput")
    id16 = nc.dram_tensor("id16", [128, 128], BF16, kind="ExternalInput")
    idf = nc.dram_tensor("idf", [128, 128], F32, kind="ExternalInput")

    xn_d = nc.dram_tensor("xn_d", [128, 16384], BF16, kind="Internal")
    z_d = nc.dram_tensor("z_d", [128, 16384], BF16, kind="Internal")
    q_d = nc.dram_tensor("q_d", [128, 16384], BF16, kind="Internal")
    attn_d = nc.dram_tensor("attn_d", [128, 16384], BF16, kind="Internal")
    y_d = nc.dram_tensor("y_d", [128, 16384], F32, kind="Internal")
    xn1_d = nc.dram_tensor("xn1_d", [128, 16384], BF16, kind="Internal")
    h1b = nc.dram_tensor("h1b", [4096, 4096], BF16, kind="Internal")
    h2b = nc.dram_tensor("h2b", [4096, 4096], BF16, kind="Internal")
    w1b = nc.dram_tensor("w1b", [DM, 4096], BF16, kind="Internal")
    w2b = nc.dram_tensor("w2b", [DM, 4096], BF16, kind="Internal")
    w3b = nc.dram_tensor("w3b", [4096, DM], BF16, kind="Internal")

    ypart = nc.dram_tensor("ypart", [128, 16384], F32, kind="ExternalOutput")
    ffn1 = nc.dram_tensor("ffn1", [128, 16384], F32, kind="ExternalOutput")
    qp_o = nc.dram_tensor("qp_o", [128, 16384], BF16, kind="ExternalOutput")
    kvpart = nc.dram_tensor("kvpart", [64, 1040], F32, kind="ExternalOutput")
    dbg_outs = {}
    if dbg:
        for nm, shp, dt in [
            ("d_xn", [128, 16384], BF16), ("d_z", [128, 16384], BF16),
            ("d_q", [128, 16384], BF16), ("d_amask", [128, 128], F32),
            ("d_kfm", [128, 2 * S_GLOB], BF16), ("d_vst", [128, NT_PRE * 260], BF16),
            ("d_sc", [128, 128], F32), ("d_chi", [128, 128], BF16),
            ("d_attn", [128, 16384], BF16), ("d_y", [128, 16384], F32),
            ("d_acc", [128, 16384], F32),
        ]:
            dbg_outs[nm] = nc.dram_tensor(nm, shp, dt, kind="ExternalOutput")

    a16 = Arena(nc, "a16", A16_TOT, BF16)
    a32 = Arena(nc, "a32", A32_TOT, F32)
    sb_id16 = nc.alloc_sbuf_tensor("sb_id16", [128, 128], BF16)
    sb_idf = nc.alloc_sbuf_tensor("sb_idf", [128, 128], F32)
    sb_maskA = nc.alloc_sbuf_tensor("sb_maskA", [128, 256], F32)
    sb_maskB = nc.alloc_sbuf_tensor("sb_maskB", [128, 256], F32)
    sb_flag = nc.alloc_sbuf_tensor("sb_flag", [128, N_MAIN], F32)
    sb_b1c = nc.alloc_sbuf_tensor("sb_b1c", [128, 256], F32)
    sb_b2c = nc.alloc_sbuf_tensor("sb_b2c", [128, 64], F32)
    sb_gmc = nc.alloc_sbuf_tensor("sb_gmc", [128, 64], F32)
    sb_btc = nc.alloc_sbuf_tensor("sb_btc", [128, 64], F32)
    sb_boc = nc.alloc_sbuf_tensor("sb_boc", [128, 8], F32)
    sb_s1c = nc.alloc_sbuf_tensor("sb_s1c", [128, 32], F32)
    sb_s2c = nc.alloc_sbuf_tensor("sb_s2c", [128, 32], F32)
    sb_hs1c = nc.alloc_sbuf_tensor("sb_hs1c", [128, 32], F32)
    sb_hs2c = nc.alloc_sbuf_tensor("sb_hs2c", [128, 32], F32)
    sb_s3c = nc.alloc_sbuf_tensor("sb_s3c", [128, 8], F32)
    sb_amall = nc.alloc_sbuf_tensor("sb_amall", [128, 128], F32)
    sb_scdbg = nc.alloc_sbuf_tensor("sb_scdbg", [128, 128], F32)
    sb_row = nc.alloc_sbuf_tensor("sb_row", [1, DM], F32)
    sb_row2 = nc.alloc_sbuf_tensor("sb_row2", [1, 512], F32)
    sb_mrow = nc.alloc_sbuf_tensor("sb_mrow", [1, 256], F32)
    sb_arow = nc.alloc_sbuf_tensor("sb_arow", [1, 256], F32)
    sb_drow = nc.alloc_sbuf_tensor("sb_drow", [1, 256], F32)
    sb_ones1 = nc.alloc_sbuf_tensor("sb_ones1", [1, 128], F32)
    sb_dsrc = nc.alloc_sbuf_tensor("sb_dsrc", [1, 16], mybir.dt.int32)
    sb_stat = nc.alloc_sbuf_tensor("sb_stat", [128, 24], F32)

    pA = nc.alloc_psum_tensor("pA", [128, 512], F32)
    pB = nc.alloc_psum_tensor("pB", [128, 512], F32)
    pD = nc.alloc_psum_tensor("pD", [128, 2048], F32)
    pT = nc.alloc_psum_tensor("pT", [128, 1024], BF16)
    pE = nc.alloc_psum_tensor("pE", [128, 512], F32)

    sX = Sem(nc, "sX")
    sG = Sem(nc, "sG")
    sT = Sem(nc, "sT")
    sV = Sem(nc, "sV")
    sA = Sem(nc, "sA")
    sO = Sem(nc, "sO")
    sP = Sem(nc, "sP")
    sW = Sem(nc, "sW")
    sXT = Sem(nc, "sXT")
    sQS = Sem(nc, "sQS")
    sAB = Sem(nc, "sAB")
    sZL = Sem(nc, "sZL")
    sFL = Sem(nc, "sFL")
    sGA = Sem(nc, "sGA")
    sGB = Sem(nc, "sGB")
    sAC = Sem(nc, "sAC")
    sAB2 = Sem(nc, "sAB2")

    ctx = dict(locals())
    _l1_p0(ctx)
    if dbg == "p0":
        ctx["own_tail"] = dict(v=(sV, sV.n), s=(sA, sA.n), t=(sT, sT.n), o=(sO, sO.n))
        _l1_dbg_tail(ctx, ["d_chi"])
        return nc, ctx
    _l1_p2a(ctx)
    if dbg == "p2a":
        ctx["own_tail"] = dict(v=(sV, sV.n), s=(sA, sA.n), t=(sT, sT.n), o=(sO, sO.n))
        _l1_dbg_tail(ctx, ["d_chi", "d_kfm", "d_vst"])
        return nc, ctx
    _l1_own(ctx)
    _l1_p2a_diag(ctx)
    if dbg == "own":
        _l1_dbg_tail(ctx, ["d_xn", "d_z", "d_q", "d_amask", "d_kfm", "d_vst", "d_sc", "d_chi"])
        return nc, ctx
    _l1_attn(ctx)
    if dbg == "attn":
        _l1_dbg_tail(ctx, ["d_attn"])
        return nc, ctx
    _l1_wo(ctx)
    _l1_moe(ctx)
    if dbg == "moe":
        _l1_dbg_tail(ctx, ["d_y", "d_acc", "d_sc", "d_attn"])
        return nc, ctx
    _l1_block1(ctx)
    _l1_ffn1(ctx)
    _l1_tail(ctx)
    return nc, ctx



class Prog:

    def __init__(self):
        self.ops = []

    def add(self, eng_name, emit, waits=(), sem=None, inc=1):
        if sem is not None:
            sem.n += inc
            tok = (sem, sem.n)
        else:
            tok = None
        self.ops.append((eng_name, sem, inc, emit, [w for w in waits if w], tok))
        return tok

    def run(self, nc, engines=("sync", "gpsimd", "tensor", "vector", "scalar"),
            serialize=False):
        if serialize:
            prev = None
            for (en, sem, inc, emit, waits, tok) in self.ops:
                if prev is not None:
                    waits.append(prev)
                if tok is not None:
                    prev = tok
        with nc.Block() as blk:
            for name in engines:
                def body(eng, name=name):
                    for (en, sem, inc, emit, waits, tok) in self.ops:
                        if en != name:
                            continue
                        for (ws, wv) in waits:
                            eng.wait_ge(ws.h, wv)
                        instr = emit(eng)
                        if sem is not None:
                            instr.then_inc(sem.h, inc)
                getattr(blk, name)(body)


A32 = dict(
    cos=0, sin=2048, x3=4096, xnf=7168, bq=8192, bkv=9216, kvb=9728,
    cown=10240, sown=10752, tmpa=11264, tmpb=11296,
    m_pl=0, m_dA=256, m_dB=512, tmpS0=768, tmpS1=1024, recb=1312,
    yst=0, x3e=4096, wotmp=7168,
    acc=0, abc=16384, abc2=16640,
    ysl=0, ysl2=2048, qnb=4096, knb=5120, bc1=6144, bc2=6656, sttmp=16896,
    facc=0, ftmp=4096, ftmp2=4608, ftmp3=5120,
)
H16 = dict(stage0=0, stage1=1024, qro=2048, kro=3072, chi=3328, clo=3392,
           qslot=0, kscr=0, vaug=1024, qscr=2176,
           hbuf=0, rt=2048, rm=2304,
           w3s=0, w3s2=1024)


def _l1_p0(c):
    a16, a32 = c["a16"], c["a32"]
    sX, sG, sT, sV, sA = c["sX"], c["sG"], c["sT"], c["sV"], c["sA"]
    H = A16_H
    p = Prog()

    loads = [
        (c["sb_id16"][:], "id16"), (c["sb_idf"][:], "idf"),
        (c["sb_maskA"][:], "maskA"), (c["sb_maskB"][:], "maskB"),
        (c["sb_flag"][:], "flagbias"), (c["sb_b1c"][:], "b1_c"),
        (c["sb_b2c"][:], "b2_c"), (c["sb_gmc"][:], "gm_c"),
        (c["sb_btc"][:], "bt_c"), (c["sb_boc"][:], "bo_c"),
        (c["sb_s1c"][:], "s1_c"), (c["sb_s2c"][:], "s2_c"),
        (c["sb_hs1c"][:], "hs1_c"), (c["sb_hs2c"][:], "hs2_c"),
        (c["sb_s3c"][:], "s3_c"), (c["sb_dsrc"][:], "diag_src"),
        (a32.view(A32["cos"], 2048), "cos_tm"),
        (a32.view(A32["sin"], 2048), "sin_tm"),
        (a32.view(A32["cown"], 512), "cos_own"),
        (a32.view(A32["sown"], 512), "sin_own"),
        (c["sb_row"][:], "bq_r"), (c["sb_row2"][:], "bkv_r"),
        (a32.view(11392, 512), "cos_diag"),
        (a32.view(11904, 512), "sin_diag"),
    ]
    for dst, nm in loads:
        p.add("sync", lambda e, d=dst, n=nm: e.dma_start(d, c[n][:]), sem=sX, inc=16)
    p.add("sync", lambda e: e.dma_start(
        a32.view(A32["x3"], 64).rearrange("p (c e) -> p c e", c=8),
        c["centT"][:].rearrange("(c p) e -> p c e", p=128)), sem=sX, inc=16)
    ldbar = (sX, sX.n)

    p.add("gpsimd", lambda e: e.dma_start(
        a16.view(A16_C, 8 * DM).rearrange("p (c o) -> p c o", c=8),
        c["wq"][:].rearrange("(c p) o -> p c o", p=128)), sem=sG, inc=16)
    p.add("gpsimd", lambda e: e.dma_start(
        a16.view(A16_C + 12288, 8 * 512).rearrange("p (c o) -> p c o", c=8),
        c["wkv"][:].rearrange("(c p) o -> p c o", p=128)), sem=sG, inc=16)
    p.add("gpsimd", lambda e: e.dma_start(
        a16.view(H + H16["chi"], 64).rearrange("p (c e) -> p c e", c=8),
        c["centT"][:].rearrange("(c p) e -> p c e", p=128)), sem=sG, inc=16)
    gbar = (sG, sG.n)
    tvone = p.add("gpsimd", lambda e: e.memset(
        a16.view(A16_A, 64 * 260).rearrange("p (b e) -> p b e", e=65)[:, :, 64:65], 1.0),
        sem=c["sP"])
    tones = p.add("gpsimd", lambda e: e.memset(c["sb_ones1"][:], 1.0), sem=c["sP"])
    p.add("vector", lambda e: (
        e.memset(c["sb_stat"][:, 18:19], 1e-5),
        e.memset(c["sb_stat"][:, 19:20], 1e-6))[-1], sem=c["sV"])

    tclo = p.add("vector", lambda e: e.tensor_sub(
        a16.view(H + H16["clo"], 64), a32.view(A32["x3"], 64), a16.view(H + H16["chi"], 64)),
        waits=[ldbar, gbar], sem=sV)
    tbq = p.add("scalar", lambda e: e.mul(c["sb_row"][:], c["sb_row"][:], 0.125),
                waits=[ldbar], sem=sA)
    tmm = p.add("tensor", lambda e: (
        e.matmul(c["pA"][:], c["sb_ones1"][:], c["sb_row"][0:1, 0:512], start=True, stop=True),
        e.matmul(c["pB"][:], c["sb_ones1"][:], c["sb_row"][0:1, 512:1024], start=True, stop=True))[-1],
        waits=[tbq, tones], sem=sT)
    tcp = p.add("scalar", lambda e: (
        e.copy(a32.view(A32["bq"], 512), c["pA"][:]),
        e.copy(a32.view(A32["bq"] + 512, 512), c["pB"][:]))[-1],
        waits=[tmm], sem=sA)
    tmm2 = p.add("tensor", lambda e: e.matmul(
        c["pA"][:], c["sb_ones1"][:], c["sb_row2"][:], start=True, stop=True),
        waits=[ldbar, tones, tcp], sem=sT)
    tbkv = p.add("scalar", lambda e: e.copy(a32.view(A32["bkv"], 512), c["pA"][:]),
                 waits=[tmm2], sem=sA)
    sW = c["sW"]
    for i in range(16):
        p.add("gpsimd", lambda e, i=i: e.dma_start(
            c["h1b"][i * 256:(i + 1) * 256, :], c["l1h1"][i * 256:(i + 1) * 256, :]),
            sem=sW, inc=16)
    for i in range(16):
        p.add("gpsimd", lambda e, i=i: e.dma_start(
            c["h2b"][i * 256:(i + 1) * 256, :], c["l1h2"][i * 256:(i + 1) * 256, :]),
            sem=sW, inc=16)
    for i in range(4):
        p.add("gpsimd", lambda e, i=i: e.dma_start(
            c["w1b"][i * 256:(i + 1) * 256, :], c["l1w1"][i * 256:(i + 1) * 256, :]),
            sem=sW, inc=16)
        p.add("gpsimd", lambda e, i=i: e.dma_start(
            c["w2b"][i * 256:(i + 1) * 256, :], c["l1w2"][i * 256:(i + 1) * 256, :]),
            sem=sW, inc=16)
    for i in range(4):
        p.add("gpsimd", lambda e, i=i: e.dma_start(
            c["w3b"][i * 1024:(i + 1) * 1024, :], c["l1w3"][i * 1024:(i + 1) * 1024, :]),
            sem=sW, inc=16)
    c["tok_conv"] = (sW, sW.n)
    p.run(c["nc"])
    c["tok_wq"] = gbar
    c["tok_wkv"] = gbar
    c["tok_vones"] = tvone
    c["tok_bkv"] = tbkv
    c["tok_bq"] = tcp
    c["tok_clo"] = tclo
    c["tok_ld"] = ldbar


def _stats_chain(p, c, x_ap, sq_ap, cb, wx, prev_read_tok, eps_col=18):
    st = c["sb_stat"]
    sV, sA = c["sV"], c["sA"]
    t_red = p.add("vector", lambda e: e.reduce_sum(st[:, cb:cb + 1], x_ap, axis=AX.X),
                  waits=[wx], sem=sV)
    t_mu = p.add("vector", lambda e: e.tensor_scalar_mul(
        st[:, cb + 1:cb + 2], st[:, cb:cb + 1], 1.0 / 1024.0), waits=[t_red], sem=sV)
    t_sq = p.add("scalar", lambda e: e.activation(
        sq_ap, x_ap, AF.Square, accum_out=st[:, cb + 2:cb + 3]),
        waits=[wx, prev_read_tok], sem=sA)
    t_va = p.add("vector", lambda e: (
        e.tensor_scalar_mul(st[:, cb + 3:cb + 4], st[:, cb + 2:cb + 3], 1.0 / 1024.0),
        e.tensor_tensor(st[:, 20:21], st[:, cb + 1:cb + 2], st[:, cb + 1:cb + 2],
                        ALU.mult))[-1],
        waits=[t_sq, t_mu], sem=sV)
    t_var = p.add("vector", lambda e: e.tensor_sub(
        st[:, cb + 4:cb + 5], st[:, cb + 3:cb + 4], st[:, 20:21]),
        waits=[t_va], sem=sV)
    t_sd = p.add("scalar", lambda e: e.activation(
        st[:, cb + 5:cb + 6], st[:, cb + 4:cb + 5], AF.Sqrt,
        bias=st[:, eps_col:eps_col + 1]), waits=[t_var], sem=sA)
    t_rc = p.add("vector", lambda e: e.reciprocal(st[:, cb + 6:cb + 7], st[:, cb + 5:cb + 6]),
                 waits=[t_sd], sem=sV)
    t_rstd = p.add("vector", lambda e: e.tensor_scalar(
        st[:, cb + 7:cb + 8], st[:, cb + 1:cb + 2], st[:, cb + 6:cb + 7],
        -1.0, ALU.mult, ALU.mult), waits=[t_rc], sem=sV)
    return t_sq, t_var, t_rstd


def _rope_emit(p, c, src_view, dst_view, nheads, cos_ap, sin_ap, wfirst, extra_waits=()):
    a32 = c["a32"]
    sV = c["sV"]
    last = wfirst
    toks = []
    for h in range(nheads):
        ta1 = a32.view(A32["tmpa"], 32)
        tb1 = a32.view(A32["tmpb"], 32)
        ta2 = a32.view(A32["tmpa"] + 64, 32)
        tb2 = a32.view(A32["tmpb"] + 64, 32)
        av = src_view[:, h * 64:h * 64 + 32]
        bv = src_view[:, h * 64 + 32:h * 64 + 64]
        g1 = p.add("vector", lambda e, av=av, bv=bv, ta1=ta1, tb1=tb1, ta2=ta2, tb2=tb2: (
            e.tensor_tensor(ta1, av, cos_ap, ALU.mult),
            e.tensor_tensor(tb1, bv, sin_ap, ALU.mult),
            e.tensor_tensor(ta2, bv, cos_ap, ALU.mult),
            e.tensor_tensor(tb2, av, sin_ap, ALU.mult))[-1],
            waits=[last] + list(extra_waits if h == 0 else []), sem=sV)
        g2 = p.add("vector", lambda e, h=h, ta1=ta1, tb1=tb1, ta2=ta2, tb2=tb2: (
            e.tensor_sub(dst_view[:, h * 64:h * 64 + 32], ta1, tb1),
            e.tensor_add(dst_view[:, h * 64 + 32:h * 64 + 64], ta2, tb2))[-1],
            waits=[g1], sem=sV)
        last = g2
        toks.append(g2)
    return toks[-1]


def _l1_p2a(c):
    a16, a32 = c["a16"], c["a32"]
    sX, sT, sV, sA = c["sX"], c["sT"], c["sV"], c["sA"]
    pA, pT = c["pA"], c["pT"]
    H = A16_H
    st = c["sb_stat"]
    x3o = [A32["x3"] + j * 1024 for j in range(3)]
    stage0 = a16.view(H + H16["stage0"], 1024)
    stage1 = a16.view(H + H16["stage1"], 1024)
    kro = a16.view(H + H16["kro"], 256)
    wkv_sb = lambda ic: a16.view(A16_C + 12288 + ic * 512, 512)

    p = Prog()
    xn_tok = [None] * NT_PRE
    tT_tok = [None] * NT_PRE
    ev_tok = [None] * NT_PRE
    kv_tok = [None] * NT_PRE
    bias_tok = [None] * NT_PRE
    vcp_tok = [None] * NT_PRE
    kT_tok = [None] * NT_PRE
    kev_tok = [None] * NT_PRE
    var_tok = [None] * NT_PRE
    prev = lambda lst, i: lst[i - 1] if i >= 1 else None

    for i in range(NT_PRE):
        xv = a32.view(x3o[i % 3], 1024)
        t_x = p.add("sync", lambda e, i=i, xv=xv: e.dma_start(
            xv, c["x_batch"][i * 128:(i + 1) * 128, :]),
            waits=[xn_tok[i - 3] if i >= 3 else c["tok_clo"]], sem=c["sXT"], inc=16)
        sq_tok, var_tok[i], t_rstd = _stats_chain(
            p, c, xv, a32.view(A32["xnf"], 1024), 0, t_x, prev(var_tok, i))
        xn_tok[i] = p.add("vector", lambda e, xv=xv: e.tensor_scalar(
            stage0, xv, st[:, 6:7], st[:, 7:8], ALU.mult, ALU.add),
            waits=[t_rstd, prev(tT_tok, i)], sem=sV)
        tT_tok[i] = p.add("tensor", lambda e: [
            e.transpose(pT[:, ic * 128:(ic + 1) * 128],
                        stage0[:, ic * 128:(ic + 1) * 128], c["sb_id16"][:])
            for ic in range(8)][-1],
            waits=[xn_tok[i], prev(ev_tok, i), prev(kev_tok, i)], sem=sT)
        ev_tok[i] = p.add("vector", lambda e: [
            e.tensor_copy(stage1[:, ic * 128:(ic + 1) * 128], pT[:, ic * 128:(ic + 1) * 128])
            for ic in range(8)][-1],
            waits=[tT_tok[i], prev(kv_tok, i)], sem=sV)
        kv_tok[i] = p.add("tensor", lambda e: [
            e.matmul(pA[:], stage1[:, ic * 128:(ic + 1) * 128], wkv_sb(ic),
                     start=(ic == 0), stop=(ic == 7)) for ic in range(8)][-1],
            waits=[ev_tok[i], prev(bias_tok, i), c["tok_wkv"] if i == 0 else None],
            sem=sT)
        bias_tok[i] = p.add("vector", lambda e: e.tensor_tensor(
            a32.view(A32["kvb"], 512), pA[:], a32.view(A32["bkv"], 512), ALU.add),
            waits=[kv_tok[i], prev(vcp_tok, i),
                   c["tok_bkv"] if i == 0 else None], sem=sV)
        rope_tok = _rope_emit(
            p, c, a32.view(A32["kvb"], 512), kro, 4,
            a32.view(A32["cos"] + i * 32, 32), a32.view(A32["sin"] + i * 32, 32),
            bias_tok[i], extra_waits=[prev(kT_tok, i)])
        vcp_tok[i] = p.add("scalar", lambda e, i=i: e.copy(
            a16.view(A16_A + i * 260, 260).rearrange("p (h x) -> p h x", h=4)[:, :, 0:64],
            a32.view(A32["kvb"] + 256, 256).rearrange("p (h x) -> p h x", h=4)),
            waits=[bias_tok[i], c["tok_vones"] if i == 0 else None], sem=sA)
        kT_tok[i] = p.add("tensor", lambda e: (
            e.transpose(pT[:, 0:128], kro[:, 0:128], c["sb_id16"][:]),
            e.transpose(pT[:, 128:256], kro[:, 128:256], c["sb_id16"][:]))[-1],
            waits=[rope_tok], sem=sT)
        kev_tok[i] = p.add("vector", lambda e, i=i: (
            e.tensor_copy(a16.view(A16_B + i * 128, 128), pT[:, 0:128]),
            e.tensor_copy(a16.view(A16_B + S_GLOB + i * 128, 128), pT[:, 128:256]))[-1],
            waits=[kT_tok[i]], sem=sV)
    p.run(c["nc"])
    c["p2a_last_v"] = (sV, sV.n)
    c["p2a_last_s"] = (sA, sA.n)


def _l1_own(c):
    a16, a32 = c["a16"], c["a32"]
    sX, sT, sV, sA, sO = c["sX"], c["sT"], c["sV"], c["sA"], c["sO"]
    pA, pT, pD, pE = c["pA"], c["pT"], c["pD"], c["pE"]
    H = A16_H
    st = c["sb_stat"]
    x3o = [A32["x3"] + j * 1024 for j in range(3)]
    stage0 = a16.view(H + H16["stage0"], 1024)
    stage1 = a16.view(H + H16["stage1"], 1024)
    qro = a16.view(H + H16["qro"], 1024)
    chi = a16.view(H + H16["chi"], 64)
    clo = a16.view(H + H16["clo"], 64)
    wq_sb = lambda ic: a16.view(A16_C + ic * DM, DM)
    xnf = a32.view(A32["xnf"], 1024)
    scp = pE[:, 256:264]

    p = Prog()
    T = NT_OWN
    xn_tok = [None] * T; tT_tok = [None] * T; ev_tok = [None] * T
    sc_tok = [None] * T; am_tok = [None] * T; zT_tok = [None] * T; scp_tok = [None] * T
    zev_tok = [None] * T; zdma_tok = [None] * T; q_tok = [None] * T
    qb_tok = [None] * T; qro_tok = [None] * T; qT_tok = [None] * T
    qev_tok = [None] * T; qdma_tok = [None] * T; xndma_tok = [None] * T
    loT_tok = [None] * T; loev_tok = [None] * T; sq2_tok = [None] * T
    var_tok = [None] * T; var2_tok = [None] * T; z_tok = [None] * T
    xlo_tok = [None] * T
    prev = lambda lst, i: lst[i - 1] if i >= 1 else None

    p2a_v = c["p2a_last_v"]
    p2a_s = c["p2a_last_s"]

    for i in range(T):
        xv = a32.view(x3o[i % 3], 1024)
        t_x = p.add("sync", lambda e, i=i, xv=xv: e.dma_start(
            xv, c["x_own"][i * 128:(i + 1) * 128, :]),
            waits=[sq2_tok[i - 3] if i >= 3 else p2a_s], sem=c["sXT"], inc=16)
        t_sq, var_tok[i], t_rstd = _stats_chain(
            p, c, xv, xnf, 0, t_x, prev(var_tok, i) or p2a_v)
        xn_tok[i] = p.add("vector", lambda e, xv=xv: (
            e.tensor_scalar(stage0, xv, st[:, 6:7], st[:, 7:8], ALU.mult, ALU.add),
            e.tensor_scalar(xnf, xv, st[:, 6:7], st[:, 7:8], ALU.mult, ALU.add))[-1],
            waits=[t_rstd, prev(loT_tok, i) or p2a_v], sem=sV)
        tT_tok[i] = p.add("tensor", lambda e: [
            e.transpose(pT[:, ic * 128:(ic + 1) * 128],
                        stage0[:, ic * 128:(ic + 1) * 128], c["sb_id16"][:])
            for ic in range(8)][-1],
            waits=[xn_tok[i], prev(ev_tok, i) or p2a_v], sem=sT)
        ev_tok[i] = p.add("vector", lambda e: [
            e.tensor_copy(stage1[:, ic * 128:(ic + 1) * 128], pT[:, ic * 128:(ic + 1) * 128])
            for ic in range(8)][-1],
            waits=[tT_tok[i], prev(q_tok, i), xndma_tok[i - 1] if i >= 1 else None],
            sem=sV)
        xndma_tok[i] = p.add("sync", lambda e, i=i: e.dma_start(
            c["xn_d"][:].rearrange("p (x t) -> p x t", x=8)[:, :, i * 128:(i + 1) * 128],
            stage1[:].rearrange("p (x t) -> p x t", x=8)),
            waits=[ev_tok[i]], sem=sO, inc=16)
        xlo_tok[i] = p.add("vector", lambda e: e.tensor_sub(qro, xnf, stage0),
                           waits=[xn_tok[i], zdma_tok[i - 1] if i >= 1 else None],
                           sem=sV)
        loT_tok[i] = p.add("tensor", lambda e: [
            e.transpose(pT[:, ic * 128:(ic + 1) * 128],
                        qro[:, ic * 128:(ic + 1) * 128], c["sb_id16"][:])
            for ic in range(8)][-1],
            waits=[xlo_tok[i], ev_tok[i]], sem=sT)
        loev_tok[i] = p.add("vector", lambda e: [
            e.tensor_copy(stage0[:, ic * 128:(ic + 1) * 128], pT[:, ic * 128:(ic + 1) * 128])
            for ic in range(8)][-1],
            waits=[loT_tok[i]], sem=sV)
        def sc_mms(e, i=i):
            last = None
            for n, (stat, mov) in enumerate([(stage1, chi), (stage0, chi), (stage1, clo)]):
                for ic in range(8):
                    last = e.matmul(pE[:, 256 + n * 8:264 + n * 8],
                                    stat[:, ic * 128:(ic + 1) * 128],
                                    mov[:, ic * 8:(ic + 1) * 8],
                                    start=(ic == 0), stop=(ic == 7))
            return last
        sc_tok[i] = p.add("tensor", sc_mms,
                          waits=[loev_tok[i], prev(scp_tok, i),
                                 c["tok_clo"] if i == 0 else None], sem=sT)
        scsb = c["sb_scdbg"][:, i * 8:(i + 1) * 8]
        t_sc0 = p.add("scalar", lambda e, scsb=scsb: e.copy(scsb, pE[:, 256:264]),
                      waits=[sc_tok[i]], sem=sA)
        t_sc1 = p.add("vector", lambda e, scsb=scsb: e.tensor_add(scsb, scsb, pE[:, 264:272]),
                      waits=[t_sc0], sem=sV)
        scp_tok[i] = p.add("vector", lambda e, scsb=scsb: e.tensor_add(scsb, scsb, pE[:, 272:280]),
                          waits=[t_sc1], sem=sV)
        t_am0 = p.add("vector", lambda e, scsb=scsb: e.reduce_max(st[:, 16:17], scsb, axis=AX.X),
                      waits=[scp_tok[i], prev(am_tok, i)], sem=sV)
        t_am0b = p.add("vector", lambda e: e.tensor_scalar_add(
            st[:, 21:22], st[:, 16:17], -1e-4), waits=[t_am0], sem=sV)
        t_am1 = p.add("vector", lambda e, i=i, scsb=scsb: e.tensor_scalar(
            c["sb_amall"][:, i * 8:(i + 1) * 8], scsb, st[:, 21:22], None, ALU.is_ge),
            waits=[t_am0b], sem=sV)
        t_alpha = p.add("scalar", lambda e: e.activation(
            st[:, 17:18], st[:, 16:17], AF.Sigmoid), waits=[t_am1], sem=sA)
        am_tok[i] = p.add("vector", lambda e, i=i: e.tensor_scalar(
            c["sb_amall"][:, i * 8:(i + 1) * 8], c["sb_amall"][:, i * 8:(i + 1) * 8],
            st[:, 17:18], None, ALU.mult), waits=[t_alpha], sem=sV)
        sq2_tok[i], var2_tok[i], t_rstd2 = _stats_chain(
            p, c, xnf, xv, 8, xn_tok[i], prev(var2_tok, i) or p2a_v)
        z_tok[i] = p.add("vector", lambda e: e.tensor_scalar(
            stage0, xnf, st[:, 14:15], st[:, 15:16], ALU.mult, ALU.add),
            waits=[t_rstd2, loT_tok[i], am_tok[i]], sem=sV)
        zT_tok[i] = p.add("tensor", lambda e: [
            e.transpose(pT[:, ic * 128:(ic + 1) * 128],
                        stage0[:, ic * 128:(ic + 1) * 128], c["sb_id16"][:])
            for ic in range(8)][-1],
            waits=[z_tok[i], loev_tok[i], sc_tok[i]], sem=sT)
        zev_tok[i] = p.add("vector", lambda e: [
            e.tensor_copy(qro[:, ic * 128:(ic + 1) * 128], pT[:, ic * 128:(ic + 1) * 128])
            for ic in range(8)][-1],
            waits=[zT_tok[i], loT_tok[i]], sem=sV)
        zdma_tok[i] = p.add("sync", lambda e, i=i: e.dma_start(
            c["z_d"][:].rearrange("p (x t) -> p x t", x=8)[:, :, i * 128:(i + 1) * 128],
            qro[:].rearrange("p (x t) -> p x t", x=8)),
            waits=[zev_tok[i]], sem=sO, inc=16)
        q_tok[i] = p.add("tensor", lambda e: [
            e.matmul(pD[:, oh * 512:(oh + 1) * 512], stage1[:, ic * 128:(ic + 1) * 128],
                     wq_sb(ic)[:, oh * 512:(oh + 1) * 512],
                     start=(ic == 0), stop=(ic == 7))
            for oh in range(2) for ic in range(8)][-1],
            waits=[ev_tok[i], qb_tok[i - 1] if i >= 1 else None,
                   c["tok_wq"] if i == 0 else None], sem=sT)
        qb_tok[i] = p.add("vector", lambda e: [
            e.scalar_tensor_tensor(xnf[:, oh * 512:(oh + 1) * 512],
                                   pD[:, oh * 512:(oh + 1) * 512], 0.125,
                                   a32.view(A32["bq"] + oh * 512, 512),
                                   ALU.mult, ALU.add) for oh in range(2)][-1],
            waits=[q_tok[i], sq2_tok[i], z_tok[i],
                   c["tok_bq"] if i == 0 else None], sem=sV)
        qro_tok[i] = _rope_emit(
            p, c, xnf, qro, 16,
            a32.view(A32["cown"] + i * 32, 32), a32.view(A32["sown"] + i * 32, 32),
            qb_tok[i], extra_waits=[z_tok[i], zdma_tok[i]])
        qT_tok[i] = p.add("tensor", lambda e: [
            e.transpose(pT[:, ic * 128:(ic + 1) * 128],
                        qro[:, ic * 128:(ic + 1) * 128], c["sb_id16"][:])
            for ic in range(8)][-1],
            waits=[qro_tok[i], zev_tok[i]], sem=sT)
        qev_tok[i] = p.add("vector", lambda e: [
            e.tensor_copy(stage1[:, ic * 128:(ic + 1) * 128], pT[:, ic * 128:(ic + 1) * 128])
            for ic in range(8)][-1],
            waits=[qT_tok[i], xndma_tok[i], q_tok[i]], sem=sV)
        qdma_tok[i] = p.add("sync", lambda e, i=i: e.dma_start(
            c["q_d"][:].rearrange("p (x t) -> p x t", x=8)[:, :, i * 128:(i + 1) * 128],
            stage1[:].rearrange("p (x t) -> p x t", x=8)),
            waits=[qev_tok[i]], sem=sO, inc=16)
    p.run(c["nc"])
    c["own_tail"] = dict(qdma=qdma_tok[-1], am=am_tok[-1], v=(sV, sV.n),
                         s=(sA, sA.n), t=(sT, sT.n), o=(sO, sO.n))


def _l1_dbg_tail(c, names):
    sO, sX = c["sO"], c["sX"]
    a16 = c["a16"]
    p = Prog()
    w = [c["own_tail"]["v"], c["own_tail"]["s"], c["own_tail"]["t"],
         c["own_tail"]["o"]] if "own_tail" in c else []
    if "moe_tail" in c:
        w += [c["moe_tail"]]
    for k in ("attn_tail", "attn_tail_v", "attn_tail_t", "attn_tail_s"):
        if k in c:
            w += [c[k]]
    srcs = {
        "d_xn": lambda: c["xn_d"][:], "d_z": lambda: c["z_d"][:],
        "d_q": lambda: c["q_d"][:], "d_amask": lambda: c["sb_amall"][:],
        "d_kfm": lambda: a16.view(A16_B, 2 * S_GLOB),
        "d_vst": lambda: a16.view(A16_A, NT_PRE * 260),
        "d_sc": lambda: c["sb_scdbg"][:],
        "d_chi": lambda: a16.view(A16_H + H16["chi"], 128),
        "d_attn": lambda: c["attn_d"][:], "d_y": lambda: c["y_d"][:],
        "d_acc": lambda: c["a32"].view(A32["acc"], 16384),
    }
    for nm in names:
        p.add("sync", lambda e, nm=nm: e.dma_start(c["dbg_outs"][nm][:], srcs[nm]()),
              waits=w, sem=sO, inc=16)
    for nm in ("ypart", "ffn1"):
        p.add("gpsimd", lambda e, nm=nm: e.dma_start(c[nm][:, 0:128], c["xn_d"][:, 0:128]),
              waits=w, sem=sO, inc=16)
    p.add("sync", lambda e: e.dma_start(c["qp_o"][:, 0:128], c["xn_d"][:, 0:128]),
          waits=w, sem=sO, inc=16)
    p.add("gpsimd", lambda e: e.dma_start(c["kvpart"][0:64, 0:128], c["xn_d"][0:64, 0:128]),
          waits=w, sem=sO, inc=16)
    fin = (sO, sO.n)
    p.add("gpsimd", lambda e: e.wait_ge(fin[0].h, fin[1]), sem=None)
    p.run(c["nc"])


def _l1_attn(c):
    a16, a32 = c["a16"], c["a32"]
    sX, sG, sT, sV, sA, sO = c["sX"], c["sG"], c["sT"], c["sV"], c["sA"], c["sO"]
    pA, pB, pE = c["pA"], c["pB"], c["pE"]
    st = c["sb_stat"]
    H = A16_H
    KD = A16_C
    VD = A16_C + 4096
    AST = A16_C + 8256
    PT0 = A16_C + 10304
    PT1 = A16_C + 10560
    ot = c["diag_tail"] | {"o": c["own_tail"]["o"]}

    p = Prog()
    dt = c["diag_tail"]
    dg_tok = dt["v"]

    qs_dma = [None] * NSLOT
    qs_done = [None] * NSLOT
    stg_dma = [None] * NSLOT
    epi_tok = [None] * NSLOT
    last_recb = None
    last_t_out = None

    for s in range(NSLOT):
        qsl = H + (s % 2) * 2048
        R = _R(s)
        qs_dma[s] = p.add("sync", lambda e, s=s, qsl=qsl: e.dma_start(
            a16.t[:, qsl:qsl + 2048].rearrange("p (x t) -> p x t", x=8),
            c["q_d"][:].rearrange("p (x t) -> p x t", x=8)[:, :, s * 256:(s + 1) * 256]),
            waits=[ot["o"], qs_done[s - 2] if s >= 2 else None], sem=c["sQS"], inc=16)
        hd_toks = []
        for h in range(16):
            kvh = h // 4
            slot_h = INV[h]
            qrow = (slot_h % 2) * 64
            qcol = qsl + (slot_h // 2) * 256
            krow = (kvh % 2) * 64
            kbase = A16_B + (kvh // 2) * S_GLOB
            red_toks = []
            for qs2 in range(2):
                mcol = 16 + qs2
                for kb4 in range(R // 4):
                    pS = pA if kb4 % 2 == 0 else pB
                    t_mm = p.add("tensor", lambda e, pS=pS, qrow=qrow, qcol=qcol,
                                 qs2=qs2, krow=krow, kbase=kbase, kb4=kb4: e.matmul(
                        pS[:], a16.t[qrow:qrow + 64, qcol + qs2 * 128:qcol + qs2 * 128 + 128],
                        a16.t[krow:krow + 64, kbase + kb4 * 512:kbase + (kb4 + 1) * 512],
                        start=True, stop=True),
                        waits=[qs_dma[s],
                               red_toks[-2] if len(red_toks) >= 2 else last_t_out],
                        sem=sT)
                    if kb4 == 0:
                        t_red = p.add("vector", lambda e, pS=pS, mcol=mcol: e.reduce_max(
                            st[:, mcol:mcol + 1], pS[:], axis=AX.X),
                            waits=[t_mm], sem=sV)
                    else:
                        t_r0 = p.add("vector", lambda e, pS=pS: e.reduce_max(
                            st[:, 20:21], pS[:], axis=AX.X), waits=[t_mm], sem=sV)
                        t_red = p.add("vector", lambda e, mcol=mcol: e.tensor_max(
                            st[:, mcol:mcol + 1], st[:, mcol:mcol + 1], st[:, 20:21]),
                            waits=[t_r0], sem=sV)
                    red_toks.append(t_red)
            tT = p.add("tensor", lambda e: (
                e.transpose(pE[0:1, 256:384], st[:, 16:17], c["sb_idf"][:]),
                e.transpose(pE[0:1, 384:512], st[:, 17:18], c["sb_idf"][:]))[-1],
                waits=[red_toks[-1], last_recb], sem=sT)
            tmr = p.add("scalar", lambda e: (
                e.copy(c["sb_mrow"][0:1, 0:128], pE[0:1, 256:384]),
                e.copy(c["sb_mrow"][0:1, 128:256], pE[0:1, 384:512]))[-1],
                waits=[tT], sem=sA)
            tmb = p.add("tensor", lambda e: e.matmul(
                pE[:, 0:256], c["sb_ones1"][:], c["sb_mrow"][:], start=True, stop=True),
                waits=[tmr, last_t_out], sem=sT)
            tmp_cp = p.add("scalar", lambda e: e.copy(
                a32.view(A32["m_pl"], 256), pE[:, 0:256]), waits=[tmb], sem=sA)
            tdA = p.add("vector", lambda e: e.tensor_add(
                a32.view(A32["m_dA"], 256), a32.view(A32["m_pl"], 256), c["sb_maskA"][:]),
                waits=[tmp_cp], sem=sV)
            tdB = p.add("vector", lambda e: e.tensor_add(
                a32.view(A32["m_dB"], 256), a32.view(A32["m_pl"], 256), c["sb_maskB"][:]),
                waits=[tmp_cp], sem=sV)
            fb_base = sum(_R(s2) - 2 for s2 in range(s))
            av_toks = []
            exp_toks = [None, None]
            for j in range(R - 2 + 2):
                diag = j >= R - 2
                dd = j - (R - 2)
                par = j % 2
                pS = pA if par == 0 else pB
                ptile = PT0 if par == 0 else PT1
                tmps = A32["tmpS0"] if par == 0 else A32["tmpS1"]
                if not diag:
                    stat = (krow, kbase + j * 128)
                    mop = A32["m_pl"]
                else:
                    stat = (krow, KD + (kvh // 2) * 2048 + (s * 2 + dd) * 128)
                    mop = A32["m_dA"] if dd == 0 else A32["m_dB"]
                t_mm = p.add("tensor", lambda e, pS=pS, stat=stat, qrow=qrow, qcol=qcol: e.matmul(
                    pS[:, 0:256], a16.t[stat[0]:stat[0] + 64, stat[1]:stat[1] + 128],
                    a16.t[qrow:qrow + 64, qcol:qcol + 256], start=True, stop=True),
                    waits=[tmb if j == 0 else None,
                           red_toks[-1] if j < 2 else None,
                           exp_toks[par] if j >= 2 else None,
                           tdA if diag and dd == 0 else None,
                           tdB if diag and dd == 1 else None,
                           dg_tok if (s == 0 and h == 0 and diag) else None], sem=sT)
                t_sub = p.add("vector", lambda e, pS=pS, tmps=tmps, mop=mop: e.tensor_sub(
                    a32.view(tmps, 256), pS[:, 0:256], a32.view(mop, 256)),
                    waits=[t_mm], sem=sV)
                if not diag:
                    fb = c["sb_flag"][:, fb_base + j:fb_base + j + 1]
                    t_exp = p.add("scalar", lambda e, tmps=tmps, ptile=ptile, fb=fb:
                                  e.activation(a16.view(ptile, 256), a32.view(tmps, 256),
                                               AF.Exp, bias=fb),
                                  waits=[t_sub], sem=sA)
                else:
                    t_exp = p.add("scalar", lambda e, tmps=tmps, ptile=ptile:
                                  e.activation(a16.view(ptile, 256), a32.view(tmps, 256),
                                               AF.Exp),
                                  waits=[t_sub], sem=sA)
                exp_toks[par] = t_exp
                vstat = (VD + (s * 2 + dd) * 260 + 65 * kvh) if diag else (j * 260 + 65 * kvh)
                t_av = p.add("tensor", lambda e, vstat=vstat, ptile=ptile, j=j: e.matmul(
                    pE[0:65, 0:256], a16.t[:, vstat:vstat + 65], a16.view(ptile, 256),
                    start=(j == 0), stop=(j == R - 1)),
                    waits=[t_exp], sem=sT)
                av_toks.append(t_av)
            t_dr = p.add("scalar", lambda e: e.copy(c["sb_drow"][:], pE[64:65, 0:256]),
                         waits=[av_toks[-1]], sem=sA)
            t_rc = p.add("vector", lambda e: e.reciprocal(c["sb_drow"][:], c["sb_drow"][:]),
                         waits=[t_dr], sem=sV)
            t_db = p.add("tensor", lambda e: e.matmul(
                pE[0:64, 256:512], c["sb_ones1"][0:1, 0:64], c["sb_drow"][:],
                start=True, stop=True), waits=[t_rc, tmr], sem=sT)
            t_rb = p.add("scalar", lambda e: e.copy(
                a32.view(A32["recb"], 256)[0:64, :], pE[0:64, 256:512]),
                waits=[t_db], sem=sA)
            last_recb = t_rb
            t_out = p.add("vector", lambda e, qrow=qrow, slot_h=slot_h, s=s: e.tensor_tensor(
                a16.t[qrow:qrow + 64, AST + (slot_h // 2) * 256:AST + (slot_h // 2) * 256 + 256],
                pE[0:64, 0:256], a32.view(A32["recb"], 256)[0:64, :], ALU.mult),
                waits=[t_rb, stg_dma[s - 1] if (s >= 1 and h <= 1) else None], sem=sV)
            hd_toks.append(t_out)
            epi_tok[s] = t_out
            last_t_out = t_out
        qs_done[s] = av_toks[-1]
        stg_dma[s] = p.add("sync", lambda e, s=s: e.dma_start(
            c["attn_d"][:].rearrange("p (x t) -> p x t", x=8)[:, :, s * 256:(s + 1) * 256],
            a16.t[:, AST:AST + 2048].rearrange("p (x t) -> p x t", x=8)),
            waits=hd_toks, sem=sO, inc=16)
    p.run(c["nc"])
    c["attn_tail"] = stg_dma[-1]
    c["attn_tail_v"] = (sV, sV.n)
    c["attn_tail_t"] = (sT, sT.n)
    c["attn_tail_s"] = (sA, sA.n)


def _l1_p2a_diag(c):
    a16, a32 = c["a16"], c["a32"]
    sX, sT, sV, sA = c["sX"], c["sT"], c["sV"], c["sA"]
    pA, pT = c["pA"], c["pT"]
    H = A16_H
    st = c["sb_stat"]
    x3o = [A32["x3"] + j * 1024 for j in range(3)]
    stage0 = a16.view(H + H16["stage0"], 1024)
    stage1 = a16.view(H + H16["stage1"], 1024)
    kro = a16.view(H + H16["kro"], 256)
    wkv_sb = lambda ic: a16.view(A16_C + 12288 + ic * 512, 512)
    KD = A16_C
    VD = A16_C + 4096
    ot = c["own_tail"]

    p = Prog()
    N = 16
    xn_tok = [None] * N; tT_tok = [None] * N; ev_tok = [None] * N
    kv_tok = [None] * N; bias_tok = [None] * N; vcp_tok = [None] * N
    kT_tok = [None] * N; kev_tok = [None] * N; var_tok = [None] * N
    prev = lambda lst, i: lst[i - 1] if i >= 1 else None
    t_ones = p.add("vector", lambda e: e.memset(
        a16.view(VD, 16 * 260).rearrange("p (b x) -> p b x", x=65)[:, :, 64:65], 1.0),
        waits=[ot["t"], ot["v"], ot["s"]], sem=c["sV"])

    for i in range(N):
        xv = a32.view(x3o[i % 3], 1024)
        t_x = p.add("sync", lambda e, i=i, xv=xv: e.dma_start(
            xv, c["x_diag"][i * 128:(i + 1) * 128, :]),
            waits=[xn_tok[i - 3] if i >= 3 else ot["s"]], sem=c["sXT"], inc=16)
        sq_tok, var_tok[i], t_rstd = _stats_chain(
            p, c, xv, a32.view(A32["xnf"], 1024), 0, t_x, prev(var_tok, i) or ot["v"])
        xn_tok[i] = p.add("vector", lambda e, xv=xv: e.tensor_scalar(
            stage0, xv, st[:, 6:7], st[:, 7:8], ALU.mult, ALU.add),
            waits=[t_rstd, prev(tT_tok, i) or ot["t"]], sem=sV)
        tT_tok[i] = p.add("tensor", lambda e: [
            e.transpose(pT[:, ic * 128:(ic + 1) * 128],
                        stage0[:, ic * 128:(ic + 1) * 128], c["sb_id16"][:])
            for ic in range(8)][-1],
            waits=[xn_tok[i], prev(ev_tok, i), prev(kev_tok, i)], sem=sT)
        ev_tok[i] = p.add("vector", lambda e: [
            e.tensor_copy(stage1[:, ic * 128:(ic + 1) * 128], pT[:, ic * 128:(ic + 1) * 128])
            for ic in range(8)][-1],
            waits=[tT_tok[i], prev(kv_tok, i)], sem=sV)
        kv_tok[i] = p.add("tensor", lambda e: [
            e.matmul(pA[:], stage1[:, ic * 128:(ic + 1) * 128], wkv_sb(ic),
                     start=(ic == 0), stop=(ic == 7)) for ic in range(8)][-1],
            waits=[ev_tok[i], prev(bias_tok, i)], sem=sT)
        bias_tok[i] = p.add("vector", lambda e: e.tensor_tensor(
            a32.view(A32["kvb"], 512), pA[:], a32.view(A32["bkv"], 512), ALU.add),
            waits=[kv_tok[i], prev(vcp_tok, i)], sem=sV)
        rope_tok = _rope_emit(
            p, c, a32.view(A32["kvb"], 512), kro, 4,
            a32.view(11392 + i * 32, 32), a32.view(11904 + i * 32, 32),
            bias_tok[i], extra_waits=[prev(kT_tok, i)])
        vcp_tok[i] = p.add("scalar", lambda e, i=i: e.copy(
            a16.view(VD + i * 260, 260).rearrange("p (h x) -> p h x", h=4)[:, :, 0:64],
            a32.view(A32["kvb"] + 256, 256).rearrange("p (h x) -> p h x", h=4)),
            waits=[bias_tok[i], t_ones if i == 0 else None], sem=sA)
        kT_tok[i] = p.add("tensor", lambda e: (
            e.transpose(pT[:, 0:128], kro[:, 0:128], c["sb_id16"][:]),
            e.transpose(pT[:, 128:256], kro[:, 128:256], c["sb_id16"][:]))[-1],
            waits=[rope_tok], sem=sT)
        kev_tok[i] = p.add("vector", lambda e, i=i: (
            e.tensor_copy(a16.view(KD + i * 128, 128), pT[:, 0:128]),
            e.tensor_copy(a16.view(KD + 2048 + i * 128, 128), pT[:, 128:256]))[-1],
            waits=[kT_tok[i]], sem=sV)
    p.run(c["nc"])
    c["diag_tail"] = dict(v=(sV, sV.n), s=(sA, sA.n), t=(sT, sT.n))


def _l1_wo(c):
    a16, a32 = c["a16"], c["a32"]
    sX, sG, sT, sV, sA, sO = c["sX"], c["sG"], c["sT"], c["sV"], c["sA"], c["sO"]
    pA, pB, pD, pT = c["pA"], c["pB"], c["pD"], c["pT"]
    at = c["attn_tail"]
    WO = A16_C
    AB0 = A16_C + 8192
    AB1 = A16_C + 12288
    x3o = [A32["x3"] + j * 1024 for j in range(3)]
    p = Prog()

    two = p.add("gpsimd", lambda e: e.dma_start(
        a16.view(WO, 8 * DM).rearrange("p (c o) -> p c o", c=8),
        c["wo"][:].rearrange("(c p) o -> p c o", p=128)),
        waits=[c["attn_tail_t"]], sem=c["sW"], inc=16)
    ab_dma = [None] * 4
    ab_done = [None] * 4
    ystg_dma = [None] * 4
    xT_tok = [None, None, None, None]
    last_ev = c["attn_tail_v"]
    last_sq = c["attn_tail_s"]
    for tc in range(4):
        ab = AB0 if tc % 2 == 0 else AB1
        ab_dma[tc] = p.add("sync", lambda e, tc=tc, ab=ab: e.dma_start(
            a16.t[:, ab:ab + 4096].rearrange("p (x t) -> p x t", x=8),
            c["attn_d"][:].rearrange("p (x t) -> p x t", x=8)[:, :, tc * 512:(tc + 1) * 512]),
            waits=[at if tc == 0 else None, ab_done[tc - 2] if tc >= 2 else None],
            sem=c["sAB"] if tc % 2 == 0 else c["sAB2"], inc=16)
        ev_toks = []
        for t4 in range(4):
            i = tc * 4 + t4
            xv = a32.view(x3o[i % 3], 1024)
            t_x = p.add("sync", lambda e, i=i, xv=xv: e.dma_start(
                xv, c["x_own"][i * 128:(i + 1) * 128, :]),
                waits=[xT_tok[(i - 3) % 4] if i >= 3 else last_sq], sem=c["sXT"], inc=16)
            tT = p.add("tensor", lambda e, xv=xv: [
                e.transpose(pT[:, 0:1024].bitcast(F32)[:, ic * 128:(ic + 1) * 128]
                            if False else pD[:, ic * 128:(ic + 1) * 128],
                            xv[:, ic * 128:(ic + 1) * 128], c["sb_idf"][:])
                for ic in range(8)][-1],
                waits=[t_x, last_ev], sem=sT)
            xT_tok[i % 4] = tT
            t_ev = p.add("vector", lambda e, t4=t4: [
                e.tensor_copy(a32.view(A32["yst"] + ic * 512 + t4 * 128, 128),
                              pD[:, ic * 128:(ic + 1) * 128]) for ic in range(8)][-1],
                waits=[tT, ystg_dma[tc - 1] if (tc >= 1 and t4 == 0) else None], sem=sV)
            last_ev = t_ev
            ev_toks.append(t_ev)
        oc_toks = []
        last_comb = None
        for oc in range(8):
            pS = pD[:, 1024:1536] if oc % 2 == 0 else pD[:, 1536:2048]
            t_mm = p.add("tensor", lambda e, pS=pS, oc=oc, ab=ab: [
                e.matmul(pS, a16.t[:, WO + ic * DM + oc * 128:WO + ic * DM + oc * 128 + 128],
                         a16.t[:, ab + ic * 512:ab + (ic + 1) * 512],
                         start=(ic == 0), stop=(ic == 7)) for ic in range(8)][-1],
                waits=[ab_dma[tc], two if (tc == 0 and oc == 0) else None,
                       ev_toks[-1] if oc == 0 else None,
                       oc_toks[-1] if oc_toks else
                       (ab_done[tc - 1] if tc >= 1 else None)], sem=sT)
            t_comb = p.add("vector", lambda e, pS=pS, oc=oc: e.scalar_tensor_tensor(
                a32.view(A32["yst"] + oc * 512, 512), pS, c["sb_boc"][:, oc:oc + 1],
                a32.view(A32["yst"] + oc * 512, 512), ALU.add, ALU.add),
                waits=[t_mm, ev_toks[-1] if oc == 0 else None], sem=sV)
            oc_toks.append(t_comb)
            last_comb = t_comb
        ab_done[tc] = last_comb
        ystg_dma[tc] = p.add("sync", lambda e, tc=tc: e.dma_start(
            c["y_d"][:].rearrange("p (x t) -> p x t", x=8)[:, :, tc * 512:(tc + 1) * 512],
            a32.view(A32["yst"], 4096).rearrange("p (x t) -> p x t", x=8)),
            waits=oc_toks, sem=sO, inc=16)
    p.run(c["nc"], serialize=True)
    c["wo_tail"] = dict(o=ystg_dma[-1], v=(sV, sV.n), s=(sA, sA.n), t=(sT, sT.n))


def _l1_moe(c):
    a16, a32 = c["a16"], c["a32"]
    sX, sG, sT, sV, sA, sO, sP = (c["sX"], c["sG"], c["sT"], c["sV"], c["sA"],
                                  c["sO"], c["sP"])
    pA, pD, pE = c["pA"], c["pD"], c["pE"]
    H = A16_H
    wt = c["wo_tail"]
    p = Prog()

    t_z = p.add("sync", lambda e: e.dma_start(
        a16.view(A16_C, 16384), c["z_d"][:]), waits=[wt["t"], wt["v"]], sem=c["sZL"], inc=16)
    t_acc = p.add("gpsimd", lambda e: e.dma_start(
        a32.view(A32["acc"], 16384), c["xn_d"][:]),
        waits=[wt["o"], wt["v"]], sem=c["sAC"], inc=16)

    t_dbg0 = p.add("scalar", lambda e: e.copy(
        c["sb_scdbg"][:, 0:64], a32.view(A32["acc"], 64)), waits=[t_acc], sem=c["sA"])
    c["_dbg0"] = t_dbg0
    wbuf_done = [wt["t"], wt["t"]]
    wdma = [None, None]
    prev_evac = None
    prev_rm = None
    prev_rt = None
    prev_hb = None
    rm_war = [None] * 8
    prev_bc = None
    bc_cp = None
    first_bc = True
    first_h = True
    for e8 in range(8):
        for qq in range(4):
            par = (e8 * 4 + qq) % 2
            wb = A16_A if par == 0 else A16_B
            sGP = c["sGA"] if par == 0 else c["sGB"]
            p.add("gpsimd", lambda e, e8=e8, qq=qq, wb=wb: e.dma_start(
                a16.t[:, wb:wb + 8192].rearrange("p (c o) -> p c o", c=8),
                c["w1t"][e8][:, qq * 1024:(qq + 1) * 1024]
                .rearrange("(c p) o -> p c o", p=128)),
                waits=[wbuf_done[par]], sem=sGP, inc=16)
            wdma[par] = p.add("gpsimd", lambda e, e8=e8, qq=qq, wb=wb: e.dma_start(
                a16.t[:, wb + 8192:wb + 16384].rearrange("p (c o) -> p c o", c=8),
                c["w2t"][e8][qq * 1024:(qq + 1) * 1024, :]
                .rearrange("(c p) o -> p c o", p=128)),
                sem=sGP, inc=16)
            for tt in range(8):
                tt1 = p.add("tensor", lambda e, e8=e8, tt=tt: (
                    e.transpose(pE[0:1, 256:384],
                                c["sb_amall"][:, (2 * tt) * 8 + e8:(2 * tt) * 8 + e8 + 1],
                                c["sb_idf"][:]),
                    e.transpose(pE[0:1, 384:512],
                                c["sb_amall"][:, (2 * tt + 1) * 8 + e8:(2 * tt + 1) * 8 + e8 + 1],
                                c["sb_idf"][:]))[-1],
                    waits=[bc_cp, c["own_tail"]["v"] if first_bc else None,
                           wt["s"] if first_bc else None], sem=sT)
                tt2 = p.add("scalar", lambda e: (
                    e.copy(c["sb_arow"][0:1, 0:128], pE[0:1, 256:384]),
                    e.copy(c["sb_arow"][0:1, 128:256], pE[0:1, 384:512]))[-1],
                    waits=[tt1], sem=sA)
                tt3 = p.add("tensor", lambda e: e.matmul(
                    pE[:, 0:256], c["sb_ones1"][:], c["sb_arow"][:], start=True, stop=True),
                    waits=[tt2], sem=sT)
                bc_cp = p.add("scalar", lambda e: e.copy(
                    a32.view(A32["abc"], 256), pE[:, 0:256]),
                    waits=[tt3, prev_bc], sem=sA)
                first_bc = False
                t_h = p.add("vector", lambda e, e8=e8, tt=tt: [
                    e.tensor_scalar(a16.view(H + H16["hbuf"] + ic * 256, 256),
                                    a16.view(A16_C + ic * 2048 + tt * 256, 256),
                                    c["sb_gmc"][:, e8 * 8 + ic:e8 * 8 + ic + 1],
                                    c["sb_btc"][:, e8 * 8 + ic:e8 * 8 + ic + 1],
                                    ALU.mult, ALU.add) for ic in range(8)][-1],
                    waits=[t_z if first_h else None, prev_hb], sem=sV)
                first_h = False
                last_y1 = None
                last_y2 = None
                rt_v = a16.view(A16_B + 16384, 256)
                rm_all = lambda fc: a16.view(H + 2048 + fc * 256, 256)
                rm_toks = []
                for fc in range(8):
                    t_y1 = p.add("tensor", lambda e, wb=wb, fc=fc: [
                        e.matmul(pA[:, 0:256],
                                 a16.t[:, wb + ic * 1024 + fc * 128:wb + ic * 1024 + fc * 128 + 128],
                                 a16.view(H + H16["hbuf"] + ic * 256, 256),
                                 start=(ic == 0), stop=(ic == 7)) for ic in range(8)][-1],
                        waits=[t_h if fc == 0 else None,
                               wdma[par] if tt == 0 and fc == 0 else None,
                               prev_rt, prev_evac if fc == 0 else None], sem=sT)
                    last_y1 = t_y1
                    t_r = p.add("scalar", lambda e, e8=e8, qq=qq, fc=fc: e.activation(
                        rt_v, pA[:, 0:256], AF.Relu,
                        bias=c["sb_b1c"][:, e8 * 32 + qq * 8 + fc:e8 * 32 + qq * 8 + fc + 1]),
                        waits=[t_y1, prev_rm], sem=sA)
                    prev_rt = t_r
                    t_rm = p.add("vector", lambda e, fc=fc: e.tensor_tensor(
                        rm_all(fc), rt_v, a32.view(A32["abc"], 256), ALU.mult),
                        waits=[t_r, bc_cp if fc == 0 else None,
                               rm_war[fc]], sem=sV)
                    prev_rm = t_rm
                    rm_toks.append(t_rm)
                for oc in range(8):
                    t_y2 = p.add("tensor", lambda e, wb=wb, oc=oc: [
                        e.matmul(pD[:, oc * 256:(oc + 1) * 256],
                                 a16.t[:, wb + 8192 + fc * 1024 + oc * 128:wb + 8192 + fc * 1024 + oc * 128 + 128],
                                 rm_all(fc), start=(fc == 0), stop=(fc == 7))
                        for fc in range(8)][-1],
                        waits=[rm_toks[-1] if oc == 0 else None,
                               prev_evac if oc == 0 else None], sem=sT)
                    last_y2 = t_y2
                for fc in range(8):
                    rm_war[fc] = last_y2
                prev_hb = last_y1
                if e8 == 0 and qq == 0 and tt == 0:
                    p.add("scalar", lambda e: e.copy(
                        c["sb_scdbg"][:, 64:128], pD[:, 0:64]),
                        waits=[last_y2], sem=c["sA"])
                ev = None
                for oc in range(8):
                    if qq == 0:
                        t1 = p.add("vector", lambda e, oc=oc, e8=e8: e.scalar_tensor_tensor(
                            a32.view(A32["sttmp"], 256), a32.view(A32["abc"], 256),
                            c["sb_b2c"][:, e8 * 8 + oc:e8 * 8 + oc + 1],
                            pD[:, oc * 256:(oc + 1) * 256], ALU.mult, ALU.add),
                            waits=[last_y2, ev, bc_cp,
                                   t_acc if e8 == 0 and tt == 0 and oc == 0 else None,
                                   c["_dbg0"] if e8 == 0 and tt == 0 and oc == 0 else None,
                                   (c["sA"], c["sA"].n) if e8 == 0 and qq == 0 and tt == 0 and oc == 0 else None],
                            sem=sV)
                        ev = p.add("vector", lambda e, oc=oc, tt=tt: e.tensor_add(
                            a32.view(A32["acc"] + oc * 2048 + tt * 256, 256),
                            a32.view(A32["acc"] + oc * 2048 + tt * 256, 256),
                            a32.view(A32["sttmp"], 256)),
                            waits=[t1], sem=sV)
                    else:
                        ev = p.add("vector", lambda e, oc=oc, tt=tt: e.tensor_add(
                            a32.view(A32["acc"] + oc * 2048 + tt * 256, 256),
                            a32.view(A32["acc"] + oc * 2048 + tt * 256, 256),
                            pD[:, oc * 256:(oc + 1) * 256]),
                            waits=[last_y2, t_acc if e8 == 0 and tt == 0 and oc == 0 else None,
                                   ev], sem=sV)
                prev_evac = ev
                prev_bc = ev if qq == 0 else prev_rt
            wbuf_done[par] = prev_evac
    fold_dma = []
    last_fold = None
    for k in range(32):
        slot = 16384 if k % 2 == 0 else 16896
        t_l = p.add("sync", lambda e, k=k, slot=slot: e.dma_start(
            a32.view(slot, 512), c["y_d"][:, k * 512:(k + 1) * 512]),
            waits=[prev_evac if k == 0 else None,
                   fold_dma[k - 2] if k >= 2 else None], sem=c["sFL"], inc=16)
        t_a = p.add("vector", lambda e, k=k, slot=slot: e.tensor_add(
            a32.view(slot, 512), a32.view(slot, 512), a32.view(A32["acc"] + k * 512, 512)),
            waits=[t_l, last_fold], sem=sV)
        last_fold = t_a
        t_o = p.add("sync", lambda e, k=k, slot=slot: e.dma_start(
            c["ypart"][:, k * 512:(k + 1) * 512], a32.view(slot, 512)),
            waits=[t_a], sem=sO, inc=16)
        fold_dma.append(t_o)
    p.run(c["nc"])
    c["moe_tail"] = fold_dma[-1]
    c["moe_tail_v"] = (sV, sV.n)
    c["moe_tail_t"] = (sT, sT.n)
    c["moe_tail_s"] = (sA, sA.n)
    c["moe_tail_g"] = (sG, sG.n)



def _l1_block1(c):
    a16, a32 = c["a16"], c["a32"]
    sX, sT, sV, sA, sO = c["sX"], c["sT"], c["sV"], c["sA"], c["sO"]
    pA, pB, pD, pT = c["pA"], c["pB"], c["pD"], c["pT"]
    st = c["sb_stat"]
    H = A16_H
    mt = c["moe_tail"]
    p = Prog()
    w = lambda *t: [x for x in t if x]

    t0 = p.add("sync", lambda e: e.dma_start(c["sb_row"][:], c["qn_r"][:]),
               waits=[mt, c["moe_tail_v"], c["moe_tail_t"], c["moe_tail_s"]],
               sem=sX, inc=16)
    for half in range(2):
        tm = p.add("tensor", lambda e, half=half: e.matmul(
            pA[:], c["sb_ones1"][:], c["sb_row"][0:1, half * 512:(half + 1) * 512],
            start=True, stop=True), sem=sT)
        p.add("scalar", lambda e, half=half: e.copy(
            a32.view(A32["qnb"] + half * 512, 512), pA[:]), sem=sA)
    p.add("sync", lambda e: e.dma_start(c["sb_row"][:], c["kn_r"][:]), sem=sX, inc=16)
    for half in range(2):
        p.add("tensor", lambda e, half=half: e.matmul(
            pA[:], c["sb_ones1"][:], c["sb_row"][0:1, half * 512:(half + 1) * 512],
            start=True, stop=True), sem=sT)
        p.add("scalar", lambda e, half=half: e.copy(
            a32.view(A32["knb"] + half * 512, 512), pA[:]), sem=sA)
    p.add("gpsimd", lambda e: e.dma_start(
        a16.view(0, 24576).rearrange("p (c o) -> p c o", c=8),
        c["wqkv"][:].rearrange("(c p) o -> p c o", p=128)), sem=c["sAC"], inc=16)
    p.add("vector", lambda e: e.memset(st[:, 23:24], 1.0), sem=sV)
    p.add("vector", lambda e: e.memset(a32.view(6144, 1040), 0.0), sem=sV)

    for sub in range(4):
        p.add("vector", lambda e: (
            e.memset(a32.view(A32["bc1"], 512)[0:1, :], 0.0),
            e.memset(a32.view(A32["bc2"], 512)[0:1, :], 0.0))[-1], sem=sV)
        for c8 in range(8):
            p.add("sync", lambda e, sub=sub, c8=c8: e.dma_start(
                a32.view(A32["ysl"] + (c8 % 2) * 1024, 512),
                c["ypart"][:, c8 * 2048 + sub * 512:c8 * 2048 + sub * 512 + 512]),
                sem=c["sFL"], inc=16)
            p.add("tensor", lambda e, c8=c8: e.matmul(
                pA[0:1, 0:512], st[:, 23:24], a32.view(A32["ysl"] + (c8 % 2) * 1024, 512),
                start=True, stop=True), sem=sT)
            p.add("vector", lambda e: e.tensor_add(
                a32.view(A32["bc2"], 512)[0:1, :], a32.view(A32["bc2"], 512)[0:1, :],
                pA[0:1, 0:512]), sem=sV)
            p.add("scalar", lambda e, c8=c8: e.activation(
                a32.view(A32["sttmp"], 512), a32.view(A32["ysl"] + (c8 % 2) * 1024, 512),
                AF.Square), sem=sA)
            p.add("tensor", lambda e: e.matmul(
                pA[0:1, 0:512], st[:, 23:24], a32.view(A32["sttmp"], 512),
                start=True, stop=True), sem=sT)
            p.add("vector", lambda e: e.tensor_add(
                a32.view(A32["bc1"], 512)[0:1, :], a32.view(A32["bc1"], 512)[0:1, :],
                pA[0:1, 0:512]), sem=sV)
        p.add("vector", lambda e: e.tensor_scalar_mul(
            a32.view(A32["bc2"], 512)[0:1, :], a32.view(A32["bc2"], 512)[0:1, :],
            1.0 / 1024.0), sem=sV)
        p.add("vector", lambda e: e.tensor_scalar_mul(
            a32.view(A32["bc1"], 512)[0:1, :], a32.view(A32["bc1"], 512)[0:1, :],
            1.0 / 1024.0), sem=sV)
        p.add("vector", lambda e: e.tensor_tensor(
            a32.view(A32["sttmp"], 512)[0:1, :], a32.view(A32["bc2"], 512)[0:1, :],
            a32.view(A32["bc2"], 512)[0:1, :], ALU.mult), sem=sV)
        p.add("vector", lambda e: e.tensor_sub(
            a32.view(A32["bc1"], 512)[0:1, :], a32.view(A32["bc1"], 512)[0:1, :],
            a32.view(A32["sttmp"], 512)[0:1, :]), sem=sV)
        p.add("scalar", lambda e: e.activation(
            a32.view(A32["bc1"], 512)[0:1, :], a32.view(A32["bc1"], 512)[0:1, :],
            AF.Sqrt, bias=st[0:1, 18:19]), sem=sA)
        p.add("vector", lambda e: e.reciprocal(
            a32.view(A32["bc1"], 512)[0:1, :], a32.view(A32["bc1"], 512)[0:1, :]), sem=sV)
        p.add("tensor", lambda e: e.matmul(
            pA[:, 0:512], c["sb_ones1"][:], a32.view(A32["bc1"], 512)[0:1, :],
            start=True, stop=True), sem=sT)
        p.add("tensor", lambda e: e.matmul(
            pB[:, 0:512], c["sb_ones1"][:], a32.view(A32["bc2"], 512)[0:1, :],
            start=True, stop=True), sem=sT)
        for c8 in range(8):
            p.add("sync", lambda e, sub=sub, c8=c8: e.dma_start(
                a32.view(A32["ysl"] + (c8 % 2) * 1024, 512),
                c["ypart"][:, c8 * 2048 + sub * 512:c8 * 2048 + sub * 512 + 512]),
                sem=c["sFL"], inc=16)
            p.add("vector", lambda e, c8=c8: e.tensor_tensor(
                a32.view(A32["sttmp"], 512), a32.view(A32["ysl"] + (c8 % 2) * 1024, 512),
                pB[:, 0:512], ALU.subtract), sem=sV)
            p.add("vector", lambda e, sub=sub, c8=c8: e.tensor_tensor(
                a16.view(A16_C + c8 * 2048 + sub * 512, 512), a32.view(A32["sttmp"], 512),
                pA[:, 0:512], ALU.mult), sem=sV)

    for stt_ in range(16):
        xtile = lambda ic, stt_=stt_: a16.view(A16_C + ic * 2048 + stt_ * 128, 128)
        for nm, obase, dstH, nbofs in (("k", DM, 0, A32["knb"]), ("q", 0, 2176, A32["qnb"])):
            for half in range(2):
                p.add("tensor", lambda e, obase=obase, half=half, xtile=xtile: [
                    e.matmul(pA[:, 0:512] if half == 0 else pB[:, 0:512], xtile(ic),
                             a16.view(ic * 3072 + obase + half * 512, 512),
                             start=(ic == 0), stop=(ic == 7)) for ic in range(8)][-1],
                    sem=sT)
                p.add("scalar", lambda e, half=half: e.activation(
                    a32.view(A32["sttmp"], 512), (pA if half == 0 else pB)[:, 0:512],
                    AF.Square, accum_out=st[:, 20 + half:21 + half]), sem=sA)
            p.add("vector", lambda e: e.tensor_add(st[:, 22:23], st[:, 20:21], st[:, 21:22]), sem=sV)
            p.add("vector", lambda e: e.tensor_scalar_mul(st[:, 22:23], st[:, 22:23], 1.0 / 1024.0), sem=sV)
            p.add("scalar", lambda e: e.activation(st[:, 22:23], st[:, 22:23], AF.Sqrt,
                                                   bias=st[:, 19:20]), sem=sA)
            p.add("vector", lambda e: e.reciprocal(st[:, 22:23], st[:, 22:23]), sem=sV)
            for half in range(2):
                p.add("vector", lambda e, half=half, nbofs=nbofs: e.tensor_tensor(
                    a32.view(A32["sttmp"], 512), (pA if half == 0 else pB)[:, 0:512],
                    a32.view(nbofs + half * 512, 512), ALU.mult), sem=sV)
                p.add("scalar", lambda e, half=half, dstH=dstH: e.activation(
                    a16.view(H + dstH + half * 512, 512), a32.view(A32["sttmp"], 512),
                    AF.Relu, scale=st[:, 22:23]), sem=sA)
        for half in range(2):
            p.add("tensor", lambda e, half=half, xtile=xtile: [
                e.matmul(pA[:, 0:512], xtile(ic),
                         a16.view(ic * 3072 + 2048 + half * 512, 512),
                         start=(ic == 0), stop=(ic == 7)) for ic in range(8)][-1],
                sem=sT)
            p.add("scalar", lambda e, half=half: e.copy(
                a16.view(H + 1024, 1040).rearrange("p (h x) -> p h x", x=65)
                [:, half * 8:(half + 1) * 8, 0:64],
                pA[:, 0:512].rearrange("p (h x) -> p h x", x=64)), sem=sA)
        p.add("vector", lambda e: e.memset(
            a16.view(H + 1024, 1040).rearrange("p (h x) -> p h x", x=65)[:, :, 64:65], 1.0),
            sem=sV)
        p.add("tensor", lambda e: [
            e.matmul(pD[0:64, h * 128:h * 128 + 65], a16.view(H + h * 64, 64),
                     a16.view(H + 1024 + h * 65, 65), start=True, stop=True)
            for h in range(16)][-1], sem=sT)
        p.add("vector", lambda e: e.tensor_add(
            a32.view(6144, 1040).rearrange("p (h x) -> p h x", x=65)[0:64],
            a32.view(6144, 1040).rearrange("p (h x) -> p h x", x=65)[0:64],
            pD[0:64, 0:2048].rearrange("p (h x) -> p h x", x=128)[:, :, 0:65]), sem=sV)
        p.add("tensor", lambda e: [
            e.transpose(pT[:, ic * 128:(ic + 1) * 128],
                        a16.view(H + 2176 + ic * 128, 128), c["sb_id16"][:])
            for ic in range(8)][-1], sem=sT)
        p.add("vector", lambda e: [
            e.tensor_copy(a16.view(H, 1024)[:, ic * 128:(ic + 1) * 128],
                          pT[:, ic * 128:(ic + 1) * 128]) for ic in range(8)][-1], sem=sV)
        p.add("sync", lambda e, stt_=stt_: e.dma_start(
            c["qp_o"][:].rearrange("p (x t) -> p x t", x=8)[:, :, stt_ * 128:(stt_ + 1) * 128],
            a16.view(H, 1024).rearrange("p (x t) -> p x t", x=8)),
            sem=sO, inc=16)
    p.add("sync", lambda e: e.dma_start(c["kvpart"][:], a32.view(6144, 1040)[0:64, :]),
          sem=sO, inc=16)
    p.run(c["nc"], serialize=True)
    c["b1_tail"] = dict(o=(sO, sO.n), v=(sV, sV.n), s=(sA, sA.n), t=(sT, sT.n))


def _l1_ffn1(c):
    a16, a32 = c["a16"], c["a32"]
    sX, sT, sV, sA, sO = c["sX"], c["sT"], c["sV"], c["sA"], c["sO"]
    pA, pB = c["pA"], c["pB"]
    H = A16_H
    bt = c["b1_tail"]
    p = Prog()
    t_conv = p.add("gpsimd", lambda e: e.wait_ge(c["sW"].h, c["tok_conv"][1]),
                   waits=[bt["o"], bt["v"], bt["t"], bt["s"]], sem=c["sP"])

    hw_war = None
    gw_war = None
    pa_war = None
    ffdma = [None] * 4
    for tt in range(4):
        sl_user = [None, None]
        silu_tok = None; ts2_tok = None; hm_tok_prev = None
        hm_tok = None
        for fg in range(32):
            par = fg % 2
            slab = H + par * 2048
            d1 = p.add("sync", lambda e, fg=fg, slab=slab: e.dma_start(
                a16.view(slab, 1024).rearrange("p (c o) -> p c o", c=8),
                c["w1b"][:, fg * 128:(fg + 1) * 128].rearrange("(c p) o -> p c o", p=128)),
                waits=[sl_user[par], t_conv if fg < 2 else None,
                       hw_war if fg < 2 else None], sem=c["sFL"], inc=16)
            d2 = p.add("sync", lambda e, fg=fg, slab=slab: e.dma_start(
                a16.view(slab + 1024, 1024).rearrange("p (c o) -> p c o", c=8),
                c["w2b"][:, fg * 128:(fg + 1) * 128].rearrange("(c p) o -> p c o", p=128)),
                sem=c["sFL"], inc=16)
            t1 = p.add("tensor", lambda e, tt=tt, slab=slab: [
                e.matmul(pA[:, 0:512], a16.view(slab + ic * 128, 128),
                         a16.view(A16_C + ic * 2048 + tt * 512, 512),
                         start=(ic == 0), stop=(ic == 7)) for ic in range(8)][-1],
                waits=[d2, silu_tok, pa_war if fg == 0 else None], sem=sT)
            t2 = p.add("tensor", lambda e, tt=tt, slab=slab: [
                e.matmul(pB[:, 0:512], a16.view(slab + 1024 + ic * 128, 128),
                         a16.view(A16_C + ic * 2048 + tt * 512, 512),
                         start=(ic == 0), stop=(ic == 7)) for ic in range(8)][-1],
                waits=[ts2_tok], sem=sT)
            sl_user[par] = t2
            silu_tok = p.add("scalar", lambda e, fg=fg: e.activation(
                a32.view(A32["ftmp"], 512), pA[:, 0:512], AF.Silu,
                scale=c["sb_s1c"][:, fg:fg + 1]),
                waits=[t1, hm_tok], sem=sA)
            ts2_tok = p.add("vector", lambda e, fg=fg: e.tensor_scalar(
                a32.view(A32["ftmp2"], 512), pB[:, 0:512], c["sb_s2c"][:, fg:fg + 1],
                None, ALU.mult), waits=[t2, hm_tok], sem=sV)
            hm_tok = p.add("vector", lambda e, fg=fg: e.tensor_tensor(
                a16.view(A16_A + fg * 512, 512), a32.view(A32["ftmp"], 512),
                a32.view(A32["ftmp2"], 512), ALU.mult),
                waits=[silu_tok, ts2_tok, hw_war if fg == 0 else None], sem=sV)
        sl_user = [None, None]
        act_tok = None
        last_b = None
        for oc in range(32):
            for ih in range(2):
                par = ih
                slab = H + par * 2048
                d = p.add("sync", lambda e, oc=oc, ih=ih, slab=slab: e.dma_start(
                    a16.view(slab, 2048).rearrange("p (c o) -> p c o", c=16),
                    c["h1b"][ih * 2048:(ih + 1) * 2048, oc * 128:(oc + 1) * 128]
                    .rearrange("(c p) o -> p c o", p=128)),
                    waits=[sl_user[par], hm_tok if oc == 0 and ih == 0 else None],
                    sem=c["sFL"], inc=16)
                t = p.add("tensor", lambda e, ih=ih, slab=slab: [
                    e.matmul(pA[:, 0:512], a16.view(slab + ic * 128, 128),
                             a16.view(A16_A + (ih * 16 + ic) * 512, 512),
                             start=(ih == 0 and ic == 0), stop=(ih == 1 and ic == 15))
                    for ic in range(16)][-1],
                    waits=[d, hm_tok if oc == 0 and ih == 0 else None,
                           act_tok if ih == 0 else None], sem=sT)
                sl_user[par] = t
                last_b = t
            act_tok = p.add("scalar", lambda e, oc=oc: e.activation(
                a16.view(A16_B + oc * 512, 512), pA[:, 0:512], AF.Gelu,
                scale=c["sb_hs1c"][:, oc:oc + 1]),
                waits=[last_b, gw_war if oc == 0 else None], sem=sA)
        hw_war_b = last_b
        sl_user = [None, None]
        act2_tok = None
        for oc in range(32):
            for ih in range(2):
                par = ih
                slab = H + par * 2048
                d = p.add("sync", lambda e, oc=oc, ih=ih, slab=slab: e.dma_start(
                    a16.view(slab, 2048).rearrange("p (c o) -> p c o", c=16),
                    c["h2b"][ih * 2048:(ih + 1) * 2048, oc * 128:(oc + 1) * 128]
                    .rearrange("(c p) o -> p c o", p=128)),
                    waits=[sl_user[par], act_tok if oc == 0 and ih == 0 else None],
                    sem=c["sFL"], inc=16)
                t = p.add("tensor", lambda e, ih=ih, slab=slab: [
                    e.matmul(pA[:, 0:512], a16.view(slab + ic * 128, 128),
                             a16.view(A16_B + (ih * 16 + ic) * 512, 512),
                             start=(ih == 0 and ic == 0), stop=(ih == 1 and ic == 15))
                    for ic in range(16)][-1],
                    waits=[d, act_tok if oc == 0 and ih == 0 else None,
                           act2_tok if ih == 0 else None], sem=sT)
                sl_user[par] = t
                last_c = t
            act2_tok = p.add("scalar", lambda e, oc=oc: e.activation(
                a16.view(A16_A + oc * 512, 512), pA[:, 0:512], AF.Gelu,
                scale=c["sb_hs2c"][:, oc:oc + 1]),
                waits=[last_c, hw_war_b if oc == 0 else None], sem=sA)
        gw_war_c = last_c
        ev_tok = None
        wdma_tok = None
        for half in range(2):
            wdma_tok = p.add("sync", lambda e, half=half: e.dma_start(
                a16.view(A16_B, 16384).rearrange("p (c o) -> p c o", c=16),
                c["w3b"][half * 2048:(half + 1) * 2048, :]
                .rearrange("(c p) o -> p c o", p=128)),
                waits=[gw_war_c if half == 0 else last_c2, act2_tok if half == 0 else None],
                sem=c["sFL"], inc=16)
            for oc in range(8):
                t = p.add("tensor", lambda e, half=half, oc=oc: [
                    e.matmul(pA[:, 0:512], a16.view(A16_B + fc * 1024 + oc * 128, 128),
                             a16.view(A16_A + (half * 16 + fc) * 512, 512),
                             start=(fc == 0), stop=(fc == 15)) for fc in range(16)][-1],
                    waits=[wdma_tok if oc == 0 else None,
                           act2_tok if half == 0 and oc == 0 else None,
                           ev_tok], sem=sT)
                last_c2 = t
                if half == 0:
                    ev_tok = p.add("vector", lambda e, oc=oc: e.tensor_scalar(
                        a32.view(A32["facc"] + oc * 512, 512), pA[:, 0:512],
                        c["sb_s3c"][:, oc:oc + 1], None, ALU.mult),
                        waits=[t, ev_tok,
                               ffdma[tt - 1] if tt >= 1 and oc == 0 else None], sem=sV)
                else:
                    ev_tok = p.add("vector", lambda e, oc=oc: e.scalar_tensor_tensor(
                        a32.view(A32["facc"] + oc * 512, 512), pA[:, 0:512],
                        c["sb_s3c"][:, oc:oc + 1], a32.view(A32["facc"] + oc * 512, 512),
                        ALU.mult, ALU.add), waits=[t, ev_tok], sem=sV)
        ffdma[tt] = p.add("sync", lambda e, tt=tt: e.dma_start(
            c["ffn1"][:].rearrange("p (x t) -> p x t", x=8)[:, :, tt * 512:(tt + 1) * 512],
            a32.view(A32["facc"], 4096).rearrange("p (x t) -> p x t", x=8)),
            waits=[ev_tok], sem=sO, inc=16)
        hw_war = last_c2
        gw_war = last_c2
        pa_war = ev_tok
    p.run(c["nc"])
    c["ffn_tail"] = (sO, sO.n)

def _l1_tail(c):
    p = Prog()
    p.add("gpsimd", lambda e: e.wait_ge(c["sO"].h, c["sO"].n),
          waits=[c["ffn_tail"], c["b1_tail"]["o"]], sem=None)
    p.run(c["nc"])



def build_launch2():
    nc = bass.Bass("TRN2", target_bir_lowering=False)
    yp_i = nc.dram_tensor("yp_i", [128, 16384], F32, kind="ExternalInput")
    ff_i = nc.dram_tensor("ff_i", [128, 16384], F32, kind="ExternalInput")
    qp_i = nc.dram_tensor("qp_i", [128, 16384], BF16, kind="ExternalInput")
    kvall = nc.dram_tensor("kvall", [4, 64, 1040], F32, kind="ExternalInput")
    wout = nc.dram_tensor("wout", [DM, DM], F32, kind="ExternalInput")
    bout_c = nc.dram_tensor("bout_c", [128, 8], F32, kind="ExternalInput")
    idf = nc.dram_tensor("idf", [128, 128], F32, kind="ExternalInput")
    out_tm = nc.dram_tensor("out_tm", [2048, DM], F32, kind="ExternalOutput")

    s_kv4 = nc.alloc_sbuf_tensor("s_kv4", [64, 4 * 1040], F32)
    s_kvb = nc.alloc_sbuf_tensor("s_kvb", [128, 1040], BF16)
    s_wout = nc.alloc_sbuf_tensor("s_wout", [128, 8192], BF16)
    s_qp = nc.alloc_sbuf_tensor("s_qp", [128, 4096], BF16)
    s_att = nc.alloc_sbuf_tensor("s_att", [128, 4096], BF16)
    s_yst = nc.alloc_sbuf_tensor("s_yst", [128, 4096], F32)
    s_y1 = nc.alloc_sbuf_tensor("s_y1", [128, 1024], F32)
    s_y2 = nc.alloc_sbuf_tensor("s_y2", [128, 1024], F32)
    s_rec = nc.alloc_sbuf_tensor("s_rec", [64, 1024], F32)
    s_drow = nc.alloc_sbuf_tensor("s_drow", [1, 1024], F32)
    s_ones = nc.alloc_sbuf_tensor("s_ones", [1, 128], F32)
    s_idf = nc.alloc_sbuf_tensor("s_idf", [128, 128], F32)
    s_boc = nc.alloc_sbuf_tensor("s_boc", [128, 8], F32)
    s_out = nc.alloc_sbuf_tensor("s_out", [128, 2048], F32)
    pA = nc.alloc_psum_tensor("pA", [128, 512], F32)
    pB = nc.alloc_psum_tensor("pB", [128, 512], F32)
    pC = nc.alloc_psum_tensor("pC", [128, 1024], F32)
    sX = Sem(nc, "sX"); sG = Sem(nc, "sG"); sT = Sem(nc, "sT")
    sV = Sem(nc, "sV"); sA = Sem(nc, "sA"); sO = Sem(nc, "sO")
    sQ2 = Sem(nc, "sQ2"); sY = Sem(nc, "sY")

    p = Prog()
    p.add("sync", lambda e: e.dma_start(s_idf[:], idf[:]), sem=sX, inc=16)
    p.add("sync", lambda e: e.dma_start(s_boc[:], bout_c[:]), sem=sX, inc=16)
    t_kvd = p.add("sync", lambda e: e.dma_start(
        s_kv4[:].rearrange("p (r n) -> p r n", r=4),
        kvall[:].rearrange("r p n -> p r n")), sem=sX, inc=16)
    t_wo = p.add("gpsimd", lambda e: e.dma_start(
        s_wout[:].rearrange("p (c o) -> p c o", c=8),
        wout[:].rearrange("(c p) o -> p c o", p=128)), sem=sG, inc=16)
    t_on = p.add("vector", lambda e: e.memset(s_ones[:], 1.0), sem=sV)
    t_kv = t_kvd
    for r in range(1, 4):
        t_kv = p.add("vector", lambda e, r=r: e.tensor_add(
            s_kv4[:, 0:1040], s_kv4[:, 0:1040], s_kv4[:, r * 1040:(r + 1) * 1040]),
            waits=[t_kv], sem=sV)
    t_kvc = p.add("scalar", lambda e: e.copy(s_kvb[0:64, :], s_kv4[:, 0:1040]),
                  waits=[t_kv], sem=sA)
    t_kvb = p.add("vector", lambda e: e.tensor_copy(s_kvb[64:128, :], s_kvb[0:64, :]),
                  waits=[t_kvc], sem=sV)

    qp_user = None
    pa_war = None
    pc_war = None
    att_tok = [None, None]
    m2_tok = [None, None]
    for tc in range(4):
        t_qp = p.add("sync", lambda e, tc=tc: e.dma_start(
            s_qp[:].rearrange("p (x t) -> p x t", x=8),
            qp_i[:].rearrange("p (x t) -> p x t", x=8)[:, :, tc * 512:(tc + 1) * 512]),
            waits=[qp_user], sem=sQ2, inc=16)
        for h in range(16):
            par = h % 2
            qrow = (h % 2) * 64
            sc = pA[0:65, 0:512] if par == 0 else pC[0:65, 0:512]
            bc = pB[0:64, 0:512] if par == 0 else pC[0:64, 512:1024]
            dr = s_drow[:, par * 512:(par + 1) * 512]
            rc = s_rec[:, par * 512:(par + 1) * 512]
            t_m1 = p.add("tensor", lambda e, h=h, qrow=qrow, sc=sc: e.matmul(
                sc, s_kvb[qrow:qrow + 64, 65 * h:65 * h + 65],
                s_qp[qrow:qrow + 64, (h // 2) * 512:(h // 2) * 512 + 512],
                start=True, stop=True),
                waits=[t_qp if h < 2 else None, t_kvb if tc == 0 and h < 2 else None,
                       att_tok[par],
                       pa_war if par == 0 and h == 0 else None,
                       pc_war if par == 1 and h == 1 else None], sem=sT)
            if h >= 14:
                qp_user = t_m1
            t_dc = p.add("scalar", lambda e, sc=sc, dr=dr: e.copy(dr, sc[64:65, :]),
                         waits=[t_m1, m2_tok[par]], sem=sA)
            t_ad = p.add("vector", lambda e, dr=dr: e.tensor_scalar_add(dr, dr, 1e-6),
                         waits=[t_dc], sem=sV)
            t_rp = p.add("vector", lambda e, dr=dr: e.reciprocal(dr, dr),
                         waits=[t_ad], sem=sV)
            t_m2 = p.add("tensor", lambda e, bc=bc, dr=dr: e.matmul(
                bc, s_ones[0:1, 0:64], dr, start=True, stop=True),
                waits=[t_rp, t_on if tc == 0 and h < 2 else None], sem=sT)
            m2_tok[par] = t_m2
            t_rc = p.add("scalar", lambda e, bc=bc, rc=rc: e.copy(rc, bc),
                         waits=[t_m2, att_tok[par]], sem=sA)
            att_tok[par] = p.add("vector", lambda e, h=h, qrow=qrow, sc=sc, rc=rc:
                e.tensor_tensor(
                    s_att[qrow:qrow + 64, (h // 2) * 512:(h // 2) * 512 + 512],
                    sc[0:64, :], rc, ALU.mult), waits=[t_rc], sem=sV)
        stt_tok = None
        ya_tok = [None, None]
        for oc in range(8):
            yb = oc % 2
            t_mo = p.add("tensor", lambda e, oc=oc: [
                e.matmul(pA[:, 0:512], s_wout[:, ic * DM + oc * 128:ic * DM + oc * 128 + 128],
                         s_att[:, ic * 512:(ic + 1) * 512],
                         start=(ic == 0), stop=(ic == 7)) for ic in range(8)][-1],
                waits=[att_tok[0] if oc == 0 else None, att_tok[1] if oc == 0 else None,
                       t_wo if tc == 0 and oc == 0 else None,
                       stt_tok], sem=sT)
            t_y1 = p.add("sync", lambda e, oc=oc, tc=tc, yb=yb: e.dma_start(
                s_y1[:, yb * 512:(yb + 1) * 512],
                yp_i[:, oc * 2048 + tc * 512:oc * 2048 + tc * 512 + 512]),
                waits=[ya_tok[yb]], sem=sY, inc=16)
            t_y2 = p.add("sync", lambda e, oc=oc, tc=tc, yb=yb: e.dma_start(
                s_y2[:, yb * 512:(yb + 1) * 512],
                ff_i[:, oc * 2048 + tc * 512:oc * 2048 + tc * 512 + 512]),
                sem=sY, inc=16)
            stt_tok = p.add("vector", lambda e, oc=oc, yb=yb: e.scalar_tensor_tensor(
                s_y1[:, yb * 512:(yb + 1) * 512], pA[:, 0:512], s_boc[:, oc:oc + 1],
                s_y1[:, yb * 512:(yb + 1) * 512], ALU.add, ALU.add),
                waits=[t_mo, t_y2], sem=sV)
            ya_tok[yb] = p.add("vector", lambda e, oc=oc, yb=yb: e.tensor_add(
                s_yst[:, oc * 512:(oc + 1) * 512], s_y1[:, yb * 512:(yb + 1) * 512],
                s_y2[:, yb * 512:(yb + 1) * 512]), waits=[stt_tok], sem=sV)
        vc_tok = None
        od_tok = [None, None]
        for t4 in range(4):
            ob = t4 % 2
            t_tr = p.add("tensor", lambda e, t4=t4: [
                e.transpose(pC[:, ic * 128:(ic + 1) * 128],
                            s_yst[:, ic * 512 + t4 * 128:ic * 512 + t4 * 128 + 128],
                            s_idf[:]) for ic in range(8)][-1],
                waits=[ya_tok[0] if t4 == 0 else None, ya_tok[1] if t4 == 0 else None,
                       att_tok[1] if t4 == 0 else None, vc_tok], sem=sT)
            vc_tok = p.add("vector", lambda e, t4=t4, ob=ob: [
                e.tensor_copy(s_out[:, ob * 1024 + ic * 128:ob * 1024 + (ic + 1) * 128],
                              pC[:, ic * 128:(ic + 1) * 128])
                for ic in range(8)][-1], waits=[t_tr, od_tok[ob]], sem=sV)
            od_tok[ob] = p.add("sync", lambda e, tc=tc, t4=t4, ob=ob: e.dma_start(
                out_tm[tc * 512 + t4 * 128:tc * 512 + t4 * 128 + 128, :],
                s_out[:, ob * 1024:(ob + 1) * 1024]), waits=[vc_tok], sem=sO, inc=16)
        pa_war = stt_tok
        pc_war = vc_tok
    p.add("gpsimd", lambda e: e.wait_ge(sO.h, sO.n), sem=None)
    p.run(nc)
    return nc



def kernel(**inputs):
    sh, per, rows_all = _host_prep(inputs)
    nc1, _ = build_launch1()
    in_maps = [dict(sh, **per[c]) for c in range(NCORE)]
    res1 = run_bass_kernel_spmd(nc1, in_maps, core_ids=list(range(NCORE)))

    nc2 = build_launch2()
    kvs = [np.asarray(res1.results[c]["kvpart"]) for c in range(NCORE)]
    in2 = []
    for c in range(NCORE):
        b = c // 4
        in2.append({
            "yp_i": np.asarray(res1.results[c]["ypart"]),
            "ff_i": np.asarray(res1.results[c]["ffn1"]),
            "qp_i": np.asarray(res1.results[c]["qp_o"]),
            "kvall": np.ascontiguousarray(np.stack([kvs[4 * b + j] for j in range(4)])),
            "wout": sh["wout"], "bout_c": sh["bout_c"], "idf": sh["idf"],
        })
    res2 = run_bass_kernel_spmd(nc2, in2, core_ids=list(range(NCORE)))
    out = np.empty((2, S_GLOB, DM), np.float32)
    for c in range(NCORE):
        b = c // 4
        out[b][rows_all[c]] = np.asarray(res2.results[c]["out_tm"])
    return out
